# version 05
# baseline (speedup 1.0000x reference)
import math
from contextlib import ExitStack

import numpy as np
import ml_dtypes
import concourse.bass as bass
import concourse.mybir as mybir
from concourse.bass_utils import run_bass_kernel_spmd

F32 = mybir.dt.float32
BF16 = mybir.dt.bfloat16
AF = mybir.ActivationFunctionType
ALU = mybir.AluOpType
AX = mybir.AxisListType

NCORES = 8
T = 2048
D = 2048
NT = T // 128
NJ = D // 128
HEADS = 16
KV = 4096
NKB = KV // 128
E = 16
I_ = 1408
NI = I_ // 128
IS = 5632
NBAND = 4
SCALE = 1.0 / math.sqrt(128.0)
HALF = T // 2
NTH = HALF // 128


class Sem:
    def __init__(self, nc, name):
        self.h = nc.alloc_semaphore(name)
        self.n = 0

    def inc(self, instr, k=1):
        instr.then_inc(self.h, k)
        self.n += k
        return self.n


def build_program():
    nc = bass.Bass("TRN2", target_bir_lowering=False)

    d_x = nc.dram_tensor("xq", [T, D], F32, kind="ExternalInput").ap()
    d_kvk = nc.dram_tensor("kv_k", [NKB, 128, D], F32, kind="ExternalInput").ap()
    d_kvv = nc.dram_tensor("kv_v", [NKB, 128, D], F32, kind="ExternalInput").ap()
    d_wq = nc.dram_tensor("Wq", [D, D], F32, kind="ExternalInput").ap()
    d_wo = nc.dram_tensor("Wo", [D, D], F32, kind="ExternalInput").ap()
    d_wr17 = nc.dram_tensor("wr17", [D, 17], F32, kind="ExternalInput").ap()
    d_wg = nc.dram_tensor("w_gate", [E, D, I_], F32, kind="ExternalInput").ap()
    d_wu = nc.dram_tensor("w_up", [E, D, I_], F32, kind="ExternalInput").ap()
    d_wd = nc.dram_tensor("w_down", [E, I_, D], F32, kind="ExternalInput").ap()
    d_wsg = nc.dram_tensor("ws_gate", [D, IS], F32, kind="ExternalInput").ap()
    d_wsu = nc.dram_tensor("ws_up", [D, IS], F32, kind="ExternalInput").ap()
    d_wsd = nc.dram_tensor("ws_down", [IS, D], F32, kind="ExternalInput").ap()
    d_eyeb = nc.dram_tensor("eye_b", [128, 128], BF16, kind="ExternalInput").ap()
    d_eyef = nc.dram_tensor("eye_f", [128, 128], F32, kind="ExternalInput").ap()
    d_eye16 = nc.dram_tensor("eye16", [16, 16], F32, kind="ExternalInput").ap()
    d_out = nc.dram_tensor("out", [T, D], F32, kind="ExternalOutput").ap()
    d_x1 = nc.dram_tensor("x1_scr", [T, D], F32, kind="Internal").ap()
    d_x1T = nc.dram_tensor("x1T_scr", [D, T], F32, kind="Internal").ap()
    d_ln2T = nc.dram_tensor("ln2T_scr", [D, T], BF16, kind="Internal").ap()

    A = nc.alloc_sbuf_tensor
    eye_b = A("s_eyeb", [128, 128], BF16)
    eye_f = A("s_eyef", [128, 128], F32)
    eye16 = A("eye16s", [16, 16], F32)
    ones16 = A("ones16", [16, 128], F32)
    sel_b = A("sel_b", [16, 128], F32)
    eps_t = A("eps_t", [128, 1], F32)
    wr17b = A("wr17b", [128, NJ * 17], BF16)
    comb = A("comb", [128, NT * 16], F32)
    combT = A("combT", [16, T], F32)
    sgcol = A("sgcol", [128, NT], F32)
    sgT = A("sgT", [16, 128], F32)
    sv = A("sv", [128, 32], F32)
    dinv = A("dinv", [128, 4], F32)
    pmask = A("pmask", [128, 16], F32)
    pwork = A("pwork", [128, 16], F32)
    cacc = A("cacc", [128, 16], F32)
    pe_r = A("pe_r", [128, 16], F32)

    KB = 1024
    MB = (nc.sbuf_base + 31) & ~31
    AT = nc.alloc_sbuf_tensor_at

    def at(name, shape, dtype, off):
        return AT(name, shape, dtype, offset=MB + off)

    big1 = at("big1", [128, NJ * T], BF16, 0)
    big0 = at("big0", [128, NJ * T], BF16, 64 * KB)
    big2 = at("big2", [128, NT * D], BF16, 64 * KB)
    xstage = [at(f"xstage{p}", [128, D], F32, 128 * KB + p * 8 * KB) for p in range(2)]
    lnstage = [at(f"lnstage{p}", [128, D], BF16, 144 * KB + p * 4 * KB) for p in range(2)]
    xsq = at("xsq", [128, D], F32, 152 * KB)
    wstage = [at(f"wstage{p}", [128, D], BF16, 160 * KB + p * 4 * KB) for p in range(2)]
    otst = [at(f"otst{p}", [128, NJ * 128], BF16, 168 * KB + p * 4 * KB) for p in range(2)]
    x1buf = [at(f"x1buf{p}", [128, D], F32, 176 * KB + p * 8 * KB) for p in range(2)]
    kstg = at("kstg", [128, NKB * 128], BF16, 128 * KB)
    ktb = [at(f"ktb{p}", [128, NKB * 128], BF16, 136 * KB + p * 8 * KB) for p in range(2)]
    vpb = [at(f"vpb{p}", [128, NKB * 132], BF16, 152 * KB + p * 8448) for p in range(2)]
    exps = [at(f"exps{p}", [128, 1024], BF16, 152 * KB + 16896 + p * 2048) for p in range(2)]
    x1Tbuf = [at(f"x1Tbuf{p}", [128, NJ * 128], F32, 128 * KB + p * 8 * KB) for p in range(2)]
    ln2Tbuf = [at(f"ln2Tbuf{p}", [128, NJ * 128], BF16, 160 * KB + p * 4 * KB) for p in range(2)]
    routst = at("routst", [128, NJ * 128], BF16, 168 * KB)
    ln2h = at("ln2h", [128, NJ * HALF], BF16, 0)
    accT = at("accT", [128, NJ * HALF], F32, 32 * KB)
    a_sb = at("a_sb", [128, NI * HALF], BF16, 96 * KB)
    wgc = [at(f"wgc{p}", [128, NJ * 128], BF16, 118 * KB + p * 4 * KB) for p in range(2)]
    wuc = [at(f"wuc{p}", [128, NJ * 128], BF16, 126 * KB + p * 4 * KB) for p in range(2)]
    wdc = [at(f"wdc{p}", [128, NI * 128], BF16, 134 * KB + p * 2816) for p in range(2)]
    sgb_ = at("sgb", [128, 1024], F32, 140 * KB)
    ubuf = at("ubuf", [128, 1024], F32, 144 * KB)
    cb_b = at("cb_b", [128, HALF], F32, 148 * KB)
    sg_b = at("sg_b", [128, HALF], F32, 152 * KB)
    outbuf = [at(f"outbuf{p}", [128, D], F32, 156 * KB + p * 8 * KB) for p in range(2)]

    SX = Sem(nc, "SX")
    SG = Sem(nc, "SG")
    SV = Sem(nc, "SV")
    SS = Sem(nc, "SS")
    ST = Sem(nc, "ST")
    SO = Sem(nc, "SO")
    M = {}

    ptb_stack = ExitStack()
    regB = ExitStack()
    regC = ExitStack()
    regD = ExitStack()
    regE = ExitStack()
    pt_b = [ptb_stack.enter_context(nc.psum_tensor(f"pt_b{p}", [128, 128], BF16))
            for p in range(2)]
    p_s2 = [regB.enter_context(nc.psum_tensor(f"p_s2{p}", [128, 1024], F32))
            for p in range(2)]
    p_o = regB.enter_context(nc.psum_tensor("p_o", [128, 1024], F32))
    p_q = [p_o[:, 0:512], p_o[:, 512:1024]]

    with nc.Block() as block:
        TE, SC, VE, GP, SY = (block.tensor, block.scalar, block.vector,
                              block.gpsimd, block.sync)

        def tr_ev(n, tr_src, ev_dst, pt, eye, tag, ev_eng, ev_sem, gbase,
                  pre_tr=None, pre_ev=None):
            for j in range(n):
                g = gbase + j

                def _t(e, j=j, g=g):
                    if pre_tr is not None and j == 0:
                        pre_tr(e)
                    prev = M.get((tag, 'ev', g - 2))
                    if prev is not None:
                        e.wait_ge(ev_sem.h, prev)
                    M[tag, 'tr', g] = ST.inc(e.transpose(pt[g % 2][:], tr_src(j), eye[:]))
                TE(_t)

                def _e(e, j=j, g=g):
                    e.wait_ge(ST.h, M[tag, 'tr', g])
                    if pre_ev is not None and j == 0:
                        pre_ev(e)
                    if ev_eng == 'scalar':
                        ins = e.copy(ev_dst(j), pt[g % 2][:])
                    else:
                        ins = e.tensor_copy(ev_dst(j), pt[g % 2][:])
                    M[tag, 'ev', g] = ev_sem.inc(ins)
                (SC if ev_eng == 'scalar' else VE)(_e)

        def emit_ln(tag, src, dst, tt, pre_scalar=None):
            def _s1(e):
                if pre_scalar is not None:
                    pre_scalar(e)
                prev = M.get((tag, 'var', tt - 1))
                if prev is not None:
                    e.wait_ge(SV.h, prev)
                M[tag, 'sq', tt] = SS.inc(e.activation(
                    xsq[:], src[:], AF.Square, accum_out=sv[:, 1:2]))
            SC(_s1)

            def _v1(e):
                m_red = SV.inc(e.reduce_sum(sv[:, 0:1], src[:], axis=AX.X))
                e.wait_ge(SV.h, m_red)
                e.wait_ge(SS.h, M[tag, 'sq', tt])
                e.tensor_tensor(sv[:, 2:3], sv[:, 0:1], sv[:, 0:1], ALU.mult)
                m_mv = SV.inc(e.tensor_scalar_mul(sv[:, 3:4], sv[:, 0:1], 1.0 / D))
                e.wait_ge(SV.h, m_mv)
                M[tag, 'var', tt] = SV.inc(e.scalar_tensor_tensor(
                    sv[:, 4:5], sv[:, 2:3], -1.0 / D, sv[:, 1:2], ALU.mult, ALU.add))
            VE(_v1)

            def _s2(e):
                e.wait_ge(SV.h, M[tag, 'var', tt])
                M[tag, 'sqrt', tt] = SS.inc(e.activation(
                    sv[:, 5:6], sv[:, 4:5], AF.Sqrt, scale=1.0 / D, bias=eps_t[:]))
            SC(_s2)

            def _v2(e):
                e.wait_ge(SS.h, M[tag, 'sqrt', tt])
                m_r = SV.inc(e.reciprocal(sv[:, 6:7], sv[:, 5:6]))
                e.wait_ge(SV.h, m_r)
                M[tag, 'norm', tt] = SV.inc(e.tensor_scalar(
                    dst[:], src[:], sv[:, 3:4], sv[:, 6:7], ALU.subtract, ALU.mult))
            VE(_v2)

        def _p0sy(e):
            SX.inc(e.dma_start(out=eye_b[:], in_=d_eyeb), 16)
            SX.inc(e.dma_start(out=eye_f[:], in_=d_eyef), 16)
            SX.inc(e.dma_start(out=eye16[:], in_=d_eye16), 16)
        SY(_p0sy)
        M['eyes'] = SX.n

        def _p0g(e):
            SG.inc(e.dma_start(
                out=wr17b[:].rearrange("p (jc q) -> p jc q", q=17),
                in_=d_wr17.rearrange("(jc p) q -> p jc q", p=128)), 16)
        GP(_p0g)
        M['wr'] = SG.n

        def _p0v(e):
            e.memset(eps_t[:], 1e-5)
            SV.inc(e.memset(ones16[:], 1.0))
        VE(_p0v)
        M['consts_v'] = SV.n

        for tt in range(NT):
            par = tt % 2

            def _sy(e, tt=tt, par=par):
                if tt >= 2:
                    e.wait_ge(SV.h, M['ln1', 'norm', tt - 2])
                    e.wait_ge(SS.h, M['ln1', 'sq', tt - 2])
                M['p1_x', tt] = SX.inc(e.dma_start(
                    out=xstage[par][:], in_=d_x[tt * 128:(tt + 1) * 128, :]), 16)
            SY(_sy)

            def _vw(e, tt=tt):
                e.wait_ge(SX.h, M['p1_x', tt])
                prev = M.get(('p1t', 'trdone', tt - 2))
                if prev is not None:
                    e.wait_ge(ST.h, prev)
            VE(_vw)

            emit_ln('ln1', xstage[par], lnstage[par], tt,
                    pre_scalar=(lambda e, tt=tt: e.wait_ge(SX.h, M['p1_x', tt])))

            def _pre_tr(e, tt=tt):
                if tt == 0:
                    e.wait_ge(SX.h, M['eyes'])
                e.wait_ge(SV.h, M['ln1', 'norm', tt])
            tr_ev(NJ,
                  (lambda j, par=par: lnstage[par][:, j * 128:(j + 1) * 128]),
                  (lambda j, tt=tt: big0[:, j * T + tt * 128: j * T + (tt + 1) * 128]),
                  pt_b, eye_b, 'p1tr', 'scalar', SS, tt * NJ, pre_tr=_pre_tr)
            M['p1t', 'trdone', tt] = ST.n
        M['p1_done'] = SS.n

        for ic in range(NJ):
            par = ic % 2

            def _gp(e, ic=ic, par=par):
                prev = M.get(('p2t', 'trdone', ic - 2))
                if prev is not None:
                    e.wait_ge(ST.h, prev)
                M['p2_w', ic] = SG.inc(e.dma_start(
                    out=wstage[par][:], in_=d_wq[ic * 128:(ic + 1) * 128, :]), 16)
            GP(_gp)

            def _pre_tr(e, ic=ic):
                e.wait_ge(SG.h, M['p2_w', ic])
                if ic == 0:
                    e.wait_ge(SS.h, M['p1_done'])
            tr_ev(NJ,
                  (lambda j, par=par: wstage[par][:, j * 128:(j + 1) * 128]),
                  (lambda j, par=par: otst[par][:, j * 128:(j + 1) * 128]),
                  pt_b, eye_b, 'p2tr', 'scalar', SS, ic * NJ, pre_tr=_pre_tr)
            M['p2t', 'trdone', ic] = ST.n

            for qg in range(4):
                k = ic * 4 + qg

                def _mm(e, ic=ic, par=par, qg=qg, k=k):
                    if qg == 0:
                        e.wait_ge(SS.h, M['p2tr', 'ev', ic * NJ + NJ - 1])
                    prev = M.get(('p2_qev', k - 2))
                    if prev is not None:
                        e.wait_ge(SS.h, prev)
                    ins = None
                    for jc in range(NJ):
                        ins = e.matmul(
                            p_q[k % 2], otst[par][:, jc * 128:(jc + 1) * 128],
                            big0[:, jc * T + qg * 512: jc * T + (qg + 1) * 512],
                            start=(jc == 0), stop=(jc == NJ - 1))
                    M['p2_mm', k] = ST.inc(ins)
                TE(_mm)

                def _ev(e, ic=ic, qg=qg, k=k):
                    e.wait_ge(ST.h, M['p2_mm', k])
                    M['p2_qev', k] = SS.inc(e.activation(
                        big1[:, ic * T + qg * 512: ic * T + (qg + 1) * 512],
                        p_q[k % 2], AF.Copy, scale=SCALE))
                SC(_ev)
        M['p2_done'] = SS.n

        def _p3v0(e):
            e.wait_ge(SS.h, M['p2_done'])
            e.memset(vpb[0][:], 1.0)
            M['consts_v2'] = SV.inc(e.memset(vpb[1][:], 1.0))
        VE(_p3v0)

        prev_exp = [None, None]
        prev_omm = [None, None]
        prev_oev = None
        for h in range(HEADS):
            hp = h % 2

            def _gp(e, h=h, hp=hp):
                if h == 0:
                    e.wait_ge(SS.h, M['p2_done'])
                    e.wait_ge(SV.h, M['consts_v2'])
                if h >= 1:
                    e.wait_ge(ST.h, M['p3_ktr', h - 1])
                src = d_kvk[:, :, h * 128:(h + 1) * 128].rearrange(
                    "kb kk dd -> kk kb dd")
                dstk = kstg[:].rearrange("p (kb dd) -> p kb dd", dd=128)
                SG.inc(e.dma_start(out=dstk[:, 0:16], in_=src[:, 0:16]), 16)
                M['p3_kst', h] = SG.inc(e.dma_start(out=dstk[:, 16:32], in_=src[:, 16:32]), 16)
                if h >= 2:
                    e.wait_ge(ST.h, M['p3_lastO', h - 2])
                srcv = d_kvv[:, :, h * 128:(h + 1) * 128].rearrange(
                    "kb kk dd -> kk kb dd")
                dstv = vpb[hp][:].rearrange("p (kb c) -> p kb c", c=132)[:, :, 0:128]
                SG.inc(e.dma_start(out=dstv[:, 0:16], in_=srcv[:, 0:16]), 16)
                M['p3_v', h] = SG.inc(e.dma_start(out=dstv[:, 16:32], in_=srcv[:, 16:32]), 16)
            GP(_gp)

            def _pre_ktr(e, h=h):
                e.wait_ge(SG.h, M['p3_kst', h])
                if h >= 2:
                    e.wait_ge(ST.h, M['p3_lastS', h - 2])
            tr_ev(NKB,
                  (lambda j: kstg[:, j * 128:(j + 1) * 128]),
                  (lambda j, hp=hp: ktb[hp][:, j * 128:(j + 1) * 128]),
                  pt_b, eye_b, 'p3k', 'vector', SV, h * NKB, pre_tr=_pre_ktr)
            M['p3_ktr', h] = ST.n
            M['p3_kdone', h] = SV.n

            for qg in range(4):
                qbase = h * T + qg * 512
                for p in range(16):
                    def _sc_mm(e, h=h, hp=hp, qg=qg, p=p, qbase=qbase):
                        if (h, qg, p) == (0, 0, 0):
                            e.wait_ge(SS.h, M['p2_done'])
                            e.wait_ge(SV.h, M['consts_v2'])
                        if p == 0:
                            e.wait_ge(SV.h, M['p3_kdone', h])
                            e.wait_ge(SG.h, M['p3_v', h])
                        if p >= 2:
                            e.wait_ge(SS.h, M['p3_exp', h, qg, p - 2])
                        elif prev_exp[p] is not None:
                            e.wait_ge(SS.h, prev_exp[p])
                        ins = None
                        for b in range(2):
                            kb = 2 * p + b
                            ins = e.matmul(
                                p_s2[p % 2][:, b * 512:(b + 1) * 512],
                                ktb[hp][:, kb * 128:(kb + 1) * 128],
                                big1[:, qbase:qbase + 512],
                                start=True, stop=True)
                        M['p3_s', h, qg, p] = ST.inc(ins)
                    TE(_sc_mm)

                    def _exp(e, h=h, qg=qg, p=p):
                        e.wait_ge(ST.h, M['p3_s', h, qg, p])
                        if p >= 2:
                            e.wait_ge(ST.h, M['p3_omm', h, qg, p - 2])
                        elif prev_omm[p] is not None:
                            e.wait_ge(ST.h, prev_omm[p])
                        M['p3_exp', h, qg, p] = SS.inc(e.activation(
                            exps[p % 2][:], p_s2[p % 2][:], AF.Exp))
                    SC(_exp)

                    def _omm(e, h=h, hp=hp, qg=qg, p=p, prev_oev=prev_oev):
                        e.wait_ge(SS.h, M['p3_exp', h, qg, p])
                        if p == 0 and prev_oev is not None:
                            e.wait_ge(SV.h, prev_oev)
                        ins = None
                        for b in range(2):
                            kb = 2 * p + b
                            for qs in range(4):
                                ins = e.matmul(
                                    p_o[:, qs * 256: qs * 256 + 129],
                                    exps[p % 2][:, b * 512 + qs * 128: b * 512 + (qs + 1) * 128],
                                    vpb[hp][:, kb * 132: kb * 132 + 129],
                                    start=(p == 0 and b == 0 and qs % 2 == 0),
                                    stop=(p == 15 and b == 1))
                        M['p3_omm', h, qg, p] = ST.inc(ins)
                    TE(_omm)

                def _oev(e, h=h, qg=qg):
                    e.wait_ge(ST.h, M['p3_omm', h, qg, 15])
                    ins = None
                    for qs in range(4):
                        ins = e.reciprocal(dinv[:, qs:qs + 1],
                                           p_o[:, qs * 256 + 128: qs * 256 + 129])
                    m_r = SV.inc(ins)
                    e.wait_ge(SV.h, m_r)
                    ins = None
                    for qs in range(4):
                        qt = qg * 4 + qs
                        ins = e.tensor_scalar_mul(
                            big2[:, qt * D + h * 128: qt * D + (h + 1) * 128],
                            p_o[:, qs * 256: qs * 256 + 128], dinv[:, qs:qs + 1])
                    M['p3_oev', h, qg] = SV.inc(ins)
                VE(_oev)

                prev_exp = [M['p3_exp', h, qg, 14], M['p3_exp', h, qg, 15]]
                prev_omm = [M['p3_omm', h, qg, 14], M['p3_omm', h, qg, 15]]
                prev_oev = M['p3_oev', h, qg]
                if qg == 3:
                    M['p3_lastS', h] = M['p3_s', h, 3, 15]
                    M['p3_lastO', h] = M['p3_omm', h, 3, 15]
        M['p3_done_v'] = SV.n
        M['p3_done_s'] = SS.n
        M['p3_done_t'] = ST.n

        regB.close()
        p_w = regC.enter_context(nc.psum_tensor("p_w", [128, 2048], F32))
        pt_f = [regC.enter_context(nc.psum_tensor(f"pt_f{p}", [128, 128], F32))
                for p in range(2)]

        for oc in range(NJ):
            par = oc % 2

            def _gp(e, oc=oc, par=par):
                prev = M.get(('p4t', 'trdone', oc - 2))
                if prev is not None:
                    e.wait_ge(ST.h, prev)
                elif oc < 2:
                    e.wait_ge(ST.h, M['p3_done_t'])
                M['p4_w', oc] = SG.inc(e.dma_start(
                    out=wstage[par][:], in_=d_wo[oc * 128:(oc + 1) * 128, :]), 16)
            GP(_gp)

            def _pre_tr(e, oc=oc):
                e.wait_ge(SG.h, M['p4_w', oc])
                if oc == 0:
                    e.wait_ge(SS.h, M['p3_done_s'])
                    e.wait_ge(SV.h, M['p3_done_v'])
            tr_ev(NJ,
                  (lambda j, par=par: wstage[par][:, j * 128:(j + 1) * 128]),
                  (lambda j, oc=oc: big1[:, j * T + oc * 128: j * T + (oc + 1) * 128]),
                  pt_b, eye_b, 'p4w', 'vector', SV, oc * NJ, pre_tr=_pre_tr)
            M['p4t', 'trdone', oc] = ST.n
        M['p4_wot'] = SV.n

        for qt in range(NT):
            par = qt % 2

            def _pre_otr(e, qt=qt):
                if qt == 0:
                    e.wait_ge(SV.h, M['p4_wot'])
            tr_ev(NJ,
                  (lambda j, qt=qt: big2[:, qt * D + j * 128: qt * D + (j + 1) * 128]),
                  (lambda j, par=par: otst[par][:, j * 128:(j + 1) * 128]),
                  pt_b, eye_b, 'p4o', 'scalar', SS, qt * NJ, pre_tr=_pre_otr)

            def _sy1(e, qt=qt, par=par):
                prev = M.get(('p4_x1', qt - 2))
                if prev is not None:
                    e.wait_ge(SV.h, prev)
                elif qt < 2:
                    e.wait_ge(ST.h, M['p3_done_t'])
                M['p4_x', qt] = SX.inc(e.dma_start(
                    out=xstage[par][:], in_=d_x[qt * 128:(qt + 1) * 128, :]), 16)
            SY(_sy1)

            def _mm(e, qt=qt, par=par):
                e.wait_ge(SS.h, M['p4o', 'ev', qt * NJ + NJ - 1])
                prev = M.get(('p4_x1', qt - 1))
                if prev is not None:
                    e.wait_ge(SV.h, prev)
                ins = None
                for ob in range(4):
                    for ic in range(NJ):
                        ins = e.matmul(
                            p_w[:, ob * 512:(ob + 1) * 512],
                            otst[par][:, ic * 128:(ic + 1) * 128],
                            big1[:, ic * T + ob * 512: ic * T + (ob + 1) * 512],
                            start=(ic == 0), stop=(ic == NJ - 1))
                M['p4_mm', qt] = ST.inc(ins)
            TE(_mm)

            def _vx1(e, qt=qt, par=par):
                e.wait_ge(ST.h, M['p4_mm', qt])
                e.wait_ge(SX.h, M['p4_x', qt])
                if qt >= 2:
                    e.wait_ge(SX.h, M['p4_x1st', qt - 2])
                M['p4_x1', qt] = SV.inc(e.tensor_tensor(
                    x1buf[par][:], p_w[:], xstage[par][:], ALU.add))
            VE(_vx1)

            def _sy2(e, qt=qt, par=par):
                e.wait_ge(SV.h, M['p4_x1', qt])
                M['p4_x1st', qt] = SX.inc(e.dma_start(
                    out=d_x1[qt * 128:(qt + 1) * 128, :], in_=x1buf[par][:]), 16)
            SY(_sy2)
        M['p4a_done_st'] = ST.n

        for qt in range(NT):
            par = qt % 2

            def _syb(e, qt=qt, par=par):
                e.wait_ge(SX.h, M['p4_x1st', qt])
                if qt >= 2:
                    e.wait_ge(SS.h, M['ln2', 'sq', qt - 2])
                    e.wait_ge(ST.h, M['p4x1t', 'trdone', qt - 2])
                M['p4b_x', qt] = SX.inc(e.dma_start(
                    out=x1buf[par][:], in_=d_x1[qt * 128:(qt + 1) * 128, :]), 16)
            SY(_syb)

            def _vw(e, qt=qt):
                e.wait_ge(SX.h, M['p4b_x', qt])
                prev = M.get(('p4l', 'trdone', qt - 2))
                if prev is not None:
                    e.wait_ge(ST.h, prev)
            VE(_vw)

            emit_ln('ln2', x1buf[par], lnstage[par], qt,
                    pre_scalar=(lambda e, qt=qt: e.wait_ge(SX.h, M['p4b_x', qt])))

            def _pre_lntr(e, qt=qt):
                e.wait_ge(SV.h, M['ln2', 'norm', qt])

            def _pre_lnev(e, qt=qt):
                prev = M.get(('p4_lnst', qt - 2))
                if prev is not None:
                    e.wait_ge(SX.h, prev)
            tr_ev(NJ,
                  (lambda j, par=par: lnstage[par][:, j * 128:(j + 1) * 128]),
                  (lambda j, par=par: ln2Tbuf[par][:, j * 128:(j + 1) * 128]),
                  pt_b, eye_b, 'p4l', 'vector', SV, qt * NJ,
                  pre_tr=_pre_lntr, pre_ev=_pre_lnev)
            M['p4l', 'trdone', qt] = ST.n
            M['p4_lnevd', qt] = SV.n

            def _pre_x1ev(e, qt=qt):
                prev = M.get(('p4_x1Tst', qt - 2))
                if prev is not None:
                    e.wait_ge(SX.h, prev)
            tr_ev(NJ,
                  (lambda j, par=par: x1buf[par][:, j * 128:(j + 1) * 128]),
                  (lambda j, par=par: x1Tbuf[par][:, j * 128:(j + 1) * 128]),
                  pt_f, eye_f, 'p4x1t', 'scalar', SS, qt * NJ, pre_ev=_pre_x1ev)
            M['p4x1t', 'trdone', qt] = ST.n
            M['p4_x1evd', qt] = SS.n

            def _sy3(e, qt=qt, par=par):
                e.wait_ge(SV.h, M['p4_lnevd', qt])
                M['p4_lnst', qt] = SX.inc(e.dma_start(
                    out=d_ln2T[:, qt * 128:(qt + 1) * 128].rearrange(
                        "(jc p) t -> p jc t", p=128),
                    in_=ln2Tbuf[par][:].rearrange("p (jc t) -> p jc t", t=128)), 16)
                e.wait_ge(SS.h, M['p4_x1evd', qt])
                M['p4_x1Tst', qt] = SX.inc(e.dma_start(
                    out=d_x1T[:, qt * 128:(qt + 1) * 128].rearrange(
                        "(jc p) t -> p jc t", p=128),
                    in_=x1Tbuf[par][:].rearrange("p (jc t) -> p jc t", t=128)), 16)
            SY(_sy3)
        M['p4_done_ss'] = SS.n
        M['p4_done_sv'] = SV.n
        M['p4_done_st'] = ST.n

        regC.close()
        ptb_stack.close()
        p_r = regD.enter_context(nc.psum_tensor("p_r", [128, 17], F32))
        pt_f5 = [regD.enter_context(nc.psum_tensor(f"pt_f5{p}", [128, 128], F32))
                 for p in range(2)]

        for qt in range(NT):
            def _sy(e, qt=qt):
                if qt == 0:
                    e.wait_ge(SS.h, M['p4_done_ss'])
                    e.wait_ge(SV.h, M['p4_done_sv'])
                    e.wait_ge(ST.h, M['p4_done_st'])
                else:
                    e.wait_ge(ST.h, M['p5_mm', qt - 1])
                M['p5_ln', qt] = SX.inc(e.dma_start(
                    out=routst[:].rearrange("p (jc t) -> p jc t", t=128),
                    in_=d_ln2T[:, qt * 128:(qt + 1) * 128].rearrange(
                        "(jc p) t -> p jc t", p=128)), 16)
            SY(_sy)

            def _mm(e, qt=qt):
                e.wait_ge(SX.h, M['p5_ln', qt])
                if qt == 0:
                    e.wait_ge(SG.h, M['wr'])
                else:
                    e.wait_ge(SS.h, M['p5_pr', qt - 1])
                ins = None
                for jc in range(NJ):
                    ins = e.matmul(p_r[:], routst[:, jc * 128:(jc + 1) * 128],
                                   wr17b[:, jc * 17:(jc + 1) * 17],
                                   start=(jc == 0), stop=(jc == NJ - 1))
                M['p5_mm', qt] = ST.inc(ins)
            TE(_mm)

            def _sc(e, qt=qt):
                e.wait_ge(ST.h, M['p5_mm', qt])
                prev = M.get(('p5_comb', qt - 1))
                if prev is not None:
                    e.wait_ge(SV.h, prev)
                e.activation(pe_r[:], p_r[:, 0:16], AF.Exp)
                M['p5_pr', qt] = SS.inc(e.activation(
                    sgcol[:, qt:qt + 1], p_r[:, 16:17], AF.Sigmoid))
            SC(_sc)

            def _v(e, qt=qt):
                e.wait_ge(SS.h, M['p5_pr', qt])
                m1 = SV.inc(e.reduce_sum(sv[:, 8:9], pe_r[:], axis=AX.X))
                e.wait_ge(SV.h, m1)
                m2 = SV.inc(e.reciprocal(sv[:, 9:10], sv[:, 8:9]))
                mx = SV.inc(e.max(pwork[:, 0:8], pe_r[:]))
                e.wait_ge(SV.h, mx)
                e.tensor_copy(pwork[:, 8:12], pwork[:, 0:4])
                mc = SV.inc(e.tensor_copy(pwork[:, 12:16], pwork[:, 0:4]))
                e.wait_ge(SV.h, mc)
                mr = SV.inc(e.match_replace(pmask[:], pwork[:, 8:16], pe_r[:], 0.0))
                e.wait_ge(SV.h, mr)
                e.tensor_tensor(cacc[:], pe_r[:], pmask[:], ALU.subtract)
                e.wait_ge(SV.h, m2)
                M['p5_comb', qt] = SV.inc(e.tensor_scalar_mul(
                    comb[:, qt * 16:(qt + 1) * 16], cacc[:], sv[:, 9:10]))
            VE(_v)

            def _ctr(e, qt=qt):
                e.wait_ge(SV.h, M['p5_comb', qt])
                prev = M.get(('p5_cev', qt - 2))
                if prev is not None:
                    e.wait_ge(SV.h, prev)
                M['p5_ctr', qt] = ST.inc(e.transpose(
                    pt_f5[qt % 2][0:16, :], comb[:, qt * 16:(qt + 1) * 16], eye_f[:]))
            TE(_ctr)

            def _cev(e, qt=qt):
                e.wait_ge(ST.h, M['p5_ctr', qt])
                M['p5_cev', qt] = SV.inc(e.tensor_copy(
                    combT[:, qt * 128:(qt + 1) * 128], pt_f5[qt % 2][0:16, :]))
            VE(_cev)

        def _str(e):
            e.wait_ge(SS.h, M['p5_pr', NT - 1])
            e.wait_ge(SV.h, M['p5_cev', NT - 2])
            M['p5_str'] = ST.inc(e.transpose(pt_f5[0][0:16, :], sgcol[:, 0:16], eye_f[:]))
        TE(_str)

        def _sgev(e):
            e.wait_ge(ST.h, M['p5_str'])
            M['p5_sgT'] = SV.inc(e.tensor_copy(sgT[:], pt_f5[0][0:16, :]))
        VE(_sgev)
        M['p5_done_v'] = SV.n
        M['p5_done_s'] = SS.n
        M['p5_done_t'] = ST.n

        regD.close()
        p_gu = [regE.enter_context(nc.psum_tensor(f"p_gu{p}", [128, 512], F32))
                for p in range(4)]
        p_y = [regE.enter_context(nc.psum_tensor(f"p_y{p}", [128, 512], F32))
               for p in range(2)]
        pt_f6 = [regE.enter_context(nc.psum_tensor(f"pt_f6{p}", [128, 128], F32))
                 for p in range(2)]

        prev_guh = None
        prev_yh = None
        prev_yev = None
        prev_sgevac = None
        prev_uevac = None
        prev_selread = None
        for Hh in range(2):
            tof = Hh * HALF

            def _sy(e, Hh=Hh, tof=tof):
                if Hh == 0:
                    e.wait_ge(SV.h, M['p5_done_v'])
                    e.wait_ge(SS.h, M['p5_done_s'])
                    e.wait_ge(ST.h, M['p5_done_t'])
                else:
                    e.wait_ge(ST.h, M['p6o', 'tr', 0, NTH * NJ - 1])
                M['p6_lnh', Hh] = SX.inc(e.dma_start(
                    out=ln2h[:].rearrange("p (jc t) -> p jc t", t=HALF),
                    in_=d_ln2T[:, tof:tof + HALF].rearrange(
                        "(jc p) t -> p jc t", p=128)), 16)
                M['p6_acc', Hh] = SX.inc(e.dma_start(
                    out=accT[:].rearrange("p (hc t) -> p hc t", t=HALF),
                    in_=d_x1T[:, tof:tof + HALF].rearrange(
                        "(hc p) t -> p hc t", p=128)), 16)
            SY(_sy)

            for q8 in range(NTH):
                qt = Hh * NTH + q8

                def _vsel(e, q8=q8, qt=qt, Hh=Hh, prev_selread=prev_selread):
                    if (Hh, q8) == (0, 0):
                        e.wait_ge(SV.h, M['p5_sgT'])
                    if prev_selread is not None:
                        e.wait_ge(ST.h, prev_selread)
                    M['p6_sel', Hh, q8] = SV.inc(e.tensor_scalar_mul(
                        sel_b[:], ones16[:], eye16[:, qt:qt + 1]))
                VE(_vsel)

                def _tsg(e, q8=q8, Hh=Hh, prev_sgevac=prev_sgevac,
                         prev_uevac=prev_uevac):
                    e.wait_ge(SV.h, M['p6_sel', Hh, q8])
                    prev = M.get(('p6_sgev', Hh, q8 - 1))
                    if prev is not None:
                        e.wait_ge(SV.h, prev)
                    elif Hh == 0:
                        e.wait_ge(SS.h, M['p5_done_s'])
                    else:
                        e.wait_ge(SS.h, prev_sgevac)
                        e.wait_ge(SV.h, prev_uevac)
                    M['p6_sgmm', Hh, q8] = ST.inc(e.matmul(
                        p_gu[0][:, 0:128], sel_b[:], sgT[:], start=True, stop=True))
                TE(_tsg)
                prev_selread = M['p6_sgmm', Hh, q8]

                def _vsg(e, q8=q8, Hh=Hh):
                    e.wait_ge(ST.h, M['p6_sgmm', Hh, q8])
                    M['p6_sgev', Hh, q8] = SV.inc(e.tensor_copy(
                        sg_b[:, q8 * 128:(q8 + 1) * 128], p_gu[0][:, 0:128]))
                VE(_vsg)

            for u in range(E + NBAND):
                is_sh = u >= E
                bb = u - E

                if not is_sh:
                    def _vsel2(e, u=u, Hh=Hh, prev_selread=prev_selread):
                        e.wait_ge(ST.h, prev_selread)
                        M['p6_sel2', Hh, u] = SV.inc(e.tensor_scalar_mul(
                            sel_b[:], ones16[:], eye16[:, u:u + 1]))
                    VE(_vsel2)

                    def _tcb(e, u=u, Hh=Hh, tof=tof,
                             prev_sgevac=prev_sgevac, prev_uevac=prev_uevac):
                        e.wait_ge(SV.h, M['p6_sel2', Hh, u])
                        if prev_sgevac is not None:
                            e.wait_ge(SS.h, prev_sgevac)
                            e.wait_ge(SV.h, prev_uevac)
                        else:
                            e.wait_ge(SV.h, M['p6_sgev', 0, NTH - 1])
                        e.matmul(p_gu[0][:], sel_b[:],
                                 combT[:, tof:tof + 512], start=True, stop=True)
                        ins = e.matmul(p_gu[1][:], sel_b[:],
                                       combT[:, tof + 512:tof + 1024],
                                       start=True, stop=True)
                        M['p6_cbmm', Hh, u] = ST.inc(ins)
                    TE(_tcb)
                    prev_selread = M['p6_cbmm', Hh, u]

                    def _vcb(e, u=u, Hh=Hh):
                        e.wait_ge(ST.h, M['p6_cbmm', Hh, u])
                        e.tensor_copy(cb_b[:, 0:512], p_gu[0][:])
                        M['p6_cbev', Hh, u] = SV.inc(
                            e.tensor_copy(cb_b[:, 512:1024], p_gu[1][:]))
                    VE(_vcb)

                for icc in range(NI):
                    gp = icc % 2

                    def _gpw(e, u=u, icc=icc, gp=gp, Hh=Hh, is_sh=is_sh, bb=bb,
                             prev_guh=prev_guh):
                        if (Hh, u, icc) == (0, 0, 0):
                            e.wait_ge(ST.h, M['p5_done_t'])
                        prev = M.get(('p6_guh', Hh, u, icc - 2))
                        if prev is not None:
                            e.wait_ge(ST.h, prev)
                        elif prev_guh is not None:
                            e.wait_ge(ST.h, prev_guh)
                        if is_sh:
                            colg = d_wsg[:, (bb * NI + icc) * 128:(bb * NI + icc + 1) * 128]
                            colu = d_wsu[:, (bb * NI + icc) * 128:(bb * NI + icc + 1) * 128]
                        else:
                            colg = d_wg[u][:, icc * 128:(icc + 1) * 128]
                            colu = d_wu[u][:, icc * 128:(icc + 1) * 128]
                        SG.inc(e.dma_start(
                            out=wgc[gp][:].rearrange("p (jc i) -> p jc i", i=128),
                            in_=colg.rearrange("(jc p) i -> p jc i", p=128)), 16)
                        M['p6_w', Hh, u, icc] = SG.inc(e.dma_start(
                            out=wuc[gp][:].rearrange("p (jc i) -> p jc i", i=128),
                            in_=colu.rearrange("(jc p) i -> p jc i", p=128)), 16)
                    GP(_gpw)

                    def _tgu(e, u=u, icc=icc, gp=gp, Hh=Hh, is_sh=is_sh,
                             prev_sgevac=prev_sgevac, prev_uevac=prev_uevac,
                             prev_yh=prev_yh):
                        e.wait_ge(SG.h, M['p6_w', Hh, u, icc])
                        if icc == 0:
                            e.wait_ge(SX.h, M['p6_lnh', Hh])
                            if not is_sh:
                                e.wait_ge(SV.h, M['p6_cbev', Hh, u])
                            elif prev_sgevac is not None:
                                e.wait_ge(SS.h, prev_sgevac)
                                e.wait_ge(SV.h, prev_uevac)
                        prev = M.get(('p6_sgevac', Hh, u, icc - 1))
                        if prev is not None:
                            e.wait_ge(SS.h, prev)
                            e.wait_ge(SV.h, M['p6_uevac', Hh, u, icc - 1])
                        ins = None
                        for tg in range(2):
                            for jc in range(NJ):
                                ins = e.matmul(
                                    p_gu[tg][:], wgc[gp][:, jc * 128:(jc + 1) * 128],
                                    ln2h[:, jc * HALF + tg * 512: jc * HALF + (tg + 1) * 512],
                                    start=(jc == 0), stop=(jc == NJ - 1))
                        M['p6_gmm', Hh, u, icc] = ST.inc(ins)
                        ins = None
                        for tg in range(2):
                            for jc in range(NJ):
                                ins = e.matmul(
                                    p_gu[2 + tg][:], wuc[gp][:, jc * 128:(jc + 1) * 128],
                                    ln2h[:, jc * HALF + tg * 512: jc * HALF + (tg + 1) * 512],
                                    start=(jc == 0), stop=(jc == NJ - 1))
                        M['p6_umm', Hh, u, icc] = ST.inc(ins)
                        M['p6_guh', Hh, u, icc] = ST.n
                    TE(_tgu)

                    def _ssi(e, u=u, icc=icc, Hh=Hh):
                        e.wait_ge(ST.h, M['p6_gmm', Hh, u, icc])
                        prev = M.get(('p6_aev', Hh, u, icc - 1))
                        if prev is not None:
                            e.wait_ge(SV.h, prev)
                        e.activation(sgb_[:, 0:512], p_gu[0][:], AF.Silu)
                        M['p6_sgevac', Hh, u, icc] = SS.inc(e.activation(
                            sgb_[:, 512:1024], p_gu[1][:], AF.Silu))
                    SC(_ssi)

                    def _vu(e, u=u, icc=icc, Hh=Hh, is_sh=is_sh, prev_yh=prev_yh):
                        e.wait_ge(ST.h, M['p6_umm', Hh, u, icc])
                        prev = M.get(('p6_aev', Hh, u, icc - 1))
                        if prev is not None:
                            e.wait_ge(SV.h, prev)
                        gate_src = sg_b if is_sh else cb_b
                        e.tensor_tensor(ubuf[:, 0:512], p_gu[2][:],
                                        gate_src[:, 0:512], ALU.mult)
                        M['p6_uevac', Hh, u, icc] = SV.inc(e.tensor_tensor(
                            ubuf[:, 512:1024], p_gu[3][:],
                            gate_src[:, 512:1024], ALU.mult))
                        e.wait_ge(SS.h, M['p6_sgevac', Hh, u, icc])
                        if icc == 0 and prev_yh is not None:
                            e.wait_ge(ST.h, prev_yh)
                        M['p6_aev', Hh, u, icc] = SV.inc(e.tensor_tensor(
                            a_sb[:, icc * HALF:(icc + 1) * HALF], sgb_[:], ubuf[:],
                            ALU.mult))
                    VE(_vu)
                prev_guh = M['p6_guh', Hh, u, NI - 1]
                prev_sgevac = M['p6_sgevac', Hh, u, NI - 1]
                prev_uevac = M['p6_uevac', Hh, u, NI - 1]
                M['p6_adone', Hh, u] = SV.n

                for hc in range(NJ):
                    wp = hc % 2

                    def _gpd(e, u=u, hc=hc, wp=wp, Hh=Hh, is_sh=is_sh, bb=bb,
                             prev_yh=prev_yh):
                        prev = M.get(('p6_yh', Hh, u, hc - 2))
                        if prev is not None:
                            e.wait_ge(ST.h, prev)
                        elif prev_yh is not None:
                            e.wait_ge(ST.h, prev_yh)
                        if is_sh:
                            blk = d_wsd[bb * I_:(bb + 1) * I_, hc * 128:(hc + 1) * 128]
                        else:
                            blk = d_wd[u][:, hc * 128:(hc + 1) * 128]
                        M['p6_wd', Hh, u, hc] = SG.inc(e.dma_start(
                            out=wdc[wp][:].rearrange("p (ic hh) -> p ic hh", hh=128),
                            in_=blk.rearrange("(ic p) hh -> p ic hh", p=128)), 16)
                    GP(_gpd)

                    def _tdn(e, u=u, hc=hc, wp=wp, Hh=Hh, prev_yev=prev_yev):
                        e.wait_ge(SG.h, M['p6_wd', Hh, u, hc])
                        if hc == 0:
                            e.wait_ge(SV.h, M['p6_adone', Hh, u])
                            e.wait_ge(SX.h, M['p6_acc', Hh])
                            if (Hh, u) == (0, 0):
                                e.wait_ge(SS.h, M['p5_done_s'])
                        ins = None
                        for tg in range(2):
                            y = hc * 2 + tg
                            prev = M.get(('p6_yev', Hh, u, y - 2))
                            if prev is not None:
                                e.wait_ge(SV.h, prev)
                            elif prev_yev is not None:
                                e.wait_ge(SV.h, prev_yev)
                            for icc in range(NI):
                                ins = e.matmul(
                                    p_y[y % 2][:], wdc[wp][:, icc * 128:(icc + 1) * 128],
                                    a_sb[:, icc * HALF + tg * 512: icc * HALF + (tg + 1) * 512],
                                    start=(icc == 0), stop=(icc == NI - 1))
                            M['p6_ymm', Hh, u, y] = ST.inc(ins)
                        M['p6_yh', Hh, u, hc] = ST.n
                    TE(_tdn)

                    def _vy(e, u=u, hc=hc, Hh=Hh):
                        for tg in range(2):
                            y = hc * 2 + tg
                            e.wait_ge(ST.h, M['p6_ymm', Hh, u, y])
                            dst = accT[:, hc * HALF + tg * 512: hc * HALF + (tg + 1) * 512]
                            M['p6_yev', Hh, u, y] = SV.inc(e.tensor_tensor(
                                dst, dst, p_y[y % 2][:], ALU.add))
                    VE(_vy)
                prev_yh = M['p6_yh', Hh, u, NJ - 1]
                prev_yev = M['p6_yev', Hh, u, 2 * NJ - 1]

            for q8 in range(NTH):
                op = q8 % 2

                def _pre_otr(e, q8=q8, Hh=Hh, prev_yev=prev_yev):
                    if q8 == 0:
                        e.wait_ge(SV.h, prev_yev)
                    prev = M.get(('p6_obst', Hh, q8 - 2))
                    if prev is not None:
                        e.wait_ge(SO.h, prev)
                tr_ev(NJ,
                      (lambda j, q8=q8: accT[:, j * HALF + q8 * 128: j * HALF + (q8 + 1) * 128]),
                      (lambda j, op=op: outbuf[op][:, j * 128:(j + 1) * 128]),
                      pt_f6, eye_f, ('p6o', Hh), 'scalar', SS, q8 * NJ,
                      pre_tr=_pre_otr)
                M['p6_obuf', Hh, q8] = SS.n

                def _syo(e, q8=q8, op=op, Hh=Hh, tof=tof):
                    e.wait_ge(SS.h, M['p6_obuf', Hh, q8])
                    r0 = tof + q8 * 128
                    M['p6_obst', Hh, q8] = SO.inc(e.dma_start(
                        out=d_out[r0:r0 + 128, :], in_=outbuf[op][:]), 16)
                SY(_syo)
            M['p6o', 'tr', Hh, NTH * NJ - 1] = M[('p6o', Hh), 'tr', NTH * NJ - 1]

        def _fin(e):
            e.wait_ge(SO.h, SO.n)
        SY(_fin)

    return nc


_NC_CACHE = None


def _get_program():
    global _NC_CACHE
    if _NC_CACHE is None:
        _NC_CACHE = build_program()
    return _NC_CACHE


def kernel(L0_attn_query, L0_attn_kv_k, L0_attn_kv_v, L0_attn_Wq, L0_attn_Wo,
           L0_ffn_w_router, L0_ffn_w_gate, L0_ffn_w_up, L0_ffn_w_down,
           L0_ffn_ws_gate, L0_ffn_ws_up, L0_ffn_ws_down, L0_ffn_w_shared_gate):
    xq = np.ascontiguousarray(np.asarray(L0_attn_query, dtype=np.float32)).reshape(-1, D)
    wr17 = np.ascontiguousarray(
        np.concatenate([np.asarray(L0_ffn_w_router, dtype=np.float32),
                        np.asarray(L0_ffn_w_shared_gate, dtype=np.float32)], axis=1))
    common = {
        "kv_k": np.ascontiguousarray(np.asarray(L0_attn_kv_k, dtype=np.float32)),
        "kv_v": np.ascontiguousarray(np.asarray(L0_attn_kv_v, dtype=np.float32)),
        "Wq": np.ascontiguousarray(np.asarray(L0_attn_Wq, dtype=np.float32)),
        "Wo": np.ascontiguousarray(np.asarray(L0_attn_Wo, dtype=np.float32)),
        "wr17": wr17,
        "w_gate": np.ascontiguousarray(np.asarray(L0_ffn_w_gate, dtype=np.float32)),
        "w_up": np.ascontiguousarray(np.asarray(L0_ffn_w_up, dtype=np.float32)),
        "w_down": np.ascontiguousarray(np.asarray(L0_ffn_w_down, dtype=np.float32)),
        "ws_gate": np.ascontiguousarray(np.asarray(L0_ffn_ws_gate, dtype=np.float32)),
        "ws_up": np.ascontiguousarray(np.asarray(L0_ffn_ws_up, dtype=np.float32)),
        "ws_down": np.ascontiguousarray(np.asarray(L0_ffn_ws_down, dtype=np.float32)),
        "eye_b": np.eye(128, dtype=np.float32).astype(ml_dtypes.bfloat16),
        "eye_f": np.eye(128, dtype=np.float32),
        "eye16": np.eye(16, dtype=np.float32),
    }
    in_maps = []
    for c in range(NCORES):
        m = dict(common)
        m["xq"] = np.ascontiguousarray(xq[c * T:(c + 1) * T, :])
        in_maps.append(m)
    nc = _get_program()
    res = run_bass_kernel_spmd(nc, in_maps, core_ids=list(range(NCORES)))
    out = np.concatenate([np.asarray(r["out"]) for r in res.results], axis=0)
    return out.reshape(4, 4096, 2048).astype(np.float32)


# version 06
# speedup vs baseline: 1.3030x; 1.3030x over previous
import math
from contextlib import ExitStack

import numpy as np
import ml_dtypes
import concourse.bass as bass
import concourse.mybir as mybir
from concourse.bass_utils import run_bass_kernel_spmd

F32 = mybir.dt.float32
BF16 = mybir.dt.bfloat16
AF = mybir.ActivationFunctionType
ALU = mybir.AluOpType
AX = mybir.AxisListType

NCORES = 8
T = 2048
D = 2048
NT = T // 128
NJ = D // 128
HEADS = 16
KV = 4096
NKB = KV // 128
E = 16
I_ = 1408
NI = I_ // 128
IS = 5632
NBAND = 4
SCALE = 1.0 / math.sqrt(128.0)
HALF = T // 2
NTH = HALF // 128


class Sem:
    def __init__(self, nc, name):
        self.h = nc.alloc_semaphore(name)
        self.n = 0

    def inc(self, instr, k=1):
        instr.then_inc(self.h, k)
        self.n += k
        return self.n


def build_program():
    nc = bass.Bass("TRN2", target_bir_lowering=False)

    d_x = nc.dram_tensor("xq", [T, D], F32, kind="ExternalInput").ap()
    d_kvk = nc.dram_tensor("kv_k", [NKB, 128, D], F32, kind="ExternalInput").ap()
    d_kvv = nc.dram_tensor("kv_v", [NKB, 128, D], F32, kind="ExternalInput").ap()
    d_wq = nc.dram_tensor("Wq", [D, D], F32, kind="ExternalInput").ap()
    d_wo = nc.dram_tensor("Wo", [D, D], F32, kind="ExternalInput").ap()
    d_wr17 = nc.dram_tensor("wr17", [D, 17], F32, kind="ExternalInput").ap()
    d_wg = nc.dram_tensor("w_gate", [E, D, I_], F32, kind="ExternalInput").ap()
    d_wu = nc.dram_tensor("w_up", [E, D, I_], F32, kind="ExternalInput").ap()
    d_wd = nc.dram_tensor("w_down", [E, I_, D], F32, kind="ExternalInput").ap()
    d_wsg = nc.dram_tensor("ws_gate", [D, IS], F32, kind="ExternalInput").ap()
    d_wsu = nc.dram_tensor("ws_up", [D, IS], F32, kind="ExternalInput").ap()
    d_wsd = nc.dram_tensor("ws_down", [IS, D], F32, kind="ExternalInput").ap()
    d_eyeb = nc.dram_tensor("eye_b", [128, 128], BF16, kind="ExternalInput").ap()
    d_eyef = nc.dram_tensor("eye_f", [128, 128], F32, kind="ExternalInput").ap()
    d_eye16 = nc.dram_tensor("eye16", [16, 16], F32, kind="ExternalInput").ap()
    d_out = nc.dram_tensor("out", [T, D], F32, kind="ExternalOutput").ap()
    d_x1 = nc.dram_tensor("x1_scr", [T, D], F32, kind="Internal").ap()
    d_x1T = nc.dram_tensor("x1T_scr", [D, T], F32, kind="Internal").ap()
    d_ln2T = nc.dram_tensor("ln2T_scr", [D, T], BF16, kind="Internal").ap()

    A = nc.alloc_sbuf_tensor
    eye_b = A("s_eyeb", [128, 128], BF16)
    eye_f = A("s_eyef", [128, 128], F32)
    eye16 = A("eye16s", [16, 16], F32)
    ones16 = A("ones16", [16, 128], F32)
    sel_b = A("sel_b", [16, 128], F32)
    eps_t = A("eps_t", [128, 1], F32)
    wr17b = A("wr17b", [128, NJ * 17], BF16)
    comb = A("comb", [128, NT * 16], F32)
    combT = A("combT", [16, T], F32)
    sgcol = A("sgcol", [128, NT], F32)
    sgT = A("sgT", [16, 128], F32)
    sv = A("sv", [128, 32], F32)
    dinv = A("dinv", [128, 4], F32)
    pmask = A("pmask", [128, 16], F32)
    pwork = A("pwork", [128, 16], F32)
    cacc = A("cacc", [128, 16], F32)
    pe_r = A("pe_r", [128, 16], F32)

    KB = 1024
    MB = (nc.sbuf_base + 31) & ~31
    AT = nc.alloc_sbuf_tensor_at

    def at(name, shape, dtype, off):
        return AT(name, shape, dtype, offset=MB + off)

    big1 = at("big1", [128, NJ * T], BF16, 0)
    big0 = at("big0", [128, NJ * T], BF16, 64 * KB)
    big2 = at("big2", [128, NT * D], BF16, 64 * KB)
    xstage = [at(f"xstage{p}", [128, D], F32, 128 * KB + p * 8 * KB) for p in range(2)]
    lnstage = [at(f"lnstage{p}", [128, D], BF16, 144 * KB + p * 4 * KB) for p in range(2)]
    xsq = at("xsq", [128, D], F32, 152 * KB)
    wstage = [at(f"wstage{p}", [128, D], BF16, 160 * KB + p * 4 * KB) for p in range(2)]
    otst = [at(f"otst{p}", [128, NJ * 128], BF16, 168 * KB + p * 4 * KB) for p in range(2)]
    x1buf = [at(f"x1buf{p}", [128, D], F32, 176 * KB + p * 8 * KB) for p in range(2)]
    kstg = at("kstg", [128, NKB * 128], BF16, 128 * KB)
    ktb = [at(f"ktb{p}", [128, NKB * 128], BF16, 136 * KB + p * 8 * KB) for p in range(2)]
    vpb = [at(f"vpb{p}", [128, NKB * 132], BF16, 152 * KB + p * 8448) for p in range(2)]
    exps = [at(f"exps{p}", [128, 1024], BF16, 152 * KB + 16896 + p * 2048) for p in range(2)]
    x1Tbuf = [at(f"x1Tbuf{p}", [128, NJ * 128], F32, 128 * KB + p * 8 * KB) for p in range(2)]
    ln2Tbuf = [at(f"ln2Tbuf{p}", [128, NJ * 128], BF16, 160 * KB + p * 4 * KB) for p in range(2)]
    routst = at("routst", [128, NJ * 128], BF16, 168 * KB)
    ln2h = at("ln2h", [128, NJ * HALF], BF16, 0)
    accT = at("accT", [128, NJ * HALF], F32, 32 * KB)
    a_sb = at("a_sb", [128, NI * HALF], BF16, 96 * KB)
    wgc = [at(f"wgc{p}", [128, NJ * 128], BF16, 118 * KB + p * 4 * KB) for p in range(2)]
    wuc = [at(f"wuc{p}", [128, NJ * 128], BF16, 126 * KB + p * 4 * KB) for p in range(2)]
    wdc = [at(f"wdc{p}", [128, NI * 128], BF16, 134 * KB + p * 2816) for p in range(2)]
    sgb_ = at("sgb", [128, 1024], F32, 140 * KB)
    ubuf = at("ubuf", [128, 1024], F32, 144 * KB)
    cb_b = at("cb_b", [128, HALF], F32, 148 * KB)
    sg_b = at("sg_b", [128, HALF], F32, 152 * KB)
    outbuf = [at(f"outbuf{p}", [128, D], F32, 156 * KB + p * 8 * KB) for p in range(2)]

    SX = Sem(nc, "SX")
    SG = Sem(nc, "SG")
    SV = Sem(nc, "SV")
    SS = Sem(nc, "SS")
    ST = Sem(nc, "ST")
    SO = Sem(nc, "SO")
    M = {}

    ptb_stack = ExitStack()
    regB = ExitStack()
    regC = ExitStack()
    regD = ExitStack()
    regE = ExitStack()
    pt_b = [ptb_stack.enter_context(nc.psum_tensor(f"pt_b{p}", [128, 128], BF16))
            for p in range(2)]
    p_s2 = [regB.enter_context(nc.psum_tensor(f"p_s2{p}", [128, 1024], F32))
            for p in range(2)]
    p_o = regB.enter_context(nc.psum_tensor("p_o", [128, 1024], F32))
    p_q = [p_o[:, 0:512], p_o[:, 512:1024]]

    with nc.Block() as block:
        TE, SC, VE, GP, SY = (block.tensor, block.scalar, block.vector,
                              block.gpsimd, block.sync)

        def tr_ev(n, tr_src, ev_dst, pt, eye, tag, ev_eng, ev_sem, gbase,
                  pre_tr=None, pre_ev=None):
            ev_base = ev_sem.n

            def _t(e):
                if pre_tr is not None:
                    pre_tr(e)
                for j in range(n):
                    g = gbase + j
                    if j < 2:
                        prev = M.get((tag, 'ev', g - 2))
                        if prev is not None:
                            e.wait_ge(ev_sem.h, prev)
                    else:
                        e.wait_ge(ev_sem.h, ev_base + (j - 2) + 1)
                    M[tag, 'tr', g] = ST.inc(e.transpose(pt[g % 2][:], tr_src(j), eye[:]))
            TE(_t)

            def _e(e):
                if pre_ev is not None:
                    pre_ev(e)
                for j in range(n):
                    g = gbase + j
                    e.wait_ge(ST.h, M[tag, 'tr', g])
                    if ev_eng == 'scalar':
                        ins = e.copy(ev_dst(j), pt[g % 2][:])
                    else:
                        ins = e.tensor_copy(ev_dst(j), pt[g % 2][:])
                    M[tag, 'ev', g] = ev_sem.inc(ins)
                    assert M[tag, 'ev', g] == ev_base + j + 1
            (SC if ev_eng == 'scalar' else VE)(_e)

        def emit_ln(tag, src, dst, tt, pre_scalar=None):
            def _s1(e):
                if pre_scalar is not None:
                    pre_scalar(e)
                prev = M.get((tag, 'var', tt - 1))
                if prev is not None:
                    e.wait_ge(SV.h, prev)
                M[tag, 'sq', tt] = SS.inc(e.activation(
                    xsq[:], src[:], AF.Square, accum_out=sv[:, 1:2]))
            SC(_s1)

            def _v1(e):
                m_red = SV.inc(e.reduce_sum(sv[:, 0:1], src[:], axis=AX.X))
                e.wait_ge(SV.h, m_red)
                e.wait_ge(SS.h, M[tag, 'sq', tt])
                e.tensor_tensor(sv[:, 2:3], sv[:, 0:1], sv[:, 0:1], ALU.mult)
                m_mv = SV.inc(e.tensor_scalar_mul(sv[:, 3:4], sv[:, 0:1], 1.0 / D))
                e.wait_ge(SV.h, m_mv)
                M[tag, 'var', tt] = SV.inc(e.scalar_tensor_tensor(
                    sv[:, 4:5], sv[:, 2:3], -1.0 / D, sv[:, 1:2], ALU.mult, ALU.add))
            VE(_v1)

            def _s2(e):
                e.wait_ge(SV.h, M[tag, 'var', tt])
                M[tag, 'sqrt', tt] = SS.inc(e.activation(
                    sv[:, 5:6], sv[:, 4:5], AF.Sqrt, scale=1.0 / D, bias=eps_t[:]))
            SC(_s2)

            def _v2(e):
                e.wait_ge(SS.h, M[tag, 'sqrt', tt])
                m_r = SV.inc(e.reciprocal(sv[:, 6:7], sv[:, 5:6]))
                e.wait_ge(SV.h, m_r)
                M[tag, 'norm', tt] = SV.inc(e.tensor_scalar(
                    dst[:], src[:], sv[:, 3:4], sv[:, 6:7], ALU.subtract, ALU.mult))
            VE(_v2)

        def _p0sy(e):
            SX.inc(e.dma_start(out=eye_b[:], in_=d_eyeb), 16)
            SX.inc(e.dma_start(out=eye_f[:], in_=d_eyef), 16)
            SX.inc(e.dma_start(out=eye16[:], in_=d_eye16), 16)
        SY(_p0sy)
        M['eyes'] = SX.n

        def _p0g(e):
            SG.inc(e.dma_start(
                out=wr17b[:].rearrange("p (jc q) -> p jc q", q=17),
                in_=d_wr17.rearrange("(jc p) q -> p jc q", p=128)), 16)
        GP(_p0g)
        M['wr'] = SG.n

        def _p0v(e):
            e.memset(eps_t[:], 1e-5)
            SV.inc(e.memset(ones16[:], 1.0))
        VE(_p0v)
        M['consts_v'] = SV.n

        for tt in range(NT):
            par = tt % 2

            def _sy(e, tt=tt, par=par):
                if tt >= 2:
                    e.wait_ge(SV.h, M['ln1', 'norm', tt - 2])
                    e.wait_ge(SS.h, M['ln1', 'sq', tt - 2])
                M['p1_x', tt] = SX.inc(e.dma_start(
                    out=xstage[par][:], in_=d_x[tt * 128:(tt + 1) * 128, :]), 16)
            SY(_sy)

            def _vw(e, tt=tt):
                e.wait_ge(SX.h, M['p1_x', tt])
                prev = M.get(('p1t', 'trdone', tt - 2))
                if prev is not None:
                    e.wait_ge(ST.h, prev)
            VE(_vw)

            emit_ln('ln1', xstage[par], lnstage[par], tt,
                    pre_scalar=(lambda e, tt=tt: e.wait_ge(SX.h, M['p1_x', tt])))

            def _pre_tr(e, tt=tt):
                if tt == 0:
                    e.wait_ge(SX.h, M['eyes'])
                e.wait_ge(SV.h, M['ln1', 'norm', tt])
            tr_ev(NJ,
                  (lambda j, par=par: lnstage[par][:, j * 128:(j + 1) * 128]),
                  (lambda j, tt=tt: big0[:, j * T + tt * 128: j * T + (tt + 1) * 128]),
                  pt_b, eye_b, 'p1tr', 'scalar', SS, tt * NJ, pre_tr=_pre_tr)
            M['p1t', 'trdone', tt] = ST.n
        M['p1_done'] = SS.n

        for ic in range(NJ):
            par = ic % 2

            def _gp(e, ic=ic, par=par):
                prev = M.get(('p2t', 'trdone', ic - 2))
                if prev is not None:
                    e.wait_ge(ST.h, prev)
                M['p2_w', ic] = SG.inc(e.dma_start(
                    out=wstage[par][:], in_=d_wq[ic * 128:(ic + 1) * 128, :]), 16)
            GP(_gp)

            def _pre_tr(e, ic=ic):
                e.wait_ge(SG.h, M['p2_w', ic])
                if ic == 0:
                    e.wait_ge(SS.h, M['p1_done'])
            tr_ev(NJ,
                  (lambda j, par=par: wstage[par][:, j * 128:(j + 1) * 128]),
                  (lambda j, par=par: otst[par][:, j * 128:(j + 1) * 128]),
                  pt_b, eye_b, 'p2tr', 'scalar', SS, ic * NJ, pre_tr=_pre_tr)
            M['p2t', 'trdone', ic] = ST.n

            qev_base = SS.n

            def _mm(e, ic=ic, par=par, qev_base=qev_base):
                e.wait_ge(SS.h, M['p2tr', 'ev', ic * NJ + NJ - 1])
                for qg in range(4):
                    k = ic * 4 + qg
                    if qg < 2:
                        prev = M.get(('p2_qev', k - 2))
                        if prev is not None:
                            e.wait_ge(SS.h, prev)
                    else:
                        e.wait_ge(SS.h, qev_base + (qg - 2) + 1)
                    ins = None
                    for jc in range(NJ):
                        ins = e.matmul(
                            p_q[k % 2], otst[par][:, jc * 128:(jc + 1) * 128],
                            big0[:, jc * T + qg * 512: jc * T + (qg + 1) * 512],
                            start=(jc == 0), stop=(jc == NJ - 1))
                    M['p2_mm', k] = ST.inc(ins)
            TE(_mm)

            def _ev(e, ic=ic, qev_base=qev_base):
                for qg in range(4):
                    k = ic * 4 + qg
                    e.wait_ge(ST.h, M['p2_mm', k])
                    M['p2_qev', k] = SS.inc(e.activation(
                        big1[:, ic * T + qg * 512: ic * T + (qg + 1) * 512],
                        p_q[k % 2], AF.Copy, scale=SCALE))
                    assert M['p2_qev', k] == qev_base + qg + 1
            SC(_ev)
        M['p2_done'] = SS.n

        def _p3v0(e):
            e.wait_ge(SS.h, M['p2_done'])
            e.memset(vpb[0][:], 1.0)
            M['consts_v2'] = SV.inc(e.memset(vpb[1][:], 1.0))
        VE(_p3v0)

        prev_exp = [None, None]
        prev_omm = [None, None]
        prev_oev = None
        for h in range(HEADS):
            hp = h % 2

            def _gp(e, h=h, hp=hp):
                if h == 0:
                    e.wait_ge(SS.h, M['p2_done'])
                    e.wait_ge(SV.h, M['consts_v2'])
                if h >= 1:
                    e.wait_ge(ST.h, M['p3_ktr', h - 1])
                src = d_kvk[:, :, h * 128:(h + 1) * 128].rearrange(
                    "kb kk dd -> kk kb dd")
                dstk = kstg[:].rearrange("p (kb dd) -> p kb dd", dd=128)
                SG.inc(e.dma_start(out=dstk[:, 0:16], in_=src[:, 0:16]), 16)
                M['p3_kst', h] = SG.inc(e.dma_start(out=dstk[:, 16:32], in_=src[:, 16:32]), 16)
                if h >= 2:
                    e.wait_ge(ST.h, M['p3_lastO', h - 2])
                srcv = d_kvv[:, :, h * 128:(h + 1) * 128].rearrange(
                    "kb kk dd -> kk kb dd")
                dstv = vpb[hp][:].rearrange("p (kb c) -> p kb c", c=132)[:, :, 0:128]
                SG.inc(e.dma_start(out=dstv[:, 0:16], in_=srcv[:, 0:16]), 16)
                M['p3_v', h] = SG.inc(e.dma_start(out=dstv[:, 16:32], in_=srcv[:, 16:32]), 16)
            GP(_gp)

            def _pre_ktr(e, h=h):
                e.wait_ge(SG.h, M['p3_kst', h])
                if h >= 2:
                    e.wait_ge(ST.h, M['p3_lastS', h - 2])
            tr_ev(NKB,
                  (lambda j: kstg[:, j * 128:(j + 1) * 128]),
                  (lambda j, hp=hp: ktb[hp][:, j * 128:(j + 1) * 128]),
                  pt_b, eye_b, 'p3k', 'vector', SV, h * NKB, pre_tr=_pre_ktr)
            M['p3_ktr', h] = ST.n
            M['p3_kdone', h] = SV.n

            for qg in range(4):
                qbase = h * T + qg * 512
                exp_base = SS.n

                def _t3(e, h=h, hp=hp, qg=qg, qbase=qbase, exp_base=exp_base,
                        prev_exp=prev_exp, prev_oev=prev_oev):
                    if (h, qg) == (0, 0):
                        e.wait_ge(SS.h, M['p2_done'])
                        e.wait_ge(SV.h, M['consts_v2'])
                    e.wait_ge(SV.h, M['p3_kdone', h])
                    e.wait_ge(SG.h, M['p3_v', h])

                    def do_omm(pp, start_pair):
                        e.wait_ge(SS.h, exp_base + pp + 1)
                        if pp == 0 and prev_oev is not None:
                            e.wait_ge(SV.h, prev_oev)
                        ins2 = None
                        for b in range(2):
                            kb = 2 * pp + b
                            for qs in range(4):
                                ins2 = e.matmul(
                                    p_o[:, qs * 256: qs * 256 + 129],
                                    exps[pp % 2][:, b * 512 + qs * 128: b * 512 + (qs + 1) * 128],
                                    vpb[hp][:, kb * 132: kb * 132 + 129],
                                    start=(pp == 0 and b == 0 and qs % 2 == 0),
                                    stop=(pp == 15 and b == 1))
                        M['p3_omm', h, qg, pp] = ST.inc(ins2)

                    for p in range(16):
                        if p >= 2:
                            e.wait_ge(SS.h, exp_base + (p - 2) + 1)
                        elif prev_exp[p] is not None:
                            e.wait_ge(SS.h, prev_exp[p])
                        ins = None
                        for b in range(2):
                            kb = 2 * p + b
                            ins = e.matmul(
                                p_s2[p % 2][:, b * 512:(b + 1) * 512],
                                ktb[hp][:, kb * 128:(kb + 1) * 128],
                                big1[:, qbase:qbase + 512],
                                start=True, stop=True)
                        M['p3_s', h, qg, p] = ST.inc(ins)
                        if p >= 1:
                            do_omm(p - 1, False)
                    do_omm(15, False)
                TE(_t3)

                def _s3(e, h=h, qg=qg, exp_base=exp_base, prev_omm=prev_omm):
                    for p in range(16):
                        e.wait_ge(ST.h, M['p3_s', h, qg, p])
                        if p >= 2:
                            e.wait_ge(ST.h, M['p3_omm', h, qg, p - 2])
                        elif prev_omm[p] is not None:
                            e.wait_ge(ST.h, prev_omm[p])
                        M['p3_exp', h, qg, p] = SS.inc(e.activation(
                            exps[p % 2][:], p_s2[p % 2][:], AF.Exp))
                        assert M['p3_exp', h, qg, p] == exp_base + p + 1
                SC(_s3)

                def _oev(e, h=h, qg=qg):
                    e.wait_ge(ST.h, M['p3_omm', h, qg, 15])
                    ins = None
                    for qs in range(4):
                        ins = e.reciprocal(dinv[:, qs:qs + 1],
                                           p_o[:, qs * 256 + 128: qs * 256 + 129])
                    m_r = SV.inc(ins)
                    e.wait_ge(SV.h, m_r)
                    ins = None
                    for qs in range(4):
                        qt = qg * 4 + qs
                        ins = e.tensor_scalar_mul(
                            big2[:, qt * D + h * 128: qt * D + (h + 1) * 128],
                            p_o[:, qs * 256: qs * 256 + 128], dinv[:, qs:qs + 1])
                    M['p3_oev', h, qg] = SV.inc(ins)
                VE(_oev)

                prev_exp = [M['p3_exp', h, qg, 14], M['p3_exp', h, qg, 15]]
                prev_omm = [M['p3_omm', h, qg, 14], M['p3_omm', h, qg, 15]]
                prev_oev = M['p3_oev', h, qg]
                if qg == 3:
                    M['p3_lastS', h] = M['p3_s', h, 3, 15]
                    M['p3_lastO', h] = M['p3_omm', h, 3, 15]
        M['p3_done_v'] = SV.n
        M['p3_done_s'] = SS.n
        M['p3_done_t'] = ST.n

        regB.close()
        p_w = regC.enter_context(nc.psum_tensor("p_w", [128, 2048], F32))
        pt_f = [regC.enter_context(nc.psum_tensor(f"pt_f{p}", [128, 128], F32))
                for p in range(2)]

        for oc in range(NJ):
            par = oc % 2

            def _gp(e, oc=oc, par=par):
                prev = M.get(('p4t', 'trdone', oc - 2))
                if prev is not None:
                    e.wait_ge(ST.h, prev)
                elif oc < 2:
                    e.wait_ge(ST.h, M['p3_done_t'])
                M['p4_w', oc] = SG.inc(e.dma_start(
                    out=wstage[par][:], in_=d_wo[oc * 128:(oc + 1) * 128, :]), 16)
            GP(_gp)

            def _pre_tr(e, oc=oc):
                e.wait_ge(SG.h, M['p4_w', oc])
                if oc == 0:
                    e.wait_ge(SS.h, M['p3_done_s'])
                    e.wait_ge(SV.h, M['p3_done_v'])
            tr_ev(NJ,
                  (lambda j, par=par: wstage[par][:, j * 128:(j + 1) * 128]),
                  (lambda j, oc=oc: big1[:, j * T + oc * 128: j * T + (oc + 1) * 128]),
                  pt_b, eye_b, 'p4w', 'vector', SV, oc * NJ, pre_tr=_pre_tr)
            M['p4t', 'trdone', oc] = ST.n
        M['p4_wot'] = SV.n

        for qt in range(NT):
            par = qt % 2

            def _pre_otr(e, qt=qt):
                if qt == 0:
                    e.wait_ge(SV.h, M['p4_wot'])
            tr_ev(NJ,
                  (lambda j, qt=qt: big2[:, qt * D + j * 128: qt * D + (j + 1) * 128]),
                  (lambda j, par=par: otst[par][:, j * 128:(j + 1) * 128]),
                  pt_b, eye_b, 'p4o', 'scalar', SS, qt * NJ, pre_tr=_pre_otr)

            def _sy1(e, qt=qt, par=par):
                prev = M.get(('p4_x1', qt - 2))
                if prev is not None:
                    e.wait_ge(SV.h, prev)
                elif qt < 2:
                    e.wait_ge(ST.h, M['p3_done_t'])
                M['p4_x', qt] = SX.inc(e.dma_start(
                    out=xstage[par][:], in_=d_x[qt * 128:(qt + 1) * 128, :]), 16)
            SY(_sy1)

            def _mm(e, qt=qt, par=par):
                e.wait_ge(SS.h, M['p4o', 'ev', qt * NJ + NJ - 1])
                prev = M.get(('p4_x1', qt - 1))
                if prev is not None:
                    e.wait_ge(SV.h, prev)
                ins = None
                for ob in range(4):
                    for ic in range(NJ):
                        ins = e.matmul(
                            p_w[:, ob * 512:(ob + 1) * 512],
                            otst[par][:, ic * 128:(ic + 1) * 128],
                            big1[:, ic * T + ob * 512: ic * T + (ob + 1) * 512],
                            start=(ic == 0), stop=(ic == NJ - 1))
                M['p4_mm', qt] = ST.inc(ins)
            TE(_mm)

            def _vx1(e, qt=qt, par=par):
                e.wait_ge(ST.h, M['p4_mm', qt])
                e.wait_ge(SX.h, M['p4_x', qt])
                if qt >= 2:
                    e.wait_ge(SX.h, M['p4_x1st', qt - 2])
                M['p4_x1', qt] = SV.inc(e.tensor_tensor(
                    x1buf[par][:], p_w[:], xstage[par][:], ALU.add))
            VE(_vx1)

            def _sy2(e, qt=qt, par=par):
                e.wait_ge(SV.h, M['p4_x1', qt])
                M['p4_x1st', qt] = SX.inc(e.dma_start(
                    out=d_x1[qt * 128:(qt + 1) * 128, :], in_=x1buf[par][:]), 16)
            SY(_sy2)
        M['p4a_done_st'] = ST.n

        for qt in range(NT):
            par = qt % 2

            def _syb(e, qt=qt, par=par):
                e.wait_ge(SX.h, M['p4_x1st', qt])
                if qt >= 2:
                    e.wait_ge(SS.h, M['ln2', 'sq', qt - 2])
                    e.wait_ge(ST.h, M['p4x1t', 'trdone', qt - 2])
                M['p4b_x', qt] = SX.inc(e.dma_start(
                    out=x1buf[par][:], in_=d_x1[qt * 128:(qt + 1) * 128, :]), 16)
            SY(_syb)

            def _vw(e, qt=qt):
                e.wait_ge(SX.h, M['p4b_x', qt])
                prev = M.get(('p4l', 'trdone', qt - 2))
                if prev is not None:
                    e.wait_ge(ST.h, prev)
            VE(_vw)

            emit_ln('ln2', x1buf[par], lnstage[par], qt,
                    pre_scalar=(lambda e, qt=qt: e.wait_ge(SX.h, M['p4b_x', qt])))

            def _pre_lntr(e, qt=qt):
                e.wait_ge(SV.h, M['ln2', 'norm', qt])

            def _pre_lnev(e, qt=qt):
                prev = M.get(('p4_lnst', qt - 2))
                if prev is not None:
                    e.wait_ge(SX.h, prev)
            tr_ev(NJ,
                  (lambda j, par=par: lnstage[par][:, j * 128:(j + 1) * 128]),
                  (lambda j, par=par: ln2Tbuf[par][:, j * 128:(j + 1) * 128]),
                  pt_b, eye_b, 'p4l', 'vector', SV, qt * NJ,
                  pre_tr=_pre_lntr, pre_ev=_pre_lnev)
            M['p4l', 'trdone', qt] = ST.n
            M['p4_lnevd', qt] = SV.n

            def _pre_x1ev(e, qt=qt):
                prev = M.get(('p4_x1Tst', qt - 2))
                if prev is not None:
                    e.wait_ge(SX.h, prev)
            tr_ev(NJ,
                  (lambda j, par=par: x1buf[par][:, j * 128:(j + 1) * 128]),
                  (lambda j, par=par: x1Tbuf[par][:, j * 128:(j + 1) * 128]),
                  pt_f, eye_f, 'p4x1t', 'scalar', SS, qt * NJ, pre_ev=_pre_x1ev)
            M['p4x1t', 'trdone', qt] = ST.n
            M['p4_x1evd', qt] = SS.n

            def _sy3(e, qt=qt, par=par):
                e.wait_ge(SV.h, M['p4_lnevd', qt])
                M['p4_lnst', qt] = SX.inc(e.dma_start(
                    out=d_ln2T[:, qt * 128:(qt + 1) * 128].rearrange(
                        "(jc p) t -> p jc t", p=128),
                    in_=ln2Tbuf[par][:].rearrange("p (jc t) -> p jc t", t=128)), 16)
                e.wait_ge(SS.h, M['p4_x1evd', qt])
                M['p4_x1Tst', qt] = SX.inc(e.dma_start(
                    out=d_x1T[:, qt * 128:(qt + 1) * 128].rearrange(
                        "(jc p) t -> p jc t", p=128),
                    in_=x1Tbuf[par][:].rearrange("p (jc t) -> p jc t", t=128)), 16)
            SY(_sy3)
        M['p4_done_ss'] = SS.n
        M['p4_done_sv'] = SV.n
        M['p4_done_st'] = ST.n

        regC.close()
        ptb_stack.close()
        p_r = regD.enter_context(nc.psum_tensor("p_r", [128, 17], F32))
        pt_f5 = [regD.enter_context(nc.psum_tensor(f"pt_f5{p}", [128, 128], F32))
                 for p in range(2)]

        for qt in range(NT):
            def _sy(e, qt=qt):
                if qt == 0:
                    e.wait_ge(SS.h, M['p4_done_ss'])
                    e.wait_ge(SV.h, M['p4_done_sv'])
                    e.wait_ge(ST.h, M['p4_done_st'])
                else:
                    e.wait_ge(ST.h, M['p5_mm', qt - 1])
                M['p5_ln', qt] = SX.inc(e.dma_start(
                    out=routst[:].rearrange("p (jc t) -> p jc t", t=128),
                    in_=d_ln2T[:, qt * 128:(qt + 1) * 128].rearrange(
                        "(jc p) t -> p jc t", p=128)), 16)
            SY(_sy)

            def _mm(e, qt=qt):
                e.wait_ge(SX.h, M['p5_ln', qt])
                if qt == 0:
                    e.wait_ge(SG.h, M['wr'])
                else:
                    e.wait_ge(SS.h, M['p5_pr', qt - 1])
                ins = None
                for jc in range(NJ):
                    ins = e.matmul(p_r[:], routst[:, jc * 128:(jc + 1) * 128],
                                   wr17b[:, jc * 17:(jc + 1) * 17],
                                   start=(jc == 0), stop=(jc == NJ - 1))
                M['p5_mm', qt] = ST.inc(ins)
            TE(_mm)

            def _sc(e, qt=qt):
                e.wait_ge(ST.h, M['p5_mm', qt])
                prev = M.get(('p5_comb', qt - 1))
                if prev is not None:
                    e.wait_ge(SV.h, prev)
                e.activation(pe_r[:], p_r[:, 0:16], AF.Exp)
                M['p5_pr', qt] = SS.inc(e.activation(
                    sgcol[:, qt:qt + 1], p_r[:, 16:17], AF.Sigmoid))
            SC(_sc)

            def _v(e, qt=qt):
                e.wait_ge(SS.h, M['p5_pr', qt])
                m1 = SV.inc(e.reduce_sum(sv[:, 8:9], pe_r[:], axis=AX.X))
                e.wait_ge(SV.h, m1)
                m2 = SV.inc(e.reciprocal(sv[:, 9:10], sv[:, 8:9]))
                mx = SV.inc(e.max(pwork[:, 0:8], pe_r[:]))
                e.wait_ge(SV.h, mx)
                e.tensor_copy(pwork[:, 8:12], pwork[:, 0:4])
                mc = SV.inc(e.tensor_copy(pwork[:, 12:16], pwork[:, 0:4]))
                e.wait_ge(SV.h, mc)
                mr = SV.inc(e.match_replace(pmask[:], pwork[:, 8:16], pe_r[:], 0.0))
                e.wait_ge(SV.h, mr)
                e.tensor_tensor(cacc[:], pe_r[:], pmask[:], ALU.subtract)
                e.wait_ge(SV.h, m2)
                M['p5_comb', qt] = SV.inc(e.tensor_scalar_mul(
                    comb[:, qt * 16:(qt + 1) * 16], cacc[:], sv[:, 9:10]))
            VE(_v)

            def _ctr(e, qt=qt):
                e.wait_ge(SV.h, M['p5_comb', qt])
                prev = M.get(('p5_cev', qt - 2))
                if prev is not None:
                    e.wait_ge(SV.h, prev)
                M['p5_ctr', qt] = ST.inc(e.transpose(
                    pt_f5[qt % 2][0:16, :], comb[:, qt * 16:(qt + 1) * 16], eye_f[:]))
            TE(_ctr)

            def _cev(e, qt=qt):
                e.wait_ge(ST.h, M['p5_ctr', qt])
                M['p5_cev', qt] = SV.inc(e.tensor_copy(
                    combT[:, qt * 128:(qt + 1) * 128], pt_f5[qt % 2][0:16, :]))
            VE(_cev)

        def _str(e):
            e.wait_ge(SS.h, M['p5_pr', NT - 1])
            e.wait_ge(SV.h, M['p5_cev', NT - 2])
            M['p5_str'] = ST.inc(e.transpose(pt_f5[0][0:16, :], sgcol[:, 0:16], eye_f[:]))
        TE(_str)

        def _sgev(e):
            e.wait_ge(ST.h, M['p5_str'])
            M['p5_sgT'] = SV.inc(e.tensor_copy(sgT[:], pt_f5[0][0:16, :]))
        VE(_sgev)
        M['p5_done_v'] = SV.n
        M['p5_done_s'] = SS.n
        M['p5_done_t'] = ST.n

        regD.close()
        p_gu = [regE.enter_context(nc.psum_tensor(f"p_gu{p}", [128, 512], F32))
                for p in range(4)]
        p_y = [regE.enter_context(nc.psum_tensor(f"p_y{p}", [128, 512], F32))
               for p in range(2)]
        pt_f6 = [regE.enter_context(nc.psum_tensor(f"pt_f6{p}", [128, 128], F32))
                 for p in range(2)]

        prev_guh = None
        prev_yh = None
        prev_yev = None
        prev_sgevac = None
        prev_uevac = None
        prev_selread = None
        for Hh in range(2):
            tof = Hh * HALF

            def _sy(e, Hh=Hh, tof=tof):
                if Hh == 0:
                    e.wait_ge(SV.h, M['p5_done_v'])
                    e.wait_ge(SS.h, M['p5_done_s'])
                    e.wait_ge(ST.h, M['p5_done_t'])
                else:
                    e.wait_ge(ST.h, M['p6o', 'tr', 0, NTH * NJ - 1])
                M['p6_lnh', Hh] = SX.inc(e.dma_start(
                    out=ln2h[:].rearrange("p (jc t) -> p jc t", t=HALF),
                    in_=d_ln2T[:, tof:tof + HALF].rearrange(
                        "(jc p) t -> p jc t", p=128)), 16)
                M['p6_acc', Hh] = SX.inc(e.dma_start(
                    out=accT[:].rearrange("p (hc t) -> p hc t", t=HALF),
                    in_=d_x1T[:, tof:tof + HALF].rearrange(
                        "(hc p) t -> p hc t", p=128)), 16)
            SY(_sy)

            for q8 in range(NTH):
                qt = Hh * NTH + q8

                def _vsel(e, q8=q8, qt=qt, Hh=Hh, prev_selread=prev_selread):
                    if (Hh, q8) == (0, 0):
                        e.wait_ge(SV.h, M['p5_sgT'])
                    if prev_selread is not None:
                        e.wait_ge(ST.h, prev_selread)
                    M['p6_sel', Hh, q8] = SV.inc(e.tensor_scalar_mul(
                        sel_b[:], ones16[:], eye16[:, qt:qt + 1]))
                VE(_vsel)

                def _tsg(e, q8=q8, Hh=Hh, prev_sgevac=prev_sgevac,
                         prev_uevac=prev_uevac):
                    e.wait_ge(SV.h, M['p6_sel', Hh, q8])
                    prev = M.get(('p6_sgev', Hh, q8 - 1))
                    if prev is not None:
                        e.wait_ge(SV.h, prev)
                    elif Hh == 0:
                        e.wait_ge(SS.h, M['p5_done_s'])
                    else:
                        e.wait_ge(SS.h, prev_sgevac)
                        e.wait_ge(SV.h, prev_uevac)
                    M['p6_sgmm', Hh, q8] = ST.inc(e.matmul(
                        p_gu[0][:, 0:128], sel_b[:], sgT[:], start=True, stop=True))
                TE(_tsg)
                prev_selread = M['p6_sgmm', Hh, q8]

                def _vsg(e, q8=q8, Hh=Hh):
                    e.wait_ge(ST.h, M['p6_sgmm', Hh, q8])
                    M['p6_sgev', Hh, q8] = SV.inc(e.tensor_copy(
                        sg_b[:, q8 * 128:(q8 + 1) * 128], p_gu[0][:, 0:128]))
                VE(_vsg)

            for u in range(E + NBAND):
                is_sh = u >= E
                bb = u - E

                if not is_sh:
                    def _vsel2(e, u=u, Hh=Hh, prev_selread=prev_selread):
                        e.wait_ge(ST.h, prev_selread)
                        M['p6_sel2', Hh, u] = SV.inc(e.tensor_scalar_mul(
                            sel_b[:], ones16[:], eye16[:, u:u + 1]))
                    VE(_vsel2)

                    def _tcb(e, u=u, Hh=Hh, tof=tof,
                             prev_sgevac=prev_sgevac, prev_uevac=prev_uevac):
                        e.wait_ge(SV.h, M['p6_sel2', Hh, u])
                        if prev_sgevac is not None:
                            e.wait_ge(SS.h, prev_sgevac)
                            e.wait_ge(SV.h, prev_uevac)
                        else:
                            e.wait_ge(SV.h, M['p6_sgev', 0, NTH - 1])
                        e.matmul(p_gu[0][:], sel_b[:],
                                 combT[:, tof:tof + 512], start=True, stop=True)
                        ins = e.matmul(p_gu[1][:], sel_b[:],
                                       combT[:, tof + 512:tof + 1024],
                                       start=True, stop=True)
                        M['p6_cbmm', Hh, u] = ST.inc(ins)
                    TE(_tcb)
                    prev_selread = M['p6_cbmm', Hh, u]

                    def _vcb(e, u=u, Hh=Hh):
                        e.wait_ge(ST.h, M['p6_cbmm', Hh, u])
                        e.tensor_copy(cb_b[:, 0:512], p_gu[0][:])
                        M['p6_cbev', Hh, u] = SV.inc(
                            e.tensor_copy(cb_b[:, 512:1024], p_gu[1][:]))
                    VE(_vcb)

                st_gu = ST.n
                ss_gu = SS.n
                sv_gu = SV.n

                def _gpw(e, u=u, Hh=Hh, is_sh=is_sh, bb=bb,
                         prev_guh=prev_guh, st_gu=st_gu):
                    if (Hh, u) == (0, 0):
                        e.wait_ge(ST.h, M['p5_done_t'])
                    for icc in range(NI):
                        gp = icc % 2
                        if icc >= 2:
                            e.wait_ge(ST.h, st_gu + 2 * (icc - 1))
                        elif prev_guh is not None:
                            e.wait_ge(ST.h, prev_guh)
                        if is_sh:
                            colg = d_wsg[:, (bb * NI + icc) * 128:(bb * NI + icc + 1) * 128]
                            colu = d_wsu[:, (bb * NI + icc) * 128:(bb * NI + icc + 1) * 128]
                        else:
                            colg = d_wg[u][:, icc * 128:(icc + 1) * 128]
                            colu = d_wu[u][:, icc * 128:(icc + 1) * 128]
                        SG.inc(e.dma_start(
                            out=wgc[gp][:].rearrange("p (jc i) -> p jc i", i=128),
                            in_=colg.rearrange("(jc p) i -> p jc i", p=128)), 16)
                        M['p6_w', Hh, u, icc] = SG.inc(e.dma_start(
                            out=wuc[gp][:].rearrange("p (jc i) -> p jc i", i=128),
                            in_=colu.rearrange("(jc p) i -> p jc i", p=128)), 16)
                GP(_gpw)

                def _tgu(e, u=u, Hh=Hh, is_sh=is_sh,
                         prev_sgevac=prev_sgevac, prev_uevac=prev_uevac,
                         st_gu=st_gu, ss_gu=ss_gu, sv_gu=sv_gu):
                    for icc in range(NI):
                        gp = icc % 2
                        e.wait_ge(SG.h, M['p6_w', Hh, u, icc])
                        if icc == 0:
                            e.wait_ge(SX.h, M['p6_lnh', Hh])
                            if not is_sh:
                                e.wait_ge(SV.h, M['p6_cbev', Hh, u])
                            elif prev_sgevac is not None:
                                e.wait_ge(SS.h, prev_sgevac)
                                e.wait_ge(SV.h, prev_uevac)
                        else:
                            e.wait_ge(SS.h, ss_gu + icc)
                            e.wait_ge(SV.h, sv_gu + 2 * icc - 1)
                        ins = None
                        for tg in range(2):
                            for jc in range(NJ):
                                ins = e.matmul(
                                    p_gu[tg][:], wgc[gp][:, jc * 128:(jc + 1) * 128],
                                    ln2h[:, jc * HALF + tg * 512: jc * HALF + (tg + 1) * 512],
                                    start=(jc == 0), stop=(jc == NJ - 1))
                        M['p6_gmm', Hh, u, icc] = ST.inc(ins)
                        ins = None
                        for tg in range(2):
                            for jc in range(NJ):
                                ins = e.matmul(
                                    p_gu[2 + tg][:], wuc[gp][:, jc * 128:(jc + 1) * 128],
                                    ln2h[:, jc * HALF + tg * 512: jc * HALF + (tg + 1) * 512],
                                    start=(jc == 0), stop=(jc == NJ - 1))
                        M['p6_umm', Hh, u, icc] = ST.inc(ins)
                        M['p6_guh', Hh, u, icc] = ST.n
                        assert M['p6_guh', Hh, u, icc] == st_gu + 2 * (icc + 1)
                TE(_tgu)

                def _ssi(e, u=u, Hh=Hh, ss_gu=ss_gu, sv_gu=sv_gu):
                    for icc in range(NI):
                        e.wait_ge(ST.h, M['p6_gmm', Hh, u, icc])
                        if icc >= 1:
                            e.wait_ge(SV.h, sv_gu + 2 * icc)
                        e.activation(sgb_[:, 0:512], p_gu[0][:], AF.Silu)
                        M['p6_sgevac', Hh, u, icc] = SS.inc(e.activation(
                            sgb_[:, 512:1024], p_gu[1][:], AF.Silu))
                        assert M['p6_sgevac', Hh, u, icc] == ss_gu + icc + 1
                SC(_ssi)

                def _vu(e, u=u, Hh=Hh, is_sh=is_sh, prev_yh=prev_yh, sv_gu=sv_gu):
                    gate_src = sg_b if is_sh else cb_b
                    for icc in range(NI):
                        e.wait_ge(ST.h, M['p6_umm', Hh, u, icc])
                        e.wait_ge(ST.h, M['p6_gmm', Hh, u, icc])
                        ins = e.tensor_tensor(ubuf[:, 0:512], p_gu[2][:],
                                              gate_src[:, 0:512], ALU.mult)
                        M['p6_uevac', Hh, u, icc] = SV.inc(e.tensor_tensor(
                            ubuf[:, 512:1024], p_gu[3][:],
                            gate_src[:, 512:1024], ALU.mult))
                        e.wait_ge(SS.h, M['p6_sgevac', Hh, u, icc])
                        if icc == 0 and prev_yh is not None:
                            e.wait_ge(ST.h, prev_yh)
                        M['p6_aev', Hh, u, icc] = SV.inc(e.tensor_tensor(
                            a_sb[:, icc * HALF:(icc + 1) * HALF], sgb_[:], ubuf[:],
                            ALU.mult))
                        assert M['p6_aev', Hh, u, icc] == sv_gu + 2 * icc + 2
                VE(_vu)
                prev_guh = M['p6_guh', Hh, u, NI - 1]
                prev_sgevac = M['p6_sgevac', Hh, u, NI - 1]
                prev_uevac = M['p6_uevac', Hh, u, NI - 1]
                M['p6_adone', Hh, u] = SV.n

                st_dn = ST.n
                sv_dn = SV.n

                def _gpd(e, u=u, Hh=Hh, is_sh=is_sh, bb=bb,
                         prev_yh=prev_yh, st_dn=st_dn):
                    for hc in range(NJ):
                        wp = hc % 2
                        if hc >= 2:
                            e.wait_ge(ST.h, st_dn + 2 * (hc - 1))
                        elif prev_yh is not None:
                            e.wait_ge(ST.h, prev_yh)
                        if is_sh:
                            blk = d_wsd[bb * I_:(bb + 1) * I_, hc * 128:(hc + 1) * 128]
                        else:
                            blk = d_wd[u][:, hc * 128:(hc + 1) * 128]
                        M['p6_wd', Hh, u, hc] = SG.inc(e.dma_start(
                            out=wdc[wp][:].rearrange("p (ic hh) -> p ic hh", hh=128),
                            in_=blk.rearrange("(ic p) hh -> p ic hh", p=128)), 16)
                GP(_gpd)

                def _tdn(e, u=u, Hh=Hh, prev_yev=prev_yev, st_dn=st_dn, sv_dn=sv_dn):
                    for hc in range(NJ):
                        wp = hc % 2
                        e.wait_ge(SG.h, M['p6_wd', Hh, u, hc])
                        if hc == 0:
                            e.wait_ge(SV.h, M['p6_adone', Hh, u])
                            e.wait_ge(SX.h, M['p6_acc', Hh])
                            if (Hh, u) == (0, 0):
                                e.wait_ge(SS.h, M['p5_done_s'])
                        ins = None
                        for tg in range(2):
                            y = hc * 2 + tg
                            if y >= 2:
                                e.wait_ge(SV.h, sv_dn + y - 1)
                            elif prev_yev is not None:
                                e.wait_ge(SV.h, prev_yev)
                            for icc in range(NI):
                                ins = e.matmul(
                                    p_y[y % 2][:], wdc[wp][:, icc * 128:(icc + 1) * 128],
                                    a_sb[:, icc * HALF + tg * 512: icc * HALF + (tg + 1) * 512],
                                    start=(icc == 0), stop=(icc == NI - 1))
                            M['p6_ymm', Hh, u, y] = ST.inc(ins)
                        M['p6_yh', Hh, u, hc] = ST.n
                        assert M['p6_yh', Hh, u, hc] == st_dn + 2 * (hc + 1)
                TE(_tdn)

                def _vy(e, u=u, Hh=Hh, sv_dn=sv_dn):
                    for hc in range(NJ):
                        for tg in range(2):
                            y = hc * 2 + tg
                            e.wait_ge(ST.h, M['p6_ymm', Hh, u, y])
                            dst = accT[:, hc * HALF + tg * 512: hc * HALF + (tg + 1) * 512]
                            M['p6_yev', Hh, u, y] = SV.inc(e.tensor_tensor(
                                dst, dst, p_y[y % 2][:], ALU.add))
                            assert M['p6_yev', Hh, u, y] == sv_dn + y + 1
                VE(_vy)
                prev_yh = M['p6_yh', Hh, u, NJ - 1]
                prev_yev = M['p6_yev', Hh, u, 2 * NJ - 1]

            for q8 in range(NTH):
                op = q8 % 2

                def _pre_otr(e, q8=q8, Hh=Hh, prev_yev=prev_yev):
                    if q8 == 0:
                        e.wait_ge(SV.h, prev_yev)
                    prev = M.get(('p6_obst', Hh, q8 - 2))
                    if prev is not None:
                        e.wait_ge(SO.h, prev)
                tr_ev(NJ,
                      (lambda j, q8=q8: accT[:, j * HALF + q8 * 128: j * HALF + (q8 + 1) * 128]),
                      (lambda j, op=op: outbuf[op][:, j * 128:(j + 1) * 128]),
                      pt_f6, eye_f, ('p6o', Hh), 'scalar', SS, q8 * NJ,
                      pre_tr=_pre_otr)
                M['p6_obuf', Hh, q8] = SS.n

                def _syo(e, q8=q8, op=op, Hh=Hh, tof=tof):
                    e.wait_ge(SS.h, M['p6_obuf', Hh, q8])
                    r0 = tof + q8 * 128
                    M['p6_obst', Hh, q8] = SO.inc(e.dma_start(
                        out=d_out[r0:r0 + 128, :], in_=outbuf[op][:]), 16)
                SY(_syo)
            M['p6o', 'tr', Hh, NTH * NJ - 1] = M[('p6o', Hh), 'tr', NTH * NJ - 1]

        def _fin(e):
            e.wait_ge(SO.h, SO.n)
        SY(_fin)

    return nc


_NC_CACHE = None


def _get_program():
    global _NC_CACHE
    if _NC_CACHE is None:
        _NC_CACHE = build_program()
    return _NC_CACHE


def kernel(L0_attn_query, L0_attn_kv_k, L0_attn_kv_v, L0_attn_Wq, L0_attn_Wo,
           L0_ffn_w_router, L0_ffn_w_gate, L0_ffn_w_up, L0_ffn_w_down,
           L0_ffn_ws_gate, L0_ffn_ws_up, L0_ffn_ws_down, L0_ffn_w_shared_gate):
    xq = np.ascontiguousarray(np.asarray(L0_attn_query, dtype=np.float32)).reshape(-1, D)
    wr17 = np.ascontiguousarray(
        np.concatenate([np.asarray(L0_ffn_w_router, dtype=np.float32),
                        np.asarray(L0_ffn_w_shared_gate, dtype=np.float32)], axis=1))
    common = {
        "kv_k": np.ascontiguousarray(np.asarray(L0_attn_kv_k, dtype=np.float32)),
        "kv_v": np.ascontiguousarray(np.asarray(L0_attn_kv_v, dtype=np.float32)),
        "Wq": np.ascontiguousarray(np.asarray(L0_attn_Wq, dtype=np.float32)),
        "Wo": np.ascontiguousarray(np.asarray(L0_attn_Wo, dtype=np.float32)),
        "wr17": wr17,
        "w_gate": np.ascontiguousarray(np.asarray(L0_ffn_w_gate, dtype=np.float32)),
        "w_up": np.ascontiguousarray(np.asarray(L0_ffn_w_up, dtype=np.float32)),
        "w_down": np.ascontiguousarray(np.asarray(L0_ffn_w_down, dtype=np.float32)),
        "ws_gate": np.ascontiguousarray(np.asarray(L0_ffn_ws_gate, dtype=np.float32)),
        "ws_up": np.ascontiguousarray(np.asarray(L0_ffn_ws_up, dtype=np.float32)),
        "ws_down": np.ascontiguousarray(np.asarray(L0_ffn_ws_down, dtype=np.float32)),
        "eye_b": np.eye(128, dtype=np.float32).astype(ml_dtypes.bfloat16),
        "eye_f": np.eye(128, dtype=np.float32),
        "eye16": np.eye(16, dtype=np.float32),
    }
    in_maps = []
    for c in range(NCORES):
        m = dict(common)
        m["xq"] = np.ascontiguousarray(xq[c * T:(c + 1) * T, :])
        in_maps.append(m)
    nc = _get_program()
    res = run_bass_kernel_spmd(nc, in_maps, core_ids=list(range(NCORES)))
    out = np.concatenate([np.asarray(r["out"]) for r in res.results], axis=0)
    return out.reshape(4, 4096, 2048).astype(np.float32)


# version 09
# speedup vs baseline: 1.3498x; 1.0360x over previous
import math
from contextlib import ExitStack

import numpy as np
import ml_dtypes
import concourse.bass as bass
import concourse.mybir as mybir
from concourse.bass_utils import run_bass_kernel_spmd

F32 = mybir.dt.float32
BF16 = mybir.dt.bfloat16
AF = mybir.ActivationFunctionType
ALU = mybir.AluOpType
AX = mybir.AxisListType

NCORES = 8
T = 2048
D = 2048
NT = T // 128
NJ = D // 128
HEADS = 16
KV = 4096
NKB = KV // 128
E = 16
I_ = 1408
NI = I_ // 128
IS = 5632
NBAND = 4
SCALE = 1.0 / math.sqrt(128.0)
HALF = T // 2
NTH = HALF // 128


class Sem:
    def __init__(self, nc, name):
        self.h = nc.alloc_semaphore(name)
        self.n = 0

    def inc(self, instr, k=1):
        instr.then_inc(self.h, k)
        self.n += k
        return self.n


def build_program():
    nc = bass.Bass("TRN2", target_bir_lowering=False)

    d_x = nc.dram_tensor("xq", [T, D], F32, kind="ExternalInput").ap()
    d_kvk = nc.dram_tensor("kv_kT", [D, KV], F32, kind="ExternalInput").ap()
    d_kvv = nc.dram_tensor("kv_v", [NKB, 128, D], F32, kind="ExternalInput").ap()
    d_wq = nc.dram_tensor("WqT", [D, D], F32, kind="ExternalInput").ap()
    d_wo = nc.dram_tensor("WoT", [D, D], F32, kind="ExternalInput").ap()
    d_wr17 = nc.dram_tensor("wr17", [D, 17], F32, kind="ExternalInput").ap()
    d_wg = nc.dram_tensor("w_gate", [E, D, I_], F32, kind="ExternalInput").ap()
    d_wu = nc.dram_tensor("w_up", [E, D, I_], F32, kind="ExternalInput").ap()
    d_wd = nc.dram_tensor("w_down", [E, I_, D], F32, kind="ExternalInput").ap()
    d_wsg = nc.dram_tensor("ws_gate", [D, IS], F32, kind="ExternalInput").ap()
    d_wsu = nc.dram_tensor("ws_up", [D, IS], F32, kind="ExternalInput").ap()
    d_wsd = nc.dram_tensor("ws_down", [IS, D], F32, kind="ExternalInput").ap()
    d_eyeb = nc.dram_tensor("eye_b", [128, 128], BF16, kind="ExternalInput").ap()
    d_eyef = nc.dram_tensor("eye_f", [128, 128], F32, kind="ExternalInput").ap()
    d_eye16 = nc.dram_tensor("eye16", [16, 16], F32, kind="ExternalInput").ap()
    d_out = nc.dram_tensor("out", [T, D], F32, kind="ExternalOutput").ap()
    d_x1 = nc.dram_tensor("x1_scr", [T, D], F32, kind="Internal").ap()
    d_x1T = nc.dram_tensor("x1T_scr", [D, T], F32, kind="Internal").ap()
    d_ln2T = nc.dram_tensor("ln2T_scr", [D, T], BF16, kind="Internal").ap()

    A = nc.alloc_sbuf_tensor
    eye_b = A("s_eyeb", [128, 128], BF16)
    eye_f = A("s_eyef", [128, 128], F32)
    eye16 = A("eye16s", [16, 16], F32)
    ones16 = A("ones16", [16, 128], F32)
    sel_b = A("sel_b", [16, 128], F32)
    eps_t = A("eps_t", [128, 1], F32)
    wr17b = A("wr17b", [128, NJ * 17], BF16)
    comb = A("comb", [128, NT * 16], F32)
    combT = A("combT", [16, T], F32)
    sgcol = A("sgcol", [128, NT], F32)
    sgT = A("sgT", [16, 128], F32)
    sv = A("sv", [128, 32], F32)
    dinv = A("dinv", [128, 4], F32)
    pmask = A("pmask", [128, 16], F32)
    pwork = A("pwork", [128, 16], F32)
    cacc = A("cacc", [128, 16], F32)
    pe_r = A("pe_r", [128, 16], F32)

    KB = 1024
    MB = (nc.sbuf_base + 31) & ~31
    AT = nc.alloc_sbuf_tensor_at

    def at(name, shape, dtype, off):
        return AT(name, shape, dtype, offset=MB + off)

    big1 = at("big1", [128, NJ * T], BF16, 0)
    big0 = at("big0", [128, NJ * T], BF16, 64 * KB)
    big2 = at("big2", [128, NT * D], BF16, 64 * KB)
    xstage = [at(f"xstage{p}", [128, D], F32, 128 * KB + p * 8 * KB) for p in range(2)]
    lnstage = [at(f"lnstage{p}", [128, D], BF16, 144 * KB + p * 4 * KB) for p in range(2)]
    xsq = at("xsq", [128, D], F32, 152 * KB)
    wstage = [at(f"wstage{p}", [128, D], BF16, 160 * KB + p * 4 * KB) for p in range(2)]
    otst = [at(f"otst{p}", [128, NJ * 128], BF16, 168 * KB + p * 4 * KB) for p in range(2)]
    x1buf = [at(f"x1buf{p}", [128, D], F32, 176 * KB + p * 8 * KB) for p in range(2)]
    kstg = at("kstg", [128, NKB * 128], BF16, 128 * KB)
    ktb = [at(f"ktb{p}", [128, NKB * 128], BF16, 136 * KB + p * 8 * KB) for p in range(2)]
    vpb = [at(f"vpb{p}", [128, NKB * 132], BF16, 152 * KB + p * 8448) for p in range(2)]
    exps = [at(f"exps{p}", [128, 1024], BF16, 152 * KB + 16896 + p * 2048) for p in range(2)]
    x1Tbuf = [at(f"x1Tbuf{p}", [128, NJ * 128], F32, 128 * KB + p * 8 * KB) for p in range(2)]
    ln2Tbuf = [at(f"ln2Tbuf{p}", [128, NJ * 128], BF16, 160 * KB + p * 4 * KB) for p in range(2)]
    routst = at("routst", [128, NJ * 128], BF16, 168 * KB)
    ln2h = at("ln2h", [128, NJ * HALF], BF16, 0)
    accT = at("accT", [128, NJ * HALF], F32, 32 * KB)
    a_sb = at("a_sb", [128, NI * HALF], BF16, 96 * KB)
    wgc = [at(f"wgc{p}", [128, NJ * 128], BF16, 118 * KB + p * 4 * KB) for p in range(2)]
    wuc = [at(f"wuc{p}", [128, NJ * 128], BF16, 126 * KB + p * 4 * KB) for p in range(2)]
    wdc = [at(f"wdc{p}", [128, NI * 128], BF16, 134 * KB + p * 2816) for p in range(2)]
    sgb_ = at("sgb", [128, 1024], F32, 140 * KB)
    ubuf = at("ubuf", [128, 1024], F32, 144 * KB)
    cb_b = at("cb_b", [128, HALF], F32, 148 * KB)
    sg_b = at("sg_b", [128, HALF], F32, 152 * KB)
    outbuf = [at(f"outbuf{p}", [128, D], F32, 156 * KB + p * 8 * KB) for p in range(2)]

    SX = Sem(nc, "SX")
    SG = Sem(nc, "SG")
    SV = Sem(nc, "SV")
    SS = Sem(nc, "SS")
    ST = Sem(nc, "ST")
    SO = Sem(nc, "SO")
    M = {}

    ptb_stack = ExitStack()
    regB = ExitStack()
    regC = ExitStack()
    regD = ExitStack()
    regE = ExitStack()
    pt_b = [ptb_stack.enter_context(nc.psum_tensor(f"pt_b{p}", [128, 128], BF16))
            for p in range(2)]
    p_s2 = [regB.enter_context(nc.psum_tensor(f"p_s2{p}", [128, 1024], F32))
            for p in range(2)]
    p_o = regB.enter_context(nc.psum_tensor("p_o", [128, 1024], F32))
    p_q = [p_o[:, 0:512], p_o[:, 512:1024]]

    with nc.Block() as block:
        TE, SC, VE, GP, SY = (block.tensor, block.scalar, block.vector,
                              block.gpsimd, block.sync)

        def tr_ev(n, tr_src, ev_dst, pt, eye, tag, ev_eng, ev_sem, gbase,
                  pre_tr=None, pre_ev=None):
            ev_base = ev_sem.n

            def _t(e):
                if pre_tr is not None:
                    pre_tr(e)
                for j in range(n):
                    g = gbase + j
                    if j < 2:
                        prev = M.get((tag, 'ev', g - 2))
                        if prev is not None:
                            e.wait_ge(ev_sem.h, prev)
                    else:
                        e.wait_ge(ev_sem.h, ev_base + (j - 2) + 1)
                    M[tag, 'tr', g] = ST.inc(e.transpose(pt[g % 2][:], tr_src(j), eye[:]))
            TE(_t)

            def _e(e):
                if pre_ev is not None:
                    pre_ev(e)
                for j in range(n):
                    g = gbase + j
                    e.wait_ge(ST.h, M[tag, 'tr', g])
                    if ev_eng == 'scalar':
                        ins = e.copy(ev_dst(j), pt[g % 2][:])
                    else:
                        ins = e.tensor_copy(ev_dst(j), pt[g % 2][:])
                    M[tag, 'ev', g] = ev_sem.inc(ins)
                    assert M[tag, 'ev', g] == ev_base + j + 1
            (SC if ev_eng == 'scalar' else VE)(_e)

        def emit_ln(tag, src, dst, tt, pre_scalar=None):
            def _s1(e):
                if pre_scalar is not None:
                    pre_scalar(e)
                prev = M.get((tag, 'var', tt - 1))
                if prev is not None:
                    e.wait_ge(SV.h, prev)
                M[tag, 'sq', tt] = SS.inc(e.activation(
                    xsq[:], src[:], AF.Square, accum_out=sv[:, 1:2]))
            SC(_s1)

            def _v1(e):
                m_red = SV.inc(e.reduce_sum(sv[:, 0:1], src[:], axis=AX.X))
                e.wait_ge(SV.h, m_red)
                e.wait_ge(SS.h, M[tag, 'sq', tt])
                e.tensor_tensor(sv[:, 2:3], sv[:, 0:1], sv[:, 0:1], ALU.mult)
                m_mv = SV.inc(e.tensor_scalar_mul(sv[:, 3:4], sv[:, 0:1], 1.0 / D))
                e.wait_ge(SV.h, m_mv)
                M[tag, 'var', tt] = SV.inc(e.scalar_tensor_tensor(
                    sv[:, 4:5], sv[:, 2:3], -1.0 / D, sv[:, 1:2], ALU.mult, ALU.add))
            VE(_v1)

            def _s2(e):
                e.wait_ge(SV.h, M[tag, 'var', tt])
                M[tag, 'sqrt', tt] = SS.inc(e.activation(
                    sv[:, 5:6], sv[:, 4:5], AF.Sqrt, scale=1.0 / D, bias=eps_t[:]))
            SC(_s2)

            def _v2(e):
                e.wait_ge(SS.h, M[tag, 'sqrt', tt])
                m_r = SV.inc(e.reciprocal(sv[:, 6:7], sv[:, 5:6]))
                e.wait_ge(SV.h, m_r)
                M[tag, 'norm', tt] = SV.inc(e.tensor_scalar(
                    dst[:], src[:], sv[:, 3:4], sv[:, 6:7], ALU.subtract, ALU.mult))
            VE(_v2)

        def _p0sy(e):
            SX.inc(e.dma_start(out=eye_b[:], in_=d_eyeb), 16)
            SX.inc(e.dma_start(out=eye_f[:], in_=d_eyef), 16)
            SX.inc(e.dma_start(out=eye16[:], in_=d_eye16), 16)
        SY(_p0sy)
        M['eyes'] = SX.n

        def _p0g(e):
            SG.inc(e.dma_start(
                out=wr17b[:].rearrange("p (jc q) -> p jc q", q=17),
                in_=d_wr17.rearrange("(jc p) q -> p jc q", p=128)), 16)
        GP(_p0g)
        M['wr'] = SG.n

        def _p0v(e):
            e.memset(eps_t[:], 1e-5)
            SV.inc(e.memset(ones16[:], 1.0))
        VE(_p0v)
        M['consts_v'] = SV.n

        for tt in range(NT):
            par = tt % 2

            def _sy(e, tt=tt, par=par):
                if tt >= 2:
                    e.wait_ge(SV.h, M['ln1', 'norm', tt - 2])
                    e.wait_ge(SS.h, M['ln1', 'sq', tt - 2])
                M['p1_x', tt] = SX.inc(e.dma_start(
                    out=xstage[par][:], in_=d_x[tt * 128:(tt + 1) * 128, :]), 16)
            SY(_sy)

            def _vw(e, tt=tt):
                e.wait_ge(SX.h, M['p1_x', tt])
                prev = M.get(('p1t', 'trdone', tt - 2))
                if prev is not None:
                    e.wait_ge(ST.h, prev)
            VE(_vw)

            emit_ln('ln1', xstage[par], lnstage[par], tt,
                    pre_scalar=(lambda e, tt=tt: e.wait_ge(SX.h, M['p1_x', tt])))

            def _pre_tr(e, tt=tt):
                if tt == 0:
                    e.wait_ge(SX.h, M['eyes'])
                e.wait_ge(SV.h, M['ln1', 'norm', tt])
            tr_ev(NJ,
                  (lambda j, par=par: lnstage[par][:, j * 128:(j + 1) * 128]),
                  (lambda j, tt=tt: big0[:, j * T + tt * 128: j * T + (tt + 1) * 128]),
                  pt_b, eye_b, 'p1tr', 'scalar', SS, tt * NJ, pre_tr=_pre_tr)
            M['p1t', 'trdone', tt] = ST.n
        M['p1_done'] = SS.n

        for ic in range(NJ):
            par = ic % 2

            def _gp(e, ic=ic, par=par):
                prev = M.get(('p2_mm_l', ic - 2))
                if prev is not None:
                    e.wait_ge(ST.h, prev)
                M['p2_w', ic] = SG.inc(e.dma_start(
                    out=otst[par][:].rearrange("p (jc m) -> p jc m", m=128),
                    in_=d_wq[:, ic * 128:(ic + 1) * 128].rearrange(
                        "(jc p) m -> p jc m", p=128)), 16)
            GP(_gp)

            qev_base = SS.n

            def _mm(e, ic=ic, par=par, qev_base=qev_base):
                e.wait_ge(SG.h, M['p2_w', ic])
                if ic == 0:
                    e.wait_ge(SS.h, M['p1_done'])
                for qg in range(4):
                    k = ic * 4 + qg
                    if qg < 2:
                        prev = M.get(('p2_qev', k - 2))
                        if prev is not None:
                            e.wait_ge(SS.h, prev)
                    else:
                        e.wait_ge(SS.h, qev_base + (qg - 2) + 1)
                    ins = None
                    for jc in range(NJ):
                        ins = e.matmul(
                            p_q[k % 2], otst[par][:, jc * 128:(jc + 1) * 128],
                            big0[:, jc * T + qg * 512: jc * T + (qg + 1) * 512],
                            start=(jc == 0), stop=(jc == NJ - 1))
                    M['p2_mm', k] = ST.inc(ins)
            TE(_mm)
            M['p2_mm_l', ic] = M['p2_mm', ic * 4 + 3]

            def _ev(e, ic=ic, qev_base=qev_base):
                for qg in range(4):
                    k = ic * 4 + qg
                    e.wait_ge(ST.h, M['p2_mm', k])
                    M['p2_qev', k] = SS.inc(e.activation(
                        big1[:, ic * T + qg * 512: ic * T + (qg + 1) * 512],
                        p_q[k % 2], AF.Copy, scale=SCALE))
                    assert M['p2_qev', k] == qev_base + qg + 1
            SC(_ev)
        M['p2_done'] = SS.n

        def _p3v0(e):
            e.wait_ge(SS.h, M['p2_done'])
            e.memset(vpb[0][:], 1.0)
            M['consts_v2'] = SV.inc(e.memset(vpb[1][:], 1.0))
        VE(_p3v0)

        prev_exp = [None, None]
        prev_omm = [None, None]
        prev_oev = None
        for h in range(HEADS):
            hp = h % 2

            def _gp(e, h=h, hp=hp):
                if h == 0:
                    e.wait_ge(SS.h, M['p2_done'])
                    e.wait_ge(SV.h, M['consts_v2'])
                if h >= 2:
                    e.wait_ge(ST.h, M['p3_lastS', h - 2])
                M['p3_kdone', h] = SG.inc(e.dma_start(
                    out=ktb[hp][:], in_=d_kvk[h * 128:(h + 1) * 128, :]), 16)
                if h >= 2:
                    e.wait_ge(ST.h, M['p3_lastO', h - 2])
                srcv = d_kvv[:, :, h * 128:(h + 1) * 128].rearrange(
                    "kb kk dd -> kk kb dd")
                dstv = vpb[hp][:].rearrange("p (kb c) -> p kb c", c=132)[:, :, 0:128]
                SG.inc(e.dma_start(out=dstv[:, 0:16], in_=srcv[:, 0:16]), 16)
                M['p3_v', h] = SG.inc(e.dma_start(out=dstv[:, 16:32], in_=srcv[:, 16:32]), 16)
            GP(_gp)

            for qg in range(4):
                qbase = h * T + qg * 512
                exp_base = SS.n

                def _t3(e, h=h, hp=hp, qg=qg, qbase=qbase, exp_base=exp_base,
                        prev_exp=prev_exp, prev_oev=prev_oev):
                    if (h, qg) == (0, 0):
                        e.wait_ge(SS.h, M['p2_done'])
                        e.wait_ge(SV.h, M['consts_v2'])
                    e.wait_ge(SG.h, M['p3_kdone', h])
                    e.wait_ge(SG.h, M['p3_v', h])

                    def do_omm(pp, start_pair):
                        e.wait_ge(SS.h, exp_base + pp + 1)
                        if pp == 0 and prev_oev is not None:
                            e.wait_ge(SV.h, prev_oev)
                        ins2 = None
                        for b in range(2):
                            kb = 2 * pp + b
                            for qs in range(4):
                                ins2 = e.matmul(
                                    p_o[:, qs * 256: qs * 256 + 129],
                                    exps[pp % 2][:, b * 512 + qs * 128: b * 512 + (qs + 1) * 128],
                                    vpb[hp][:, kb * 132: kb * 132 + 129],
                                    start=(pp == 0 and b == 0 and qs % 2 == 0),
                                    stop=(pp == 15 and b == 1))
                        M['p3_omm', h, qg, pp] = ST.inc(ins2)

                    for p in range(16):
                        if p >= 2:
                            e.wait_ge(SS.h, exp_base + (p - 2) + 1)
                        elif prev_exp[p] is not None:
                            e.wait_ge(SS.h, prev_exp[p])
                        ins = None
                        for b in range(2):
                            kb = 2 * p + b
                            ins = e.matmul(
                                p_s2[p % 2][:, b * 512:(b + 1) * 512],
                                ktb[hp][:, kb * 128:(kb + 1) * 128],
                                big1[:, qbase:qbase + 512],
                                start=True, stop=True)
                        M['p3_s', h, qg, p] = ST.inc(ins)
                        if p >= 1:
                            do_omm(p - 1, False)
                    do_omm(15, False)
                TE(_t3)

                def _s3(e, h=h, qg=qg, exp_base=exp_base, prev_omm=prev_omm):
                    for p in range(16):
                        e.wait_ge(ST.h, M['p3_s', h, qg, p])
                        if p >= 2:
                            e.wait_ge(ST.h, M['p3_omm', h, qg, p - 2])
                        elif prev_omm[p] is not None:
                            e.wait_ge(ST.h, prev_omm[p])
                        M['p3_exp', h, qg, p] = SS.inc(e.activation(
                            exps[p % 2][:], p_s2[p % 2][:], AF.Exp))
                        assert M['p3_exp', h, qg, p] == exp_base + p + 1
                SC(_s3)

                def _oev(e, h=h, qg=qg):
                    e.wait_ge(ST.h, M['p3_omm', h, qg, 15])
                    ins = None
                    for qs in range(4):
                        ins = e.reciprocal(dinv[:, qs:qs + 1],
                                           p_o[:, qs * 256 + 128: qs * 256 + 129])
                    m_r = SV.inc(ins)
                    e.wait_ge(SV.h, m_r)
                    ins = None
                    for qs in range(4):
                        qt = qg * 4 + qs
                        ins = e.tensor_scalar_mul(
                            big2[:, qt * D + h * 128: qt * D + (h + 1) * 128],
                            p_o[:, qs * 256: qs * 256 + 128], dinv[:, qs:qs + 1])
                    M['p3_oev', h, qg] = SV.inc(ins)
                VE(_oev)

                prev_exp = [M['p3_exp', h, qg, 14], M['p3_exp', h, qg, 15]]
                prev_omm = [M['p3_omm', h, qg, 14], M['p3_omm', h, qg, 15]]
                prev_oev = M['p3_oev', h, qg]
                if qg == 3:
                    M['p3_lastS', h] = M['p3_s', h, 3, 15]
                    M['p3_lastO', h] = M['p3_omm', h, 3, 15]
        M['p3_done_v'] = SV.n
        M['p3_done_s'] = SS.n
        M['p3_done_t'] = ST.n

        regB.close()
        p_w = regC.enter_context(nc.psum_tensor("p_w", [128, 2048], F32))
        pt_f = [regC.enter_context(nc.psum_tensor(f"pt_f{p}", [128, 128], F32))
                for p in range(2)]

        def _gp4(e):
            e.wait_ge(ST.h, M['p3_lastS', HEADS - 1])
            for ic in range(NJ):
                M['p4_w', ic] = SG.inc(e.dma_start(
                    out=big1[:, ic * T:(ic + 1) * T],
                    in_=d_wo[ic * 128:(ic + 1) * 128, :]), 16)
        GP(_gp4)

        for qt in range(NT):
            par = qt % 2

            def _pre_otr(e, qt=qt):
                if qt == 0:
                    e.wait_ge(SS.h, M['p1_done'])
                    e.wait_ge(SV.h, M['p3_done_v'])
            tr_ev(NJ,
                  (lambda j, qt=qt: big2[:, qt * D + j * 128: qt * D + (j + 1) * 128]),
                  (lambda j, par=par: otst[par][:, j * 128:(j + 1) * 128]),
                  pt_b, eye_b, 'p4o', 'scalar', SS, qt * NJ, pre_tr=_pre_otr)

            def _sy1(e, qt=qt, par=par):
                prev = M.get(('p4_x1', qt - 2))
                if prev is not None:
                    e.wait_ge(SV.h, prev)
                elif qt < 2:
                    e.wait_ge(ST.h, M['p3_done_t'])
                M['p4_x', qt] = SX.inc(e.dma_start(
                    out=xstage[par][:], in_=d_x[qt * 128:(qt + 1) * 128, :]), 16)
            SY(_sy1)

            def _mm(e, qt=qt, par=par):
                e.wait_ge(SS.h, M['p4o', 'ev', qt * NJ + NJ - 1])
                if qt == 0:
                    e.wait_ge(SG.h, M['p4_w', NJ - 1])
                prev = M.get(('p4_x1', qt - 1))
                if prev is not None:
                    e.wait_ge(SV.h, prev)
                ins = None
                for ob in range(4):
                    for ic in range(NJ):
                        ins = e.matmul(
                            p_w[:, ob * 512:(ob + 1) * 512],
                            otst[par][:, ic * 128:(ic + 1) * 128],
                            big1[:, ic * T + ob * 512: ic * T + (ob + 1) * 512],
                            start=(ic == 0), stop=(ic == NJ - 1))
                M['p4_mm', qt] = ST.inc(ins)
            TE(_mm)

            def _vx1(e, qt=qt, par=par):
                e.wait_ge(ST.h, M['p4_mm', qt])
                e.wait_ge(SX.h, M['p4_x', qt])
                if qt >= 2:
                    e.wait_ge(SX.h, M['p4_x1st', qt - 2])
                M['p4_x1', qt] = SV.inc(e.tensor_tensor(
                    x1buf[par][:], p_w[:], xstage[par][:], ALU.add))
            VE(_vx1)

            def _sy2(e, qt=qt, par=par):
                e.wait_ge(SV.h, M['p4_x1', qt])
                M['p4_x1st', qt] = SX.inc(e.dma_start(
                    out=d_x1[qt * 128:(qt + 1) * 128, :], in_=x1buf[par][:]), 16)
            SY(_sy2)
        M['p4a_done_st'] = ST.n

        for qt in range(NT):
            par = qt % 2

            def _syb(e, qt=qt, par=par):
                e.wait_ge(SX.h, M['p4_x1st', qt])
                if qt >= 2:
                    e.wait_ge(SS.h, M['ln2', 'sq', qt - 2])
                    e.wait_ge(ST.h, M['p4x1t', 'trdone', qt - 2])
                M['p4b_x', qt] = SX.inc(e.dma_start(
                    out=x1buf[par][:], in_=d_x1[qt * 128:(qt + 1) * 128, :]), 16)
            SY(_syb)

            def _vw(e, qt=qt):
                e.wait_ge(SX.h, M['p4b_x', qt])
                prev = M.get(('p4l', 'trdone', qt - 2))
                if prev is not None:
                    e.wait_ge(ST.h, prev)
            VE(_vw)

            emit_ln('ln2', x1buf[par], lnstage[par], qt,
                    pre_scalar=(lambda e, qt=qt: e.wait_ge(SX.h, M['p4b_x', qt])))

            def _pre_lntr(e, qt=qt):
                e.wait_ge(SV.h, M['ln2', 'norm', qt])

            def _pre_lnev(e, qt=qt):
                prev = M.get(('p4_lnst', qt - 2))
                if prev is not None:
                    e.wait_ge(SX.h, prev)
            tr_ev(NJ,
                  (lambda j, par=par: lnstage[par][:, j * 128:(j + 1) * 128]),
                  (lambda j, par=par: ln2Tbuf[par][:, j * 128:(j + 1) * 128]),
                  pt_b, eye_b, 'p4l', 'vector', SV, qt * NJ,
                  pre_tr=_pre_lntr, pre_ev=_pre_lnev)
            M['p4l', 'trdone', qt] = ST.n
            M['p4_lnevd', qt] = SV.n

            def _pre_x1ev(e, qt=qt):
                prev = M.get(('p4_x1Tst', qt - 2))
                if prev is not None:
                    e.wait_ge(SX.h, prev)
            tr_ev(NJ,
                  (lambda j, par=par: x1buf[par][:, j * 128:(j + 1) * 128]),
                  (lambda j, par=par: x1Tbuf[par][:, j * 128:(j + 1) * 128]),
                  pt_f, eye_f, 'p4x1t', 'scalar', SS, qt * NJ, pre_ev=_pre_x1ev)
            M['p4x1t', 'trdone', qt] = ST.n
            M['p4_x1evd', qt] = SS.n

            def _sy3(e, qt=qt, par=par):
                e.wait_ge(SV.h, M['p4_lnevd', qt])
                M['p4_lnst', qt] = SX.inc(e.dma_start(
                    out=d_ln2T[:, qt * 128:(qt + 1) * 128].rearrange(
                        "(jc p) t -> p jc t", p=128),
                    in_=ln2Tbuf[par][:].rearrange("p (jc t) -> p jc t", t=128)), 16)
                e.wait_ge(SS.h, M['p4_x1evd', qt])
                M['p4_x1Tst', qt] = SX.inc(e.dma_start(
                    out=d_x1T[:, qt * 128:(qt + 1) * 128].rearrange(
                        "(jc p) t -> p jc t", p=128),
                    in_=x1Tbuf[par][:].rearrange("p (jc t) -> p jc t", t=128)), 16)
            SY(_sy3)
        M['p4_done_ss'] = SS.n
        M['p4_done_sv'] = SV.n
        M['p4_done_st'] = ST.n

        regC.close()
        ptb_stack.close()
        p_r = regD.enter_context(nc.psum_tensor("p_r", [128, 17], F32))
        pt_f5 = [regD.enter_context(nc.psum_tensor(f"pt_f5{p}", [128, 128], F32))
                 for p in range(2)]

        for qt in range(NT):
            def _sy(e, qt=qt):
                if qt == 0:
                    e.wait_ge(SS.h, M['p4_done_ss'])
                    e.wait_ge(SV.h, M['p4_done_sv'])
                    e.wait_ge(ST.h, M['p4_done_st'])
                else:
                    e.wait_ge(ST.h, M['p5_mm', qt - 1])
                M['p5_ln', qt] = SX.inc(e.dma_start(
                    out=routst[:].rearrange("p (jc t) -> p jc t", t=128),
                    in_=d_ln2T[:, qt * 128:(qt + 1) * 128].rearrange(
                        "(jc p) t -> p jc t", p=128)), 16)
            SY(_sy)

            def _mm(e, qt=qt):
                e.wait_ge(SX.h, M['p5_ln', qt])
                if qt == 0:
                    e.wait_ge(SG.h, M['wr'])
                else:
                    e.wait_ge(SS.h, M['p5_pr', qt - 1])
                ins = None
                for jc in range(NJ):
                    ins = e.matmul(p_r[:], routst[:, jc * 128:(jc + 1) * 128],
                                   wr17b[:, jc * 17:(jc + 1) * 17],
                                   start=(jc == 0), stop=(jc == NJ - 1))
                M['p5_mm', qt] = ST.inc(ins)
            TE(_mm)

            def _sc(e, qt=qt):
                e.wait_ge(ST.h, M['p5_mm', qt])
                prev = M.get(('p5_comb', qt - 1))
                if prev is not None:
                    e.wait_ge(SV.h, prev)
                e.activation(pe_r[:], p_r[:, 0:16], AF.Exp)
                M['p5_pr', qt] = SS.inc(e.activation(
                    sgcol[:, qt:qt + 1], p_r[:, 16:17], AF.Sigmoid))
            SC(_sc)

            def _v(e, qt=qt):
                e.wait_ge(SS.h, M['p5_pr', qt])
                m1 = SV.inc(e.reduce_sum(sv[:, 8:9], pe_r[:], axis=AX.X))
                e.wait_ge(SV.h, m1)
                m2 = SV.inc(e.reciprocal(sv[:, 9:10], sv[:, 8:9]))
                mx = SV.inc(e.max(pwork[:, 0:8], pe_r[:]))
                e.wait_ge(SV.h, mx)
                e.tensor_copy(pwork[:, 8:12], pwork[:, 0:4])
                mc = SV.inc(e.tensor_copy(pwork[:, 12:16], pwork[:, 0:4]))
                e.wait_ge(SV.h, mc)
                mr = SV.inc(e.match_replace(pmask[:], pwork[:, 8:16], pe_r[:], 0.0))
                e.wait_ge(SV.h, mr)
                e.tensor_tensor(cacc[:], pe_r[:], pmask[:], ALU.subtract)
                e.wait_ge(SV.h, m2)
                M['p5_comb', qt] = SV.inc(e.tensor_scalar_mul(
                    comb[:, qt * 16:(qt + 1) * 16], cacc[:], sv[:, 9:10]))
            VE(_v)

            def _ctr(e, qt=qt):
                e.wait_ge(SV.h, M['p5_comb', qt])
                prev = M.get(('p5_cev', qt - 2))
                if prev is not None:
                    e.wait_ge(SV.h, prev)
                M['p5_ctr', qt] = ST.inc(e.transpose(
                    pt_f5[qt % 2][0:16, :], comb[:, qt * 16:(qt + 1) * 16], eye_f[:]))
            TE(_ctr)

            def _cev(e, qt=qt):
                e.wait_ge(ST.h, M['p5_ctr', qt])
                M['p5_cev', qt] = SV.inc(e.tensor_copy(
                    combT[:, qt * 128:(qt + 1) * 128], pt_f5[qt % 2][0:16, :]))
            VE(_cev)

        def _str(e):
            e.wait_ge(SS.h, M['p5_pr', NT - 1])
            e.wait_ge(SV.h, M['p5_cev', NT - 2])
            M['p5_str'] = ST.inc(e.transpose(pt_f5[0][0:16, :], sgcol[:, 0:16], eye_f[:]))
        TE(_str)

        def _sgev(e):
            e.wait_ge(ST.h, M['p5_str'])
            M['p5_sgT'] = SV.inc(e.tensor_copy(sgT[:], pt_f5[0][0:16, :]))
        VE(_sgev)
        M['p5_done_v'] = SV.n
        M['p5_done_s'] = SS.n
        M['p5_done_t'] = ST.n

        regD.close()
        p_gu = [regE.enter_context(nc.psum_tensor(f"p_gu{p}", [128, 512], F32))
                for p in range(4)]
        p_y = [regE.enter_context(nc.psum_tensor(f"p_y{p}", [128, 512], F32))
               for p in range(2)]
        pt_f6 = [regE.enter_context(nc.psum_tensor(f"pt_f6{p}", [128, 128], F32))
                 for p in range(2)]

        prev_guh = None
        prev_yh = None
        prev_yev = None
        prev_silu_l = None
        prev_uev_l = None
        prev_aev_l = None
        prev_selread = None
        for Hh in range(2):
            tof = Hh * HALF

            def _sy(e, Hh=Hh, tof=tof):
                if Hh == 0:
                    e.wait_ge(SV.h, M['p5_done_v'])
                    e.wait_ge(SS.h, M['p5_done_s'])
                    e.wait_ge(ST.h, M['p5_done_t'])
                else:
                    e.wait_ge(ST.h, M['p6o', 'tr', 0, NTH * NJ - 1])
                M['p6_lnh', Hh] = SX.inc(e.dma_start(
                    out=ln2h[:].rearrange("p (jc t) -> p jc t", t=HALF),
                    in_=d_ln2T[:, tof:tof + HALF].rearrange(
                        "(jc p) t -> p jc t", p=128)), 16)
                M['p6_acc', Hh] = SX.inc(e.dma_start(
                    out=accT[:].rearrange("p (hc t) -> p hc t", t=HALF),
                    in_=d_x1T[:, tof:tof + HALF].rearrange(
                        "(hc p) t -> p hc t", p=128)), 16)
            SY(_sy)

            for q8 in range(NTH):
                qt = Hh * NTH + q8

                def _vsel(e, q8=q8, qt=qt, Hh=Hh, prev_selread=prev_selread):
                    if (Hh, q8) == (0, 0):
                        e.wait_ge(SV.h, M['p5_sgT'])
                    if prev_selread is not None:
                        e.wait_ge(ST.h, prev_selread)
                    M['p6_sel', Hh, q8] = SV.inc(e.tensor_scalar_mul(
                        sel_b[:], ones16[:], eye16[:, qt:qt + 1]))
                VE(_vsel)

                def _tsg(e, q8=q8, Hh=Hh, prev_silu_l=prev_silu_l,
                         prev_uev_l=prev_uev_l):
                    e.wait_ge(SV.h, M['p6_sel', Hh, q8])
                    prev = M.get(('p6_sgev', Hh, q8 - 1))
                    if prev is not None:
                        e.wait_ge(SV.h, prev)
                    elif Hh == 0:
                        e.wait_ge(SS.h, M['p5_done_s'])
                    else:
                        e.wait_ge(SS.h, prev_silu_l)
                        e.wait_ge(SV.h, prev_uev_l)
                    M['p6_sgmm', Hh, q8] = ST.inc(e.matmul(
                        p_gu[0][:, 0:128], sel_b[:], sgT[:], start=True, stop=True))
                TE(_tsg)
                prev_selread = M['p6_sgmm', Hh, q8]

                def _vsg(e, q8=q8, Hh=Hh):
                    e.wait_ge(ST.h, M['p6_sgmm', Hh, q8])
                    M['p6_sgev', Hh, q8] = SV.inc(e.tensor_copy(
                        sg_b[:, q8 * 128:(q8 + 1) * 128], p_gu[0][:, 0:128]))
                VE(_vsg)

            for u in range(E + NBAND):
                is_sh = u >= E
                bb = u - E

                if not is_sh:
                    def _vsel2(e, u=u, Hh=Hh, prev_selread=prev_selread):
                        e.wait_ge(ST.h, prev_selread)
                        M['p6_sel2', Hh, u] = SV.inc(e.tensor_scalar_mul(
                            sel_b[:], ones16[:], eye16[:, u:u + 1]))
                    VE(_vsel2)

                    def _tcb(e, u=u, Hh=Hh, tof=tof,
                             prev_silu_l=prev_silu_l, prev_uev_l=prev_uev_l):
                        e.wait_ge(SV.h, M['p6_sel2', Hh, u])
                        if prev_silu_l is not None:
                            e.wait_ge(SS.h, prev_silu_l)
                            e.wait_ge(SV.h, prev_uev_l)
                        else:
                            e.wait_ge(SV.h, M['p6_sgev', 0, NTH - 1])
                        e.matmul(p_gu[0][:], sel_b[:],
                                 combT[:, tof:tof + 512], start=True, stop=True)
                        ins = e.matmul(p_gu[1][:], sel_b[:],
                                       combT[:, tof + 512:tof + 1024],
                                       start=True, stop=True)
                        M['p6_cbmm', Hh, u] = ST.inc(ins)
                    TE(_tcb)
                    prev_selread = M['p6_cbmm', Hh, u]

                    def _vcb(e, u=u, Hh=Hh):
                        e.wait_ge(ST.h, M['p6_cbmm', Hh, u])
                        e.tensor_copy(cb_b[:, 0:512], p_gu[0][:])
                        M['p6_cbev', Hh, u] = SV.inc(
                            e.tensor_copy(cb_b[:, 512:1024], p_gu[1][:]))
                    VE(_vcb)

                NG = 2 * NI
                st_gu = ST.n
                ss_gu = SS.n
                sv_gu = SV.n

                def _gpw(e, u=u, Hh=Hh, is_sh=is_sh, bb=bb,
                         prev_guh=prev_guh, st_gu=st_gu):
                    if (Hh, u) == (0, 0):
                        e.wait_ge(ST.h, M['p5_done_t'])
                    for icc in range(NI):
                        gp = icc % 2
                        if icc >= 2:
                            e.wait_ge(ST.h, st_gu + 4 * (icc - 1))
                        elif prev_guh is not None:
                            e.wait_ge(ST.h, prev_guh)
                        if is_sh:
                            colg = d_wsg[:, (bb * NI + icc) * 128:(bb * NI + icc + 1) * 128]
                            colu = d_wsu[:, (bb * NI + icc) * 128:(bb * NI + icc + 1) * 128]
                        else:
                            colg = d_wg[u][:, icc * 128:(icc + 1) * 128]
                            colu = d_wu[u][:, icc * 128:(icc + 1) * 128]
                        SG.inc(e.dma_start(
                            out=wgc[gp][:].rearrange("p (jc i) -> p jc i", i=128),
                            in_=colg.rearrange("(jc p) i -> p jc i", p=128)), 16)
                        M['p6_w', Hh, u, icc] = SG.inc(e.dma_start(
                            out=wuc[gp][:].rearrange("p (jc i) -> p jc i", i=128),
                            in_=colu.rearrange("(jc p) i -> p jc i", p=128)), 16)
                GP(_gpw)

                def _tgu(e, u=u, Hh=Hh, is_sh=is_sh,
                         prev_silu_l=prev_silu_l, prev_uev_l=prev_uev_l,
                         st_gu=st_gu, ss_gu=ss_gu, sv_gu=sv_gu):
                    for k in range(NG):
                        icc, tg = k // 2, k % 2
                        gp = icc % 2
                        if tg == 0:
                            e.wait_ge(SG.h, M['p6_w', Hh, u, icc])
                        if k == 0:
                            e.wait_ge(SX.h, M['p6_lnh', Hh])
                        if k < 2:
                            if not is_sh:
                                e.wait_ge(SV.h, M['p6_cbev', Hh, u])
                            elif prev_silu_l is not None:
                                e.wait_ge(SS.h, prev_silu_l)
                                e.wait_ge(SV.h, prev_uev_l)
                        else:
                            e.wait_ge(SS.h, ss_gu + k - 1)
                            e.wait_ge(SV.h, sv_gu + 2 * k - 3)
                        ins = None
                        for jc in range(NJ):
                            ins = e.matmul(
                                p_gu[tg][:], wgc[gp][:, jc * 128:(jc + 1) * 128],
                                ln2h[:, jc * HALF + tg * 512: jc * HALF + (tg + 1) * 512],
                                start=(jc == 0), stop=(jc == NJ - 1))
                        M['p6_gmm', Hh, u, k] = ST.inc(ins)
                        ins = None
                        for jc in range(NJ):
                            ins = e.matmul(
                                p_gu[2 + tg][:], wuc[gp][:, jc * 128:(jc + 1) * 128],
                                ln2h[:, jc * HALF + tg * 512: jc * HALF + (tg + 1) * 512],
                                start=(jc == 0), stop=(jc == NJ - 1))
                        M['p6_umm', Hh, u, k] = ST.inc(ins)
                        if tg == 1:
                            M['p6_guh', Hh, u, icc] = ST.n
                            assert M['p6_guh', Hh, u, icc] == st_gu + 4 * (icc + 1)
                TE(_tgu)

                def _ssi(e, u=u, Hh=Hh, prev_aev_l=prev_aev_l,
                         ss_gu=ss_gu, sv_gu=sv_gu):
                    for k in range(NG):
                        tg = k % 2
                        e.wait_ge(ST.h, M['p6_gmm', Hh, u, k])
                        if k >= 2:
                            e.wait_ge(SV.h, sv_gu + 2 * k - 2)
                        elif prev_aev_l is not None:
                            e.wait_ge(SV.h, prev_aev_l)
                        M['p6_sgevac', Hh, u, k] = SS.inc(e.activation(
                            sgb_[:, tg * 512:(tg + 1) * 512], p_gu[tg][:], AF.Silu))
                        assert M['p6_sgevac', Hh, u, k] == ss_gu + k + 1
                SC(_ssi)

                def _vu(e, u=u, Hh=Hh, is_sh=is_sh, prev_yh=prev_yh, sv_gu=sv_gu):
                    gate_src = sg_b if is_sh else cb_b
                    for k in range(NG):
                        icc, tg = k // 2, k % 2
                        e.wait_ge(ST.h, M['p6_umm', Hh, u, k])
                        M['p6_uevac', Hh, u, k] = SV.inc(e.tensor_tensor(
                            ubuf[:, tg * 512:(tg + 1) * 512], p_gu[2 + tg][:],
                            gate_src[:, tg * 512:(tg + 1) * 512], ALU.mult))
                        e.wait_ge(SS.h, M['p6_sgevac', Hh, u, k])
                        if k == 0 and prev_yh is not None:
                            e.wait_ge(ST.h, prev_yh)
                        M['p6_aev', Hh, u, k] = SV.inc(e.tensor_tensor(
                            a_sb[:, icc * HALF + tg * 512: icc * HALF + (tg + 1) * 512],
                            sgb_[:, tg * 512:(tg + 1) * 512],
                            ubuf[:, tg * 512:(tg + 1) * 512], ALU.mult))
                        assert M['p6_aev', Hh, u, k] == sv_gu + 2 * k + 2
                VE(_vu)
                prev_guh = M['p6_guh', Hh, u, NI - 1]
                prev_silu_l = M['p6_sgevac', Hh, u, NG - 1]
                prev_uev_l = M['p6_uevac', Hh, u, NG - 1]
                prev_aev_l = M['p6_aev', Hh, u, NG - 1]
                M['p6_adone', Hh, u] = SV.n

                st_dn = ST.n
                sv_dn = SV.n

                def _gpd(e, u=u, Hh=Hh, is_sh=is_sh, bb=bb,
                         prev_yh=prev_yh, st_dn=st_dn):
                    for hc in range(NJ):
                        wp = hc % 2
                        if hc >= 2:
                            e.wait_ge(ST.h, st_dn + 2 * (hc - 1))
                        elif prev_yh is not None:
                            e.wait_ge(ST.h, prev_yh)
                        if is_sh:
                            blk = d_wsd[bb * I_:(bb + 1) * I_, hc * 128:(hc + 1) * 128]
                        else:
                            blk = d_wd[u][:, hc * 128:(hc + 1) * 128]
                        M['p6_wd', Hh, u, hc] = SG.inc(e.dma_start(
                            out=wdc[wp][:].rearrange("p (ic hh) -> p ic hh", hh=128),
                            in_=blk.rearrange("(ic p) hh -> p ic hh", p=128)), 16)
                GP(_gpd)

                def _tdn(e, u=u, Hh=Hh, prev_yev=prev_yev, st_dn=st_dn, sv_dn=sv_dn):
                    for hc in range(NJ):
                        wp = hc % 2
                        e.wait_ge(SG.h, M['p6_wd', Hh, u, hc])
                        if hc == 0:
                            e.wait_ge(SV.h, M['p6_adone', Hh, u])
                            e.wait_ge(SX.h, M['p6_acc', Hh])
                            if (Hh, u) == (0, 0):
                                e.wait_ge(SS.h, M['p5_done_s'])
                        ins = None
                        for tg in range(2):
                            y = hc * 2 + tg
                            if y >= 2:
                                e.wait_ge(SV.h, sv_dn + y - 1)
                            elif prev_yev is not None:
                                e.wait_ge(SV.h, prev_yev)
                            for icc in range(NI):
                                ins = e.matmul(
                                    p_y[y % 2][:], wdc[wp][:, icc * 128:(icc + 1) * 128],
                                    a_sb[:, icc * HALF + tg * 512: icc * HALF + (tg + 1) * 512],
                                    start=(icc == 0), stop=(icc == NI - 1))
                            M['p6_ymm', Hh, u, y] = ST.inc(ins)
                        M['p6_yh', Hh, u, hc] = ST.n
                        assert M['p6_yh', Hh, u, hc] == st_dn + 2 * (hc + 1)
                TE(_tdn)

                def _vy(e, u=u, Hh=Hh, sv_dn=sv_dn):
                    for hc in range(NJ):
                        for tg in range(2):
                            y = hc * 2 + tg
                            e.wait_ge(ST.h, M['p6_ymm', Hh, u, y])
                            dst = accT[:, hc * HALF + tg * 512: hc * HALF + (tg + 1) * 512]
                            M['p6_yev', Hh, u, y] = SV.inc(e.tensor_tensor(
                                dst, dst, p_y[y % 2][:], ALU.add))
                            assert M['p6_yev', Hh, u, y] == sv_dn + y + 1
                VE(_vy)
                prev_yh = M['p6_yh', Hh, u, NJ - 1]
                prev_yev = M['p6_yev', Hh, u, 2 * NJ - 1]

            for q8 in range(NTH):
                op = q8 % 2

                def _pre_otr(e, q8=q8, Hh=Hh, prev_yev=prev_yev):
                    if q8 == 0:
                        e.wait_ge(SV.h, prev_yev)
                    prev = M.get(('p6_obst', Hh, q8 - 2))
                    if prev is not None:
                        e.wait_ge(SO.h, prev)
                tr_ev(NJ,
                      (lambda j, q8=q8: accT[:, j * HALF + q8 * 128: j * HALF + (q8 + 1) * 128]),
                      (lambda j, op=op: outbuf[op][:, j * 128:(j + 1) * 128]),
                      pt_f6, eye_f, ('p6o', Hh), 'scalar', SS, q8 * NJ,
                      pre_tr=_pre_otr)
                M['p6_obuf', Hh, q8] = SS.n

                def _syo(e, q8=q8, op=op, Hh=Hh, tof=tof):
                    e.wait_ge(SS.h, M['p6_obuf', Hh, q8])
                    r0 = tof + q8 * 128
                    M['p6_obst', Hh, q8] = SO.inc(e.dma_start(
                        out=d_out[r0:r0 + 128, :], in_=outbuf[op][:]), 16)
                SY(_syo)
            M['p6o', 'tr', Hh, NTH * NJ - 1] = M[('p6o', Hh), 'tr', NTH * NJ - 1]

        def _fin(e):
            e.wait_ge(SO.h, SO.n)
        SY(_fin)

    return nc


_NC_CACHE = None


def _get_program():
    global _NC_CACHE
    if _NC_CACHE is None:
        _NC_CACHE = build_program()
    return _NC_CACHE


def kernel(L0_attn_query, L0_attn_kv_k, L0_attn_kv_v, L0_attn_Wq, L0_attn_Wo,
           L0_ffn_w_router, L0_ffn_w_gate, L0_ffn_w_up, L0_ffn_w_down,
           L0_ffn_ws_gate, L0_ffn_ws_up, L0_ffn_ws_down, L0_ffn_w_shared_gate):
    xq = np.ascontiguousarray(np.asarray(L0_attn_query, dtype=np.float32)).reshape(-1, D)
    wr17 = np.ascontiguousarray(
        np.concatenate([np.asarray(L0_ffn_w_router, dtype=np.float32),
                        np.asarray(L0_ffn_w_shared_gate, dtype=np.float32)], axis=1))
    common = {
        "kv_kT": np.ascontiguousarray(
            np.asarray(L0_attn_kv_k, dtype=np.float32).reshape(KV, D).T),
        "kv_v": np.ascontiguousarray(np.asarray(L0_attn_kv_v, dtype=np.float32)),
        "WqT": np.ascontiguousarray(np.asarray(L0_attn_Wq, dtype=np.float32).T),
        "WoT": np.ascontiguousarray(np.asarray(L0_attn_Wo, dtype=np.float32).T),
        "wr17": wr17,
        "w_gate": np.ascontiguousarray(np.asarray(L0_ffn_w_gate, dtype=np.float32)),
        "w_up": np.ascontiguousarray(np.asarray(L0_ffn_w_up, dtype=np.float32)),
        "w_down": np.ascontiguousarray(np.asarray(L0_ffn_w_down, dtype=np.float32)),
        "ws_gate": np.ascontiguousarray(np.asarray(L0_ffn_ws_gate, dtype=np.float32)),
        "ws_up": np.ascontiguousarray(np.asarray(L0_ffn_ws_up, dtype=np.float32)),
        "ws_down": np.ascontiguousarray(np.asarray(L0_ffn_ws_down, dtype=np.float32)),
        "eye_b": np.eye(128, dtype=np.float32).astype(ml_dtypes.bfloat16),
        "eye_f": np.eye(128, dtype=np.float32),
        "eye16": np.eye(16, dtype=np.float32),
    }
    in_maps = []
    for c in range(NCORES):
        m = dict(common)
        m["xq"] = np.ascontiguousarray(xq[c * T:(c + 1) * T, :])
        in_maps.append(m)
    nc = _get_program()
    res = run_bass_kernel_spmd(nc, in_maps, core_ids=list(range(NCORES)))
    out = np.concatenate([np.asarray(r["out"]) for r in res.results], axis=0)
    return out.reshape(4, 4096, 2048).astype(np.float32)


# version 10
# speedup vs baseline: 1.4364x; 1.0641x over previous
import math
from contextlib import ExitStack

import numpy as np
import ml_dtypes
import concourse.bass as bass
import concourse.mybir as mybir
from concourse.bass_utils import run_bass_kernel_spmd

F32 = mybir.dt.float32
BF16 = mybir.dt.bfloat16
AF = mybir.ActivationFunctionType
ALU = mybir.AluOpType
AX = mybir.AxisListType

NCORES = 8
T = 2048
D = 2048
NT = T // 128
NJ = D // 128
HEADS = 16
KV = 4096
NKB = KV // 128
E = 16
I_ = 1408
NI = I_ // 128
IS = 5632
NBAND = 4
SCALE = 1.0 / math.sqrt(128.0)
HALF = T // 2
NTH = HALF // 128


class Sem:
    def __init__(self, nc, name):
        self.h = nc.alloc_semaphore(name)
        self.n = 0

    def inc(self, instr, k=1):
        instr.then_inc(self.h, k)
        self.n += k
        return self.n


def build_program():
    nc = bass.Bass("TRN2", target_bir_lowering=False)

    d_x = nc.dram_tensor("xq", [T, D], F32, kind="ExternalInput").ap()
    d_kvk = nc.dram_tensor("kv_kT", [D, KV], F32, kind="ExternalInput").ap()
    d_kvv = nc.dram_tensor("kv_v", [NKB, 128, D], F32, kind="ExternalInput").ap()
    d_wq = nc.dram_tensor("WqT", [D, D], F32, kind="ExternalInput").ap()
    d_wo = nc.dram_tensor("WoT", [D, D], F32, kind="ExternalInput").ap()
    d_wr17 = nc.dram_tensor("wr17", [D, 17], F32, kind="ExternalInput").ap()
    d_wg = nc.dram_tensor("w_gate", [E, D, I_], F32, kind="ExternalInput").ap()
    d_wu = nc.dram_tensor("w_up", [E, D, I_], F32, kind="ExternalInput").ap()
    d_wd = nc.dram_tensor("w_down", [E, I_, D], F32, kind="ExternalInput").ap()
    d_wsg = nc.dram_tensor("ws_gate", [D, IS], F32, kind="ExternalInput").ap()
    d_wsu = nc.dram_tensor("ws_up", [D, IS], F32, kind="ExternalInput").ap()
    d_wsd = nc.dram_tensor("ws_down", [IS, D], F32, kind="ExternalInput").ap()
    d_eyeb = nc.dram_tensor("eye_b", [128, 128], BF16, kind="ExternalInput").ap()
    d_eyef = nc.dram_tensor("eye_f", [128, 128], F32, kind="ExternalInput").ap()
    d_eye16 = nc.dram_tensor("eye16", [16, 16], F32, kind="ExternalInput").ap()
    d_out = nc.dram_tensor("out", [T, D], F32, kind="ExternalOutput").ap()
    d_x1 = nc.dram_tensor("x1_scr", [T, D], F32, kind="Internal").ap()
    d_x1T = nc.dram_tensor("x1T_scr", [D, T], F32, kind="Internal").ap()
    d_ln2T = nc.dram_tensor("ln2T_scr", [D, T], BF16, kind="Internal").ap()

    A = nc.alloc_sbuf_tensor
    eye_b = A("s_eyeb", [128, 128], BF16)
    eye_f = A("s_eyef", [128, 128], F32)
    eye16 = A("eye16s", [16, 16], F32)
    ones16 = A("ones16", [16, 128], F32)
    sel_b = A("sel_b", [16, 128], F32)
    eps_t = A("eps_t", [128, 1], F32)
    wr17b = A("wr17b", [128, NJ * 17], BF16)
    comb = A("comb", [128, NT * 16], F32)
    combT = A("combT", [16, T], F32)
    sgcol = A("sgcol", [128, NT], F32)
    sgT = A("sgT", [16, 128], F32)
    sv = A("sv", [128, 32], F32)
    dinv = A("dinv", [128, 4], F32)
    pmask = A("pmask", [128, 16], F32)
    pwork = A("pwork", [128, 16], F32)
    cacc = A("cacc", [128, 16], F32)
    pe_r = A("pe_r", [128, 16], F32)

    KB = 1024
    MB = (nc.sbuf_base + 31) & ~31
    AT = nc.alloc_sbuf_tensor_at

    def at(name, shape, dtype, off):
        return AT(name, shape, dtype, offset=MB + off)

    big1 = at("big1", [128, NJ * T], BF16, 0)
    big0 = at("big0", [128, NJ * T], BF16, 64 * KB)
    big2 = at("big2", [128, NT * D], BF16, 64 * KB)
    xstage = [at(f"xstage{p}", [128, D], F32, 128 * KB + p * 8 * KB) for p in range(2)]
    lnstage = [at(f"lnstage{p}", [128, D], BF16, 144 * KB + p * 4 * KB) for p in range(2)]
    xsq = at("xsq", [128, D], F32, 152 * KB)
    wstage = [at(f"wstage{p}", [128, D], BF16, 160 * KB + p * 4 * KB) for p in range(2)]
    otst = [at(f"otst{p}", [128, NJ * 128], BF16, 168 * KB + p * 4 * KB) for p in range(2)]
    x1buf = [at(f"x1buf{p}", [128, D], F32, 176 * KB + p * 8 * KB) for p in range(2)]
    kstg = at("kstg", [128, NKB * 128], BF16, 128 * KB)
    ktb = [at(f"ktb{p}", [128, NKB * 128], BF16, 136 * KB + p * 8 * KB) for p in range(2)]
    vpb = [at(f"vpb{p}", [128, NKB * 132], BF16, 152 * KB + p * 8448) for p in range(2)]
    exps = [at(f"exps{p}", [128, 1024], BF16, 152 * KB + 16896 + p * 2048) for p in range(2)]
    x1Tbuf = [at(f"x1Tbuf{p}", [128, NJ * 128], F32, 128 * KB + p * 8 * KB) for p in range(2)]
    ln2Tbuf = [at(f"ln2Tbuf{p}", [128, NJ * 128], BF16, 160 * KB + p * 4 * KB) for p in range(2)]
    routst = at("routst", [128, NJ * 128], BF16, 168 * KB)
    ln2h = at("ln2h", [128, NJ * HALF], BF16, 0)
    accT = at("accT", [128, NJ * HALF], F32, 32 * KB)
    a_sb = at("a_sb", [128, NI * HALF], BF16, 96 * KB)
    wgc = [at(f"wgc{p}", [128, NJ * 128], BF16, off)
           for p, off in enumerate((118 * KB, 122 * KB, 176 * KB))]
    wuc = [at(f"wuc{p}", [128, NJ * 128], BF16, off)
           for p, off in enumerate((126 * KB, 130 * KB, 180 * KB))]
    wdc = [at(f"wdc{p}", [128, NI * 128], BF16, off)
           for p, off in enumerate((134 * KB, 134 * KB + 2816, 184 * KB))]
    sgb_ = at("sgb", [128, 1024], F32, 140 * KB)
    ubuf = at("ubuf", [128, 1024], F32, 144 * KB)
    cb_b = at("cb_b", [128, HALF], F32, 148 * KB)
    sg_b = at("sg_b", [128, HALF], F32, 152 * KB)
    outbuf = [at(f"outbuf{p}", [128, D], F32, 156 * KB + p * 8 * KB) for p in range(2)]

    SX = Sem(nc, "SX")
    SG = Sem(nc, "SG")
    SV = Sem(nc, "SV")
    SS = Sem(nc, "SS")
    ST = Sem(nc, "ST")
    SO = Sem(nc, "SO")
    M = {}

    ptb_stack = ExitStack()
    regB = ExitStack()
    regC = ExitStack()
    regD = ExitStack()
    regE = ExitStack()
    pt_b = [ptb_stack.enter_context(nc.psum_tensor(f"pt_b{p}", [128, 128], BF16))
            for p in range(2)]
    p_s2 = [regB.enter_context(nc.psum_tensor(f"p_s2{p}", [128, 1024], F32))
            for p in range(2)]
    p_o = regB.enter_context(nc.psum_tensor("p_o", [128, 1024], F32))
    p_q = [p_o[:, 0:512], p_o[:, 512:1024]]

    with nc.Block() as block:
        TE, SC, VE, GP, SY = (block.tensor, block.scalar, block.vector,
                              block.gpsimd, block.sync)

        def tr_ev(n, tr_src, ev_dst, pt, eye, tag, ev_eng, ev_sem, gbase,
                  pre_tr=None, pre_ev=None):
            ev_base = ev_sem.n

            def _t(e):
                if pre_tr is not None:
                    pre_tr(e)
                for j in range(n):
                    g = gbase + j
                    if j < 2:
                        prev = M.get((tag, 'ev', g - 2))
                        if prev is not None:
                            e.wait_ge(ev_sem.h, prev)
                    else:
                        e.wait_ge(ev_sem.h, ev_base + (j - 2) + 1)
                    M[tag, 'tr', g] = ST.inc(e.transpose(pt[g % 2][:], tr_src(j), eye[:]))
            TE(_t)

            def _e(e):
                if pre_ev is not None:
                    pre_ev(e)
                for j in range(n):
                    g = gbase + j
                    e.wait_ge(ST.h, M[tag, 'tr', g])
                    if ev_eng == 'scalar':
                        ins = e.copy(ev_dst(j), pt[g % 2][:])
                    else:
                        ins = e.tensor_copy(ev_dst(j), pt[g % 2][:])
                    M[tag, 'ev', g] = ev_sem.inc(ins)
                    assert M[tag, 'ev', g] == ev_base + j + 1
            (SC if ev_eng == 'scalar' else VE)(_e)

        def emit_ln(tag, src, dst, tt, pre_scalar=None):
            def _s1(e):
                if pre_scalar is not None:
                    pre_scalar(e)
                prev = M.get((tag, 'var', tt - 1))
                if prev is not None:
                    e.wait_ge(SV.h, prev)
                M[tag, 'sq', tt] = SS.inc(e.activation(
                    xsq[:], src[:], AF.Square, accum_out=sv[:, 1:2]))
            SC(_s1)

            def _v1(e):
                m_red = SV.inc(e.reduce_sum(sv[:, 0:1], src[:], axis=AX.X))
                e.wait_ge(SV.h, m_red)
                e.wait_ge(SS.h, M[tag, 'sq', tt])
                e.tensor_tensor(sv[:, 2:3], sv[:, 0:1], sv[:, 0:1], ALU.mult)
                m_mv = SV.inc(e.tensor_scalar_mul(sv[:, 3:4], sv[:, 0:1], 1.0 / D))
                e.wait_ge(SV.h, m_mv)
                M[tag, 'var', tt] = SV.inc(e.scalar_tensor_tensor(
                    sv[:, 4:5], sv[:, 2:3], -1.0 / D, sv[:, 1:2], ALU.mult, ALU.add))
            VE(_v1)

            def _s2(e):
                e.wait_ge(SV.h, M[tag, 'var', tt])
                M[tag, 'sqrt', tt] = SS.inc(e.activation(
                    sv[:, 5:6], sv[:, 4:5], AF.Sqrt, scale=1.0 / D, bias=eps_t[:]))
            SC(_s2)

            def _v2(e):
                e.wait_ge(SS.h, M[tag, 'sqrt', tt])
                m_r = SV.inc(e.reciprocal(sv[:, 6:7], sv[:, 5:6]))
                e.wait_ge(SV.h, m_r)
                M[tag, 'norm', tt] = SV.inc(e.tensor_scalar(
                    dst[:], src[:], sv[:, 3:4], sv[:, 6:7], ALU.subtract, ALU.mult))
            VE(_v2)

        def _p0sy(e):
            SX.inc(e.dma_start(out=eye_b[:], in_=d_eyeb), 16)
            SX.inc(e.dma_start(out=eye_f[:], in_=d_eyef), 16)
            SX.inc(e.dma_start(out=eye16[:], in_=d_eye16), 16)
        SY(_p0sy)
        M['eyes'] = SX.n

        def _p0g(e):
            SG.inc(e.dma_start(
                out=wr17b[:].rearrange("p (jc q) -> p jc q", q=17),
                in_=d_wr17.rearrange("(jc p) q -> p jc q", p=128)), 16)
        GP(_p0g)
        M['wr'] = SG.n

        def _p0v(e):
            e.memset(eps_t[:], 1e-5)
            SV.inc(e.memset(ones16[:], 1.0))
        VE(_p0v)
        M['consts_v'] = SV.n

        for tt in range(NT):
            par = tt % 2

            def _sy(e, tt=tt, par=par):
                if tt >= 2:
                    e.wait_ge(SV.h, M['ln1', 'norm', tt - 2])
                    e.wait_ge(SS.h, M['ln1', 'sq', tt - 2])
                M['p1_x', tt] = SX.inc(e.dma_start(
                    out=xstage[par][:], in_=d_x[tt * 128:(tt + 1) * 128, :]), 16)
            SY(_sy)

            def _vw(e, tt=tt):
                e.wait_ge(SX.h, M['p1_x', tt])
                prev = M.get(('p1t', 'trdone', tt - 2))
                if prev is not None:
                    e.wait_ge(ST.h, prev)
            VE(_vw)

            emit_ln('ln1', xstage[par], lnstage[par], tt,
                    pre_scalar=(lambda e, tt=tt: e.wait_ge(SX.h, M['p1_x', tt])))

            def _pre_tr(e, tt=tt):
                if tt == 0:
                    e.wait_ge(SX.h, M['eyes'])
                e.wait_ge(SV.h, M['ln1', 'norm', tt])
            tr_ev(NJ,
                  (lambda j, par=par: lnstage[par][:, j * 128:(j + 1) * 128]),
                  (lambda j, tt=tt: big0[:, j * T + tt * 128: j * T + (tt + 1) * 128]),
                  pt_b, eye_b, 'p1tr', 'scalar', SS, tt * NJ, pre_tr=_pre_tr)
            M['p1t', 'trdone', tt] = ST.n
        M['p1_done'] = SS.n

        for ic in range(NJ):
            par = ic % 2

            def _gp(e, ic=ic, par=par):
                prev = M.get(('p2_mm_l', ic - 2))
                if prev is not None:
                    e.wait_ge(ST.h, prev)
                M['p2_w', ic] = SG.inc(e.dma_start(
                    out=otst[par][:].rearrange("p (jc m) -> p jc m", m=128),
                    in_=d_wq[:, ic * 128:(ic + 1) * 128].rearrange(
                        "(jc p) m -> p jc m", p=128)), 16)
            GP(_gp)

            qev_base = SS.n

            def _mm(e, ic=ic, par=par, qev_base=qev_base):
                e.wait_ge(SG.h, M['p2_w', ic])
                if ic == 0:
                    e.wait_ge(SS.h, M['p1_done'])
                for qg in range(4):
                    k = ic * 4 + qg
                    if qg < 2:
                        prev = M.get(('p2_qev', k - 2))
                        if prev is not None:
                            e.wait_ge(SS.h, prev)
                    else:
                        e.wait_ge(SS.h, qev_base + (qg - 2) + 1)
                    ins = None
                    for jc in range(NJ):
                        ins = e.matmul(
                            p_q[k % 2], otst[par][:, jc * 128:(jc + 1) * 128],
                            big0[:, jc * T + qg * 512: jc * T + (qg + 1) * 512],
                            start=(jc == 0), stop=(jc == NJ - 1))
                    M['p2_mm', k] = ST.inc(ins)
            TE(_mm)
            M['p2_mm_l', ic] = M['p2_mm', ic * 4 + 3]

            def _ev(e, ic=ic, qev_base=qev_base):
                for qg in range(4):
                    k = ic * 4 + qg
                    e.wait_ge(ST.h, M['p2_mm', k])
                    M['p2_qev', k] = SS.inc(e.activation(
                        big1[:, ic * T + qg * 512: ic * T + (qg + 1) * 512],
                        p_q[k % 2], AF.Copy, scale=SCALE))
                    assert M['p2_qev', k] == qev_base + qg + 1
            SC(_ev)
        M['p2_done'] = SS.n

        def _p3v0(e):
            e.wait_ge(SS.h, M['p2_done'])
            e.memset(vpb[0][:], 1.0)
            M['consts_v2'] = SV.inc(e.memset(vpb[1][:], 1.0))
        VE(_p3v0)

        prev_exp = [None, None]
        prev_omm = [None, None]
        prev_oev = None
        for h in range(HEADS):
            hp = h % 2

            def _gp(e, h=h, hp=hp):
                if h == 0:
                    e.wait_ge(SS.h, M['p2_done'])
                    e.wait_ge(SV.h, M['consts_v2'])
                if h >= 2:
                    e.wait_ge(ST.h, M['p3_lastS', h - 2])
                M['p3_kdone', h] = SG.inc(e.dma_start(
                    out=ktb[hp][:], in_=d_kvk[h * 128:(h + 1) * 128, :]), 16)
                if h >= 2:
                    e.wait_ge(ST.h, M['p3_lastO', h - 2])
                srcv = d_kvv[:, :, h * 128:(h + 1) * 128].rearrange(
                    "kb kk dd -> kk kb dd")
                dstv = vpb[hp][:].rearrange("p (kb c) -> p kb c", c=132)[:, :, 0:128]
                SG.inc(e.dma_start(out=dstv[:, 0:16], in_=srcv[:, 0:16]), 16)
                M['p3_v', h] = SG.inc(e.dma_start(out=dstv[:, 16:32], in_=srcv[:, 16:32]), 16)
            GP(_gp)

            for qg in range(4):
                qbase = h * T + qg * 512
                exp_base = SS.n

                def _t3(e, h=h, hp=hp, qg=qg, qbase=qbase, exp_base=exp_base,
                        prev_exp=prev_exp, prev_oev=prev_oev):
                    if (h, qg) == (0, 0):
                        e.wait_ge(SS.h, M['p2_done'])
                        e.wait_ge(SV.h, M['consts_v2'])
                    e.wait_ge(SG.h, M['p3_kdone', h])
                    e.wait_ge(SG.h, M['p3_v', h])

                    def do_omm(pp, start_pair):
                        e.wait_ge(SS.h, exp_base + pp + 1)
                        if pp == 0 and prev_oev is not None:
                            e.wait_ge(SV.h, prev_oev)
                        ins2 = None
                        for b in range(2):
                            kb = 2 * pp + b
                            for qs in range(4):
                                ins2 = e.matmul(
                                    p_o[:, qs * 256: qs * 256 + 129],
                                    exps[pp % 2][:, b * 512 + qs * 128: b * 512 + (qs + 1) * 128],
                                    vpb[hp][:, kb * 132: kb * 132 + 129],
                                    start=(pp == 0 and b == 0 and qs % 2 == 0),
                                    stop=(pp == 15 and b == 1))
                        M['p3_omm', h, qg, pp] = ST.inc(ins2)

                    for p in range(16):
                        if p >= 2:
                            e.wait_ge(SS.h, exp_base + (p - 2) + 1)
                        elif prev_exp[p] is not None:
                            e.wait_ge(SS.h, prev_exp[p])
                        ins = None
                        for b in range(2):
                            kb = 2 * p + b
                            ins = e.matmul(
                                p_s2[p % 2][:, b * 512:(b + 1) * 512],
                                ktb[hp][:, kb * 128:(kb + 1) * 128],
                                big1[:, qbase:qbase + 512],
                                start=True, stop=True)
                        M['p3_s', h, qg, p] = ST.inc(ins)
                        if p >= 1:
                            do_omm(p - 1, False)
                    do_omm(15, False)
                TE(_t3)

                def _s3(e, h=h, qg=qg, exp_base=exp_base, prev_omm=prev_omm):
                    for p in range(16):
                        e.wait_ge(ST.h, M['p3_s', h, qg, p])
                        if p >= 2:
                            e.wait_ge(ST.h, M['p3_omm', h, qg, p - 2])
                        elif prev_omm[p] is not None:
                            e.wait_ge(ST.h, prev_omm[p])
                        M['p3_exp', h, qg, p] = SS.inc(e.activation(
                            exps[p % 2][:], p_s2[p % 2][:], AF.Exp))
                        assert M['p3_exp', h, qg, p] == exp_base + p + 1
                SC(_s3)

                def _oev(e, h=h, qg=qg):
                    e.wait_ge(ST.h, M['p3_omm', h, qg, 15])
                    ins = None
                    for qs in range(4):
                        ins = e.reciprocal(dinv[:, qs:qs + 1],
                                           p_o[:, qs * 256 + 128: qs * 256 + 129])
                    m_r = SV.inc(ins)
                    e.wait_ge(SV.h, m_r)
                    ins = None
                    for qs in range(4):
                        qt = qg * 4 + qs
                        ins = e.tensor_scalar_mul(
                            big2[:, qt * D + h * 128: qt * D + (h + 1) * 128],
                            p_o[:, qs * 256: qs * 256 + 128], dinv[:, qs:qs + 1])
                    M['p3_oev', h, qg] = SV.inc(ins)
                VE(_oev)

                prev_exp = [M['p3_exp', h, qg, 14], M['p3_exp', h, qg, 15]]
                prev_omm = [M['p3_omm', h, qg, 14], M['p3_omm', h, qg, 15]]
                prev_oev = M['p3_oev', h, qg]
                if qg == 3:
                    M['p3_lastS', h] = M['p3_s', h, 3, 15]
                    M['p3_lastO', h] = M['p3_omm', h, 3, 15]
        M['p3_done_v'] = SV.n
        M['p3_done_s'] = SS.n
        M['p3_done_t'] = ST.n

        regB.close()
        p_w = regC.enter_context(nc.psum_tensor("p_w", [128, 2048], F32))
        pt_f = [regC.enter_context(nc.psum_tensor(f"pt_f{p}", [128, 128], F32))
                for p in range(2)]

        def _gp4(e):
            e.wait_ge(ST.h, M['p3_lastS', HEADS - 1])
            for ic in range(NJ):
                M['p4_w', ic] = SG.inc(e.dma_start(
                    out=big1[:, ic * T:(ic + 1) * T],
                    in_=d_wo[ic * 128:(ic + 1) * 128, :]), 16)
        GP(_gp4)

        for step in range(NT + 1):
            if step >= 1:
                qtl = step - 1
                parl = qtl % 2

                def _pre_lntr(e, qtl=qtl):
                    e.wait_ge(SV.h, M['ln2', 'norm', qtl])
                    e.wait_ge(SS.h, M['p4o', 'ev', qtl * NJ + NJ - 1])

                def _pre_lnev(e, qtl=qtl):
                    prev = M.get(('p4_lnst', qtl - 2))
                    if prev is not None:
                        e.wait_ge(SX.h, prev)
                tr_ev(NJ,
                      (lambda j, parl=parl: lnstage[parl][:, j * 128:(j + 1) * 128]),
                      (lambda j, parl=parl: ln2Tbuf[parl][:, j * 128:(j + 1) * 128]),
                      pt_b, eye_b, 'p4l', 'vector', SV, qtl * NJ,
                      pre_tr=_pre_lntr, pre_ev=_pre_lnev)
                M['p4l', 'trdone', qtl] = ST.n
                M['p4_lnevd', qtl] = SV.n

                def _pre_x1ev(e, qtl=qtl):
                    prev = M.get(('p4_x1Tst', qtl - 2))
                    if prev is not None:
                        e.wait_ge(SX.h, prev)
                tr_ev(NJ,
                      (lambda j, parl=parl: x1buf[parl][:, j * 128:(j + 1) * 128]),
                      (lambda j, parl=parl: x1Tbuf[parl][:, j * 128:(j + 1) * 128]),
                      pt_f, eye_f, 'p4x1t', 'scalar', SS, qtl * NJ, pre_ev=_pre_x1ev)
                M['p4x1t', 'trdone', qtl] = ST.n
                M['p4_x1evd', qtl] = SS.n

                def _sy3(e, qtl=qtl, parl=parl):
                    e.wait_ge(SV.h, M['p4_lnevd', qtl])
                    M['p4_lnst', qtl] = SX.inc(e.dma_start(
                        out=d_ln2T[:, qtl * 128:(qtl + 1) * 128].rearrange(
                            "(jc p) t -> p jc t", p=128),
                        in_=ln2Tbuf[parl][:].rearrange("p (jc t) -> p jc t", t=128)), 16)
                    e.wait_ge(SS.h, M['p4_x1evd', qtl])
                    M['p4_x1Tst', qtl] = SX.inc(e.dma_start(
                        out=d_x1T[:, qtl * 128:(qtl + 1) * 128].rearrange(
                            "(jc p) t -> p jc t", p=128),
                        in_=x1Tbuf[parl][:].rearrange("p (jc t) -> p jc t", t=128)), 16)
                SY(_sy3)

            if step >= NT:
                continue
            qt = step
            par = qt % 2

            def _pre_otr(e, qt=qt):
                if qt == 0:
                    e.wait_ge(SS.h, M['p1_done'])
                    e.wait_ge(SV.h, M['p3_done_v'])
                else:
                    e.wait_ge(SV.h, M['p4l', 'ev', (qt - 1) * NJ + NJ - 1])
            tr_ev(NJ,
                  (lambda j, qt=qt: big2[:, qt * D + j * 128: qt * D + (j + 1) * 128]),
                  (lambda j, par=par: otst[par][:, j * 128:(j + 1) * 128]),
                  pt_b, eye_b, 'p4o', 'scalar', SS, qt * NJ, pre_tr=_pre_otr)

            def _sy1(e, qt=qt, par=par):
                prev = M.get(('p4_x1', qt - 2))
                if prev is not None:
                    e.wait_ge(SV.h, prev)
                elif qt < 2:
                    e.wait_ge(ST.h, M['p3_done_t'])
                M['p4_x', qt] = SX.inc(e.dma_start(
                    out=xstage[par][:], in_=d_x[qt * 128:(qt + 1) * 128, :]), 16)
            SY(_sy1)

            def _mm(e, qt=qt, par=par):
                e.wait_ge(SS.h, M['p4o', 'ev', qt * NJ + NJ - 1])
                if qt == 0:
                    e.wait_ge(SG.h, M['p4_w', NJ - 1])
                prev = M.get(('p4_x1', qt - 1))
                if prev is not None:
                    e.wait_ge(SV.h, prev)
                ins = None
                for ob in range(4):
                    for ic in range(NJ):
                        ins = e.matmul(
                            p_w[:, ob * 512:(ob + 1) * 512],
                            otst[par][:, ic * 128:(ic + 1) * 128],
                            big1[:, ic * T + ob * 512: ic * T + (ob + 1) * 512],
                            start=(ic == 0), stop=(ic == NJ - 1))
                M['p4_mm', qt] = ST.inc(ins)
            TE(_mm)

            def _vx1(e, qt=qt, par=par):
                e.wait_ge(ST.h, M['p4_mm', qt])
                e.wait_ge(SX.h, M['p4_x', qt])
                if qt >= 2:
                    e.wait_ge(SS.h, M['ln2', 'sq', qt - 2])
                    e.wait_ge(ST.h, M['p4x1t', 'trdone', qt - 2])
                    e.wait_ge(ST.h, M['p4l', 'trdone', qt - 2])
                M['p4_x1', qt] = SV.inc(e.tensor_tensor(
                    x1buf[par][:], p_w[:], xstage[par][:], ALU.add))
            VE(_vx1)

            emit_ln('ln2', x1buf[par], lnstage[par], qt,
                    pre_scalar=(lambda e, qt=qt: e.wait_ge(SV.h, M['p4_x1', qt])))
        M['p4_done_ss'] = SS.n
        M['p4_done_sv'] = SV.n
        M['p4_done_st'] = ST.n

        regC.close()
        ptb_stack.close()
        p_r = regD.enter_context(nc.psum_tensor("p_r", [128, 17], F32))
        pt_f5 = [regD.enter_context(nc.psum_tensor(f"pt_f5{p}", [128, 128], F32))
                 for p in range(2)]

        for qt in range(NT):
            def _sy(e, qt=qt):
                if qt == 0:
                    e.wait_ge(SS.h, M['p4_done_ss'])
                    e.wait_ge(SV.h, M['p4_done_sv'])
                    e.wait_ge(ST.h, M['p4_done_st'])
                else:
                    e.wait_ge(ST.h, M['p5_mm', qt - 1])
                M['p5_ln', qt] = SX.inc(e.dma_start(
                    out=routst[:].rearrange("p (jc t) -> p jc t", t=128),
                    in_=d_ln2T[:, qt * 128:(qt + 1) * 128].rearrange(
                        "(jc p) t -> p jc t", p=128)), 16)
            SY(_sy)

            def _mm(e, qt=qt):
                e.wait_ge(SX.h, M['p5_ln', qt])
                if qt == 0:
                    e.wait_ge(SG.h, M['wr'])
                else:
                    e.wait_ge(SS.h, M['p5_pr', qt - 1])
                ins = None
                for jc in range(NJ):
                    ins = e.matmul(p_r[:], routst[:, jc * 128:(jc + 1) * 128],
                                   wr17b[:, jc * 17:(jc + 1) * 17],
                                   start=(jc == 0), stop=(jc == NJ - 1))
                M['p5_mm', qt] = ST.inc(ins)
            TE(_mm)

            def _sc(e, qt=qt):
                e.wait_ge(ST.h, M['p5_mm', qt])
                prev = M.get(('p5_comb', qt - 1))
                if prev is not None:
                    e.wait_ge(SV.h, prev)
                e.activation(pe_r[:], p_r[:, 0:16], AF.Exp)
                M['p5_pr', qt] = SS.inc(e.activation(
                    sgcol[:, qt:qt + 1], p_r[:, 16:17], AF.Sigmoid))
            SC(_sc)

            def _v(e, qt=qt):
                e.wait_ge(SS.h, M['p5_pr', qt])
                m1 = SV.inc(e.reduce_sum(sv[:, 8:9], pe_r[:], axis=AX.X))
                e.wait_ge(SV.h, m1)
                m2 = SV.inc(e.reciprocal(sv[:, 9:10], sv[:, 8:9]))
                mx = SV.inc(e.max(pwork[:, 0:8], pe_r[:]))
                e.wait_ge(SV.h, mx)
                e.tensor_copy(pwork[:, 8:12], pwork[:, 0:4])
                mc = SV.inc(e.tensor_copy(pwork[:, 12:16], pwork[:, 0:4]))
                e.wait_ge(SV.h, mc)
                mr = SV.inc(e.match_replace(pmask[:], pwork[:, 8:16], pe_r[:], 0.0))
                e.wait_ge(SV.h, mr)
                e.tensor_tensor(cacc[:], pe_r[:], pmask[:], ALU.subtract)
                e.wait_ge(SV.h, m2)
                M['p5_comb', qt] = SV.inc(e.tensor_scalar_mul(
                    comb[:, qt * 16:(qt + 1) * 16], cacc[:], sv[:, 9:10]))
            VE(_v)

            def _ctr(e, qt=qt):
                e.wait_ge(SV.h, M['p5_comb', qt])
                prev = M.get(('p5_cev', qt - 2))
                if prev is not None:
                    e.wait_ge(SV.h, prev)
                M['p5_ctr', qt] = ST.inc(e.transpose(
                    pt_f5[qt % 2][0:16, :], comb[:, qt * 16:(qt + 1) * 16], eye_f[:]))
            TE(_ctr)

            def _cev(e, qt=qt):
                e.wait_ge(ST.h, M['p5_ctr', qt])
                M['p5_cev', qt] = SV.inc(e.tensor_copy(
                    combT[:, qt * 128:(qt + 1) * 128], pt_f5[qt % 2][0:16, :]))
            VE(_cev)

        def _str(e):
            e.wait_ge(SS.h, M['p5_pr', NT - 1])
            e.wait_ge(SV.h, M['p5_cev', NT - 2])
            M['p5_str'] = ST.inc(e.transpose(pt_f5[0][0:16, :], sgcol[:, 0:16], eye_f[:]))
        TE(_str)

        def _sgev(e):
            e.wait_ge(ST.h, M['p5_str'])
            M['p5_sgT'] = SV.inc(e.tensor_copy(sgT[:], pt_f5[0][0:16, :]))
        VE(_sgev)
        M['p5_done_v'] = SV.n
        M['p5_done_s'] = SS.n
        M['p5_done_t'] = ST.n

        regD.close()
        p_gu = [regE.enter_context(nc.psum_tensor(f"p_gu{p}", [128, 512], F32))
                for p in range(4)]
        p_y = [regE.enter_context(nc.psum_tensor(f"p_y{p}", [128, 512], F32))
               for p in range(2)]
        pt_f6 = [regE.enter_context(nc.psum_tensor(f"pt_f6{p}", [128, 128], F32))
                 for p in range(2)]

        prev_guh = None
        prev_yh = None
        prev_yev = None
        prev_silu_l = None
        prev_uev_l = None
        prev_aev_l = None
        prev_selread = None
        for Hh in range(2):
            tof = Hh * HALF

            def _sy(e, Hh=Hh, tof=tof):
                if Hh == 0:
                    e.wait_ge(SV.h, M['p5_done_v'])
                    e.wait_ge(SS.h, M['p5_done_s'])
                    e.wait_ge(ST.h, M['p5_done_t'])
                else:
                    e.wait_ge(ST.h, M['p6o', 'tr', 0, NTH * NJ - 1])
                M['p6_lnh', Hh] = SX.inc(e.dma_start(
                    out=ln2h[:].rearrange("p (jc t) -> p jc t", t=HALF),
                    in_=d_ln2T[:, tof:tof + HALF].rearrange(
                        "(jc p) t -> p jc t", p=128)), 16)
                M['p6_acc', Hh] = SX.inc(e.dma_start(
                    out=accT[:].rearrange("p (hc t) -> p hc t", t=HALF),
                    in_=d_x1T[:, tof:tof + HALF].rearrange(
                        "(hc p) t -> p hc t", p=128)), 16)
            SY(_sy)

            for q8 in range(NTH):
                qt = Hh * NTH + q8

                def _vsel(e, q8=q8, qt=qt, Hh=Hh, prev_selread=prev_selread):
                    if (Hh, q8) == (0, 0):
                        e.wait_ge(SV.h, M['p5_sgT'])
                    if prev_selread is not None:
                        e.wait_ge(ST.h, prev_selread)
                    M['p6_sel', Hh, q8] = SV.inc(e.tensor_scalar_mul(
                        sel_b[:], ones16[:], eye16[:, qt:qt + 1]))
                VE(_vsel)

                def _tsg(e, q8=q8, Hh=Hh, prev_silu_l=prev_silu_l,
                         prev_uev_l=prev_uev_l):
                    e.wait_ge(SV.h, M['p6_sel', Hh, q8])
                    prev = M.get(('p6_sgev', Hh, q8 - 1))
                    if prev is not None:
                        e.wait_ge(SV.h, prev)
                    elif Hh == 0:
                        e.wait_ge(SS.h, M['p5_done_s'])
                    else:
                        e.wait_ge(SS.h, prev_silu_l)
                        e.wait_ge(SV.h, prev_uev_l)
                    M['p6_sgmm', Hh, q8] = ST.inc(e.matmul(
                        p_gu[0][:, 0:128], sel_b[:], sgT[:], start=True, stop=True))
                TE(_tsg)
                prev_selread = M['p6_sgmm', Hh, q8]

                def _vsg(e, q8=q8, Hh=Hh):
                    e.wait_ge(ST.h, M['p6_sgmm', Hh, q8])
                    M['p6_sgev', Hh, q8] = SV.inc(e.tensor_copy(
                        sg_b[:, q8 * 128:(q8 + 1) * 128], p_gu[0][:, 0:128]))
                VE(_vsg)

            for u in range(E + NBAND):
                is_sh = u >= E
                bb = u - E

                if not is_sh:
                    def _vsel2(e, u=u, Hh=Hh, prev_selread=prev_selread):
                        e.wait_ge(ST.h, prev_selread)
                        M['p6_sel2', Hh, u] = SV.inc(e.tensor_scalar_mul(
                            sel_b[:], ones16[:], eye16[:, u:u + 1]))
                    VE(_vsel2)

                    def _tcb(e, u=u, Hh=Hh, tof=tof,
                             prev_silu_l=prev_silu_l, prev_uev_l=prev_uev_l):
                        e.wait_ge(SV.h, M['p6_sel2', Hh, u])
                        if prev_silu_l is not None:
                            e.wait_ge(SS.h, prev_silu_l)
                            e.wait_ge(SV.h, prev_uev_l)
                        else:
                            e.wait_ge(SV.h, M['p6_sgev', 0, NTH - 1])
                        e.matmul(p_gu[0][:], sel_b[:],
                                 combT[:, tof:tof + 512], start=True, stop=True)
                        ins = e.matmul(p_gu[1][:], sel_b[:],
                                       combT[:, tof + 512:tof + 1024],
                                       start=True, stop=True)
                        M['p6_cbmm', Hh, u] = ST.inc(ins)
                    TE(_tcb)
                    prev_selread = M['p6_cbmm', Hh, u]

                    def _vcb(e, u=u, Hh=Hh):
                        e.wait_ge(ST.h, M['p6_cbmm', Hh, u])
                        e.tensor_copy(cb_b[:, 0:512], p_gu[0][:])
                        M['p6_cbev', Hh, u] = SV.inc(
                            e.tensor_copy(cb_b[:, 512:1024], p_gu[1][:]))
                    VE(_vcb)

                NG = 2 * NI
                st_gu = ST.n
                ss_gu = SS.n
                sv_gu = SV.n

                def _gpw(e, u=u, Hh=Hh, is_sh=is_sh, bb=bb,
                         prev_guh=prev_guh, st_gu=st_gu):
                    if (Hh, u) == (0, 0):
                        e.wait_ge(ST.h, M['p5_done_t'])
                    for icc in range(NI):
                        gp = icc % 3
                        if icc >= 3:
                            e.wait_ge(ST.h, st_gu + 4 * (icc - 2))
                        elif prev_guh is not None:
                            e.wait_ge(ST.h, prev_guh)
                        if is_sh:
                            colg = d_wsg[:, (bb * NI + icc) * 128:(bb * NI + icc + 1) * 128]
                            colu = d_wsu[:, (bb * NI + icc) * 128:(bb * NI + icc + 1) * 128]
                        else:
                            colg = d_wg[u][:, icc * 128:(icc + 1) * 128]
                            colu = d_wu[u][:, icc * 128:(icc + 1) * 128]
                        SG.inc(e.dma_start(
                            out=wgc[gp][:].rearrange("p (jc i) -> p jc i", i=128),
                            in_=colg.rearrange("(jc p) i -> p jc i", p=128)), 16)
                        M['p6_w', Hh, u, icc] = SG.inc(e.dma_start(
                            out=wuc[gp][:].rearrange("p (jc i) -> p jc i", i=128),
                            in_=colu.rearrange("(jc p) i -> p jc i", p=128)), 16)
                GP(_gpw)

                def _tgu(e, u=u, Hh=Hh, is_sh=is_sh,
                         prev_silu_l=prev_silu_l, prev_uev_l=prev_uev_l,
                         st_gu=st_gu, ss_gu=ss_gu, sv_gu=sv_gu):
                    for k in range(NG):
                        icc, tg = k // 2, k % 2
                        gp = icc % 3
                        if tg == 0:
                            e.wait_ge(SG.h, M['p6_w', Hh, u, icc])
                        if k == 0:
                            e.wait_ge(SX.h, M['p6_lnh', Hh])
                        if k < 2:
                            if not is_sh:
                                e.wait_ge(SV.h, M['p6_cbev', Hh, u])
                            elif prev_silu_l is not None:
                                e.wait_ge(SS.h, prev_silu_l)
                                e.wait_ge(SV.h, prev_uev_l)
                        else:
                            e.wait_ge(SS.h, ss_gu + k - 1)
                            e.wait_ge(SV.h, sv_gu + 2 * k - 3)
                        ins = None
                        for jc in range(NJ):
                            ins = e.matmul(
                                p_gu[tg][:], wgc[gp][:, jc * 128:(jc + 1) * 128],
                                ln2h[:, jc * HALF + tg * 512: jc * HALF + (tg + 1) * 512],
                                start=(jc == 0), stop=(jc == NJ - 1))
                        M['p6_gmm', Hh, u, k] = ST.inc(ins)
                        ins = None
                        for jc in range(NJ):
                            ins = e.matmul(
                                p_gu[2 + tg][:], wuc[gp][:, jc * 128:(jc + 1) * 128],
                                ln2h[:, jc * HALF + tg * 512: jc * HALF + (tg + 1) * 512],
                                start=(jc == 0), stop=(jc == NJ - 1))
                        M['p6_umm', Hh, u, k] = ST.inc(ins)
                        if tg == 1:
                            M['p6_guh', Hh, u, icc] = ST.n
                            assert M['p6_guh', Hh, u, icc] == st_gu + 4 * (icc + 1)
                TE(_tgu)

                def _ssi(e, u=u, Hh=Hh, prev_aev_l=prev_aev_l,
                         ss_gu=ss_gu, sv_gu=sv_gu):
                    for k in range(NG):
                        tg = k % 2
                        e.wait_ge(ST.h, M['p6_gmm', Hh, u, k])
                        if k >= 2:
                            e.wait_ge(SV.h, sv_gu + 2 * k - 2)
                        elif prev_aev_l is not None:
                            e.wait_ge(SV.h, prev_aev_l)
                        M['p6_sgevac', Hh, u, k] = SS.inc(e.activation(
                            sgb_[:, tg * 512:(tg + 1) * 512], p_gu[tg][:], AF.Silu))
                        assert M['p6_sgevac', Hh, u, k] == ss_gu + k + 1
                SC(_ssi)

                def _vu(e, u=u, Hh=Hh, is_sh=is_sh, prev_yh=prev_yh, sv_gu=sv_gu):
                    gate_src = sg_b if is_sh else cb_b
                    for k in range(NG):
                        icc, tg = k // 2, k % 2
                        e.wait_ge(ST.h, M['p6_umm', Hh, u, k])
                        M['p6_uevac', Hh, u, k] = SV.inc(e.tensor_tensor(
                            ubuf[:, tg * 512:(tg + 1) * 512], p_gu[2 + tg][:],
                            gate_src[:, tg * 512:(tg + 1) * 512], ALU.mult))
                        e.wait_ge(SS.h, M['p6_sgevac', Hh, u, k])
                        if k == 0 and prev_yh is not None:
                            e.wait_ge(ST.h, prev_yh)
                        M['p6_aev', Hh, u, k] = SV.inc(e.tensor_tensor(
                            a_sb[:, icc * HALF + tg * 512: icc * HALF + (tg + 1) * 512],
                            sgb_[:, tg * 512:(tg + 1) * 512],
                            ubuf[:, tg * 512:(tg + 1) * 512], ALU.mult))
                        assert M['p6_aev', Hh, u, k] == sv_gu + 2 * k + 2
                VE(_vu)
                prev_guh = M['p6_guh', Hh, u, NI - 1]
                prev_silu_l = M['p6_sgevac', Hh, u, NG - 1]
                prev_uev_l = M['p6_uevac', Hh, u, NG - 1]
                prev_aev_l = M['p6_aev', Hh, u, NG - 1]
                M['p6_adone', Hh, u] = SV.n

                st_dn = ST.n
                sv_dn = SV.n

                def _gpd(e, u=u, Hh=Hh, is_sh=is_sh, bb=bb,
                         prev_yh=prev_yh, st_dn=st_dn):
                    for hc in range(NJ):
                        wp = hc % 3
                        if hc >= 3:
                            e.wait_ge(ST.h, st_dn + 2 * (hc - 2))
                        elif prev_yh is not None:
                            e.wait_ge(ST.h, prev_yh)
                        if is_sh:
                            blk = d_wsd[bb * I_:(bb + 1) * I_, hc * 128:(hc + 1) * 128]
                        else:
                            blk = d_wd[u][:, hc * 128:(hc + 1) * 128]
                        M['p6_wd', Hh, u, hc] = SG.inc(e.dma_start(
                            out=wdc[wp][:].rearrange("p (ic hh) -> p ic hh", hh=128),
                            in_=blk.rearrange("(ic p) hh -> p ic hh", p=128)), 16)
                GP(_gpd)

                def _tdn(e, u=u, Hh=Hh, prev_yev=prev_yev, st_dn=st_dn, sv_dn=sv_dn):
                    for hc in range(NJ):
                        wp = hc % 3
                        e.wait_ge(SG.h, M['p6_wd', Hh, u, hc])
                        if hc == 0:
                            e.wait_ge(SV.h, M['p6_adone', Hh, u])
                            e.wait_ge(SX.h, M['p6_acc', Hh])
                            if (Hh, u) == (0, 0):
                                e.wait_ge(SS.h, M['p5_done_s'])
                        ins = None
                        for tg in range(2):
                            y = hc * 2 + tg
                            if y >= 2:
                                e.wait_ge(SV.h, sv_dn + y - 1)
                            elif prev_yev is not None:
                                e.wait_ge(SV.h, prev_yev)
                            for icc in range(NI):
                                ins = e.matmul(
                                    p_y[y % 2][:], wdc[wp][:, icc * 128:(icc + 1) * 128],
                                    a_sb[:, icc * HALF + tg * 512: icc * HALF + (tg + 1) * 512],
                                    start=(icc == 0), stop=(icc == NI - 1))
                            M['p6_ymm', Hh, u, y] = ST.inc(ins)
                        M['p6_yh', Hh, u, hc] = ST.n
                        assert M['p6_yh', Hh, u, hc] == st_dn + 2 * (hc + 1)
                TE(_tdn)

                def _vy(e, u=u, Hh=Hh, sv_dn=sv_dn):
                    for hc in range(NJ):
                        for tg in range(2):
                            y = hc * 2 + tg
                            e.wait_ge(ST.h, M['p6_ymm', Hh, u, y])
                            dst = accT[:, hc * HALF + tg * 512: hc * HALF + (tg + 1) * 512]
                            M['p6_yev', Hh, u, y] = SV.inc(e.tensor_tensor(
                                dst, dst, p_y[y % 2][:], ALU.add))
                            assert M['p6_yev', Hh, u, y] == sv_dn + y + 1
                VE(_vy)
                prev_yh = M['p6_yh', Hh, u, NJ - 1]
                prev_yev = M['p6_yev', Hh, u, 2 * NJ - 1]

            for q8 in range(NTH):
                op = q8 % 2

                def _pre_otr(e, q8=q8, Hh=Hh, prev_yev=prev_yev):
                    if q8 == 0:
                        e.wait_ge(SV.h, prev_yev)
                    prev = M.get(('p6_obst', Hh, q8 - 2))
                    if prev is not None:
                        e.wait_ge(SO.h, prev)
                tr_ev(NJ,
                      (lambda j, q8=q8: accT[:, j * HALF + q8 * 128: j * HALF + (q8 + 1) * 128]),
                      (lambda j, op=op: outbuf[op][:, j * 128:(j + 1) * 128]),
                      pt_f6, eye_f, ('p6o', Hh), 'scalar', SS, q8 * NJ,
                      pre_tr=_pre_otr)
                M['p6_obuf', Hh, q8] = SS.n

                def _syo(e, q8=q8, op=op, Hh=Hh, tof=tof):
                    e.wait_ge(SS.h, M['p6_obuf', Hh, q8])
                    r0 = tof + q8 * 128
                    M['p6_obst', Hh, q8] = SO.inc(e.dma_start(
                        out=d_out[r0:r0 + 128, :], in_=outbuf[op][:]), 16)
                SY(_syo)
            M['p6o', 'tr', Hh, NTH * NJ - 1] = M[('p6o', Hh), 'tr', NTH * NJ - 1]

        def _fin(e):
            e.wait_ge(SO.h, SO.n)
        SY(_fin)

    return nc


_NC_CACHE = None


def _get_program():
    global _NC_CACHE
    if _NC_CACHE is None:
        _NC_CACHE = build_program()
    return _NC_CACHE


def kernel(L0_attn_query, L0_attn_kv_k, L0_attn_kv_v, L0_attn_Wq, L0_attn_Wo,
           L0_ffn_w_router, L0_ffn_w_gate, L0_ffn_w_up, L0_ffn_w_down,
           L0_ffn_ws_gate, L0_ffn_ws_up, L0_ffn_ws_down, L0_ffn_w_shared_gate):
    xq = np.ascontiguousarray(np.asarray(L0_attn_query, dtype=np.float32)).reshape(-1, D)
    wr17 = np.ascontiguousarray(
        np.concatenate([np.asarray(L0_ffn_w_router, dtype=np.float32),
                        np.asarray(L0_ffn_w_shared_gate, dtype=np.float32)], axis=1))
    common = {
        "kv_kT": np.ascontiguousarray(
            np.asarray(L0_attn_kv_k, dtype=np.float32).reshape(KV, D).T),
        "kv_v": np.ascontiguousarray(np.asarray(L0_attn_kv_v, dtype=np.float32)),
        "WqT": np.ascontiguousarray(np.asarray(L0_attn_Wq, dtype=np.float32).T),
        "WoT": np.ascontiguousarray(np.asarray(L0_attn_Wo, dtype=np.float32).T),
        "wr17": wr17,
        "w_gate": np.ascontiguousarray(np.asarray(L0_ffn_w_gate, dtype=np.float32)),
        "w_up": np.ascontiguousarray(np.asarray(L0_ffn_w_up, dtype=np.float32)),
        "w_down": np.ascontiguousarray(np.asarray(L0_ffn_w_down, dtype=np.float32)),
        "ws_gate": np.ascontiguousarray(np.asarray(L0_ffn_ws_gate, dtype=np.float32)),
        "ws_up": np.ascontiguousarray(np.asarray(L0_ffn_ws_up, dtype=np.float32)),
        "ws_down": np.ascontiguousarray(np.asarray(L0_ffn_ws_down, dtype=np.float32)),
        "eye_b": np.eye(128, dtype=np.float32).astype(ml_dtypes.bfloat16),
        "eye_f": np.eye(128, dtype=np.float32),
        "eye16": np.eye(16, dtype=np.float32),
    }
    in_maps = []
    for c in range(NCORES):
        m = dict(common)
        m["xq"] = np.ascontiguousarray(xq[c * T:(c + 1) * T, :])
        in_maps.append(m)
    nc = _get_program()
    res = run_bass_kernel_spmd(nc, in_maps, core_ids=list(range(NCORES)))
    out = np.concatenate([np.asarray(r["out"]) for r in res.results], axis=0)
    return out.reshape(4, 4096, 2048).astype(np.float32)


# version 11
# speedup vs baseline: 1.4633x; 1.0187x over previous
import math
from contextlib import ExitStack

import numpy as np
import ml_dtypes
import concourse.bass as bass
import concourse.mybir as mybir
from concourse.bass_utils import run_bass_kernel_spmd

F32 = mybir.dt.float32
BF16 = mybir.dt.bfloat16
AF = mybir.ActivationFunctionType
ALU = mybir.AluOpType
AX = mybir.AxisListType

NCORES = 8
T = 2048
D = 2048
NT = T // 128
NJ = D // 128
HEADS = 16
KV = 4096
NKB = KV // 128
E = 16
I_ = 1408
NI = I_ // 128
IS = 5632
NBAND = 4
SCALE = 1.0 / math.sqrt(128.0)
HALF = T // 2
NTH = HALF // 128


class Sem:
    def __init__(self, nc, name):
        self.h = nc.alloc_semaphore(name)
        self.n = 0

    def inc(self, instr, k=1):
        instr.then_inc(self.h, k)
        self.n += k
        return self.n


def build_program():
    nc = bass.Bass("TRN2", target_bir_lowering=False)

    d_x = nc.dram_tensor("xq", [T, D], F32, kind="ExternalInput").ap()
    d_kvk = nc.dram_tensor("kv_kT", [D, KV], F32, kind="ExternalInput").ap()
    d_kvv = nc.dram_tensor("kv_v", [NKB, 128, D], F32, kind="ExternalInput").ap()
    d_wq = nc.dram_tensor("WqT", [D, D], F32, kind="ExternalInput").ap()
    d_wo = nc.dram_tensor("WoT", [D, D], F32, kind="ExternalInput").ap()
    d_wr17 = nc.dram_tensor("wr17", [D, 17], F32, kind="ExternalInput").ap()
    d_wg = nc.dram_tensor("w_gate", [E, D, I_], F32, kind="ExternalInput").ap()
    d_wu = nc.dram_tensor("w_up", [E, D, I_], F32, kind="ExternalInput").ap()
    d_wd = nc.dram_tensor("w_down", [E, I_, D], F32, kind="ExternalInput").ap()
    d_wsg = nc.dram_tensor("ws_gate", [D, IS], F32, kind="ExternalInput").ap()
    d_wsu = nc.dram_tensor("ws_up", [D, IS], F32, kind="ExternalInput").ap()
    d_wsd = nc.dram_tensor("ws_down", [IS, D], F32, kind="ExternalInput").ap()
    d_eyeb = nc.dram_tensor("eye_b", [128, 128], BF16, kind="ExternalInput").ap()
    d_eyef = nc.dram_tensor("eye_f", [128, 128], F32, kind="ExternalInput").ap()
    d_eye16 = nc.dram_tensor("eye16", [16, 16], F32, kind="ExternalInput").ap()
    d_out = nc.dram_tensor("out", [T, D], F32, kind="ExternalOutput").ap()
    d_x1 = nc.dram_tensor("x1_scr", [T, D], F32, kind="Internal").ap()
    d_x1T = nc.dram_tensor("x1T_scr", [D, T], F32, kind="Internal").ap()
    d_ln2T = nc.dram_tensor("ln2T_scr", [D, T], BF16, kind="Internal").ap()

    A = nc.alloc_sbuf_tensor
    eye_b = A("s_eyeb", [128, 128], BF16)
    eye_f = A("s_eyef", [128, 128], F32)
    eye16 = A("eye16s", [16, 16], F32)
    ones16 = A("ones16", [16, 128], F32)
    sel_b = A("sel_b", [16, 128], F32)
    eps_t = A("eps_t", [128, 1], F32)
    wr17b = A("wr17b", [128, NJ * 17], BF16)
    comb = A("comb", [128, NT * 16], F32)
    combT = A("combT", [16, T], F32)
    sgcol = A("sgcol", [128, NT], F32)
    sgT = A("sgT", [16, 128], F32)
    sv = A("sv", [128, 32], F32)
    dinv = A("dinv", [128, 4], F32)
    pmask = A("pmask", [128, 16], F32)
    pwork = A("pwork", [128, 16], F32)
    cacc = A("cacc", [128, 16], F32)
    pe_r = A("pe_r", [128, 16], F32)

    KB = 1024
    MB = (nc.sbuf_base + 31) & ~31
    AT = nc.alloc_sbuf_tensor_at

    def at(name, shape, dtype, off):
        return AT(name, shape, dtype, offset=MB + off)

    big1 = at("big1", [128, NJ * T], BF16, 0)
    big0 = at("big0", [128, NJ * T], BF16, 64 * KB)
    big2 = at("big2", [128, NT * D], BF16, 64 * KB)
    xstage = [at(f"xstage{p}", [128, D], F32, 128 * KB + p * 8 * KB) for p in range(2)]
    lnstage = [at(f"lnstage{p}", [128, D], BF16, 144 * KB + p * 4 * KB) for p in range(2)]
    xsq = at("xsq", [128, D], F32, 152 * KB)
    wstage = [at(f"wstage{p}", [128, D], BF16, 160 * KB + p * 4 * KB) for p in range(2)]
    otst = [at(f"otst{p}", [128, NJ * 128], BF16, 168 * KB + p * 4 * KB) for p in range(2)]
    x1buf = [at(f"x1buf{p}", [128, D], F32, 176 * KB + p * 8 * KB) for p in range(2)]
    kstg = at("kstg", [128, NKB * 128], BF16, 128 * KB)
    ktb = [at(f"ktb{p}", [128, NKB * 128], BF16, 136 * KB + p * 8 * KB) for p in range(2)]
    vpb = [at(f"vpb{p}", [128, NKB * 132], BF16, 152 * KB + p * 8448) for p in range(2)]
    exps = [at(f"exps{p}", [128, 1024], BF16, 152 * KB + 16896 + p * 2048) for p in range(2)]
    x1Tbuf = [at(f"x1Tbuf{p}", [128, NJ * 128], F32, 128 * KB + p * 8 * KB) for p in range(2)]
    ln2Tbuf = [at(f"ln2Tbuf{p}", [128, NJ * 128], BF16, 160 * KB + p * 4 * KB) for p in range(2)]
    routst = [at(f"routst{p}", [128, NJ * 128], BF16, 168 * KB + p * 4 * KB)
              for p in range(2)]
    ln2h = at("ln2h", [128, NJ * HALF], BF16, 0)
    accT = at("accT", [128, NJ * HALF], F32, 32 * KB)
    a_sb = at("a_sb", [128, NI * HALF], BF16, 96 * KB)
    wgc = [at(f"wgc{p}", [128, NJ * 128], BF16, off)
           for p, off in enumerate((118 * KB, 122 * KB, 176 * KB))]
    wuc = [at(f"wuc{p}", [128, NJ * 128], BF16, off)
           for p, off in enumerate((126 * KB, 130 * KB, 180 * KB))]
    wdc = [at(f"wdc{p}", [128, NI * 128], BF16, off)
           for p, off in enumerate((134 * KB, 134 * KB + 2816, 184 * KB))]
    sgb_ = at("sgb", [128, 1024], F32, 140 * KB)
    ubuf = at("ubuf", [128, 1024], F32, 144 * KB)
    cb_b = at("cb_b", [128, HALF], F32, 148 * KB)
    sg_b = at("sg_b", [128, HALF], F32, 152 * KB)
    outbuf = [at(f"outbuf{p}", [128, D], F32, 156 * KB + p * 8 * KB) for p in range(2)]

    SX = Sem(nc, "SX")
    SG = Sem(nc, "SG")
    SV = Sem(nc, "SV")
    SS = Sem(nc, "SS")
    ST = Sem(nc, "ST")
    SO = Sem(nc, "SO")
    M = {}

    ptb_stack = ExitStack()
    regB = ExitStack()
    regC = ExitStack()
    regD = ExitStack()
    regE = ExitStack()
    pt_b = [ptb_stack.enter_context(nc.psum_tensor(f"pt_b{p}", [128, 128], BF16))
            for p in range(2)]
    p_s2 = [regB.enter_context(nc.psum_tensor(f"p_s2{p}", [128, 1024], F32))
            for p in range(2)]
    p_o = regB.enter_context(nc.psum_tensor("p_o", [128, 1024], F32))
    p_q = [p_o[:, 0:512], p_o[:, 512:1024]]

    with nc.Block() as block:
        TE, SC, VE, GP, SY = (block.tensor, block.scalar, block.vector,
                              block.gpsimd, block.sync)

        def tr_ev(n, tr_src, ev_dst, pt, eye, tag, ev_eng, ev_sem, gbase,
                  pre_tr=None, pre_ev=None):
            ev_base = ev_sem.n

            def _t(e):
                if pre_tr is not None:
                    pre_tr(e)
                for j in range(n):
                    g = gbase + j
                    if j < 2:
                        prev = M.get((tag, 'ev', g - 2))
                        if prev is not None:
                            e.wait_ge(ev_sem.h, prev)
                    else:
                        e.wait_ge(ev_sem.h, ev_base + (j - 2) + 1)
                    M[tag, 'tr', g] = ST.inc(e.transpose(pt[g % 2][:], tr_src(j), eye[:]))
            TE(_t)

            def _e(e):
                if pre_ev is not None:
                    pre_ev(e)
                for j in range(n):
                    g = gbase + j
                    e.wait_ge(ST.h, M[tag, 'tr', g])
                    if ev_eng == 'scalar':
                        ins = e.copy(ev_dst(j), pt[g % 2][:])
                    else:
                        ins = e.tensor_copy(ev_dst(j), pt[g % 2][:])
                    M[tag, 'ev', g] = ev_sem.inc(ins)
                    assert M[tag, 'ev', g] == ev_base + j + 1
            (SC if ev_eng == 'scalar' else VE)(_e)

        def emit_ln(tag, src, dst, tt, pre_scalar=None):
            def _s1(e):
                if pre_scalar is not None:
                    pre_scalar(e)
                prev = M.get((tag, 'var', tt - 1))
                if prev is not None:
                    e.wait_ge(SV.h, prev)
                M[tag, 'sq', tt] = SS.inc(e.activation(
                    xsq[:], src[:], AF.Square, accum_out=sv[:, 1:2]))
            SC(_s1)

            def _v1(e):
                m_red = SV.inc(e.reduce_sum(sv[:, 0:1], src[:], axis=AX.X))
                e.wait_ge(SV.h, m_red)
                e.wait_ge(SS.h, M[tag, 'sq', tt])
                e.tensor_tensor(sv[:, 2:3], sv[:, 0:1], sv[:, 0:1], ALU.mult)
                m_mv = SV.inc(e.tensor_scalar_mul(sv[:, 3:4], sv[:, 0:1], 1.0 / D))
                e.wait_ge(SV.h, m_mv)
                M[tag, 'var', tt] = SV.inc(e.scalar_tensor_tensor(
                    sv[:, 4:5], sv[:, 2:3], -1.0 / D, sv[:, 1:2], ALU.mult, ALU.add))
            VE(_v1)

            def _s2(e):
                e.wait_ge(SV.h, M[tag, 'var', tt])
                M[tag, 'sqrt', tt] = SS.inc(e.activation(
                    sv[:, 5:6], sv[:, 4:5], AF.Sqrt, scale=1.0 / D, bias=eps_t[:]))
            SC(_s2)

            def _v2(e):
                e.wait_ge(SS.h, M[tag, 'sqrt', tt])
                m_r = SV.inc(e.reciprocal(sv[:, 6:7], sv[:, 5:6]))
                e.wait_ge(SV.h, m_r)
                M[tag, 'norm', tt] = SV.inc(e.tensor_scalar(
                    dst[:], src[:], sv[:, 3:4], sv[:, 6:7], ALU.subtract, ALU.mult))
            VE(_v2)

        def _p0sy(e):
            SX.inc(e.dma_start(out=eye_b[:], in_=d_eyeb), 16)
            SX.inc(e.dma_start(out=eye_f[:], in_=d_eyef), 16)
            SX.inc(e.dma_start(out=eye16[:], in_=d_eye16), 16)
        SY(_p0sy)
        M['eyes'] = SX.n

        def _p0g(e):
            SG.inc(e.dma_start(
                out=wr17b[:].rearrange("p (jc q) -> p jc q", q=17),
                in_=d_wr17.rearrange("(jc p) q -> p jc q", p=128)), 16)
        GP(_p0g)
        M['wr'] = SG.n

        def _p0v(e):
            e.memset(eps_t[:], 1e-5)
            SV.inc(e.memset(ones16[:], 1.0))
        VE(_p0v)
        M['consts_v'] = SV.n

        for tt in range(NT):
            par = tt % 2

            def _sy(e, tt=tt, par=par):
                if tt >= 2:
                    e.wait_ge(SV.h, M['ln1', 'norm', tt - 2])
                    e.wait_ge(SS.h, M['ln1', 'sq', tt - 2])
                M['p1_x', tt] = SX.inc(e.dma_start(
                    out=xstage[par][:], in_=d_x[tt * 128:(tt + 1) * 128, :]), 16)
            SY(_sy)

            def _vw(e, tt=tt):
                e.wait_ge(SX.h, M['p1_x', tt])
                prev = M.get(('p1t', 'trdone', tt - 2))
                if prev is not None:
                    e.wait_ge(ST.h, prev)
            VE(_vw)

            emit_ln('ln1', xstage[par], lnstage[par], tt,
                    pre_scalar=(lambda e, tt=tt: e.wait_ge(SX.h, M['p1_x', tt])))

            def _pre_tr(e, tt=tt):
                if tt == 0:
                    e.wait_ge(SX.h, M['eyes'])
                e.wait_ge(SV.h, M['ln1', 'norm', tt])
            tr_ev(NJ,
                  (lambda j, par=par: lnstage[par][:, j * 128:(j + 1) * 128]),
                  (lambda j, tt=tt: big0[:, j * T + tt * 128: j * T + (tt + 1) * 128]),
                  pt_b, eye_b, 'p1tr', 'scalar', SS, tt * NJ, pre_tr=_pre_tr)
            M['p1t', 'trdone', tt] = ST.n
        M['p1_done'] = SS.n

        for ic in range(NJ):
            par = ic % 2

            def _gp(e, ic=ic, par=par):
                prev = M.get(('p2_mm_l', ic - 2))
                if prev is not None:
                    e.wait_ge(ST.h, prev)
                M['p2_w', ic] = SG.inc(e.dma_start(
                    out=otst[par][:].rearrange("p (jc m) -> p jc m", m=128),
                    in_=d_wq[:, ic * 128:(ic + 1) * 128].rearrange(
                        "(jc p) m -> p jc m", p=128)), 16)
            GP(_gp)

            qev_base = SS.n

            def _mm(e, ic=ic, par=par, qev_base=qev_base):
                e.wait_ge(SG.h, M['p2_w', ic])
                for qg in range(4):
                    if ic == 0:
                        e.wait_ge(SS.h, M['p1tr', 'ev', (qg * 4 + 4) * NJ - 1])
                    k = ic * 4 + qg
                    if qg < 2:
                        prev = M.get(('p2_qev', k - 2))
                        if prev is not None:
                            e.wait_ge(SS.h, prev)
                    else:
                        e.wait_ge(SS.h, qev_base + (qg - 2) + 1)
                    ins = None
                    for jc in range(NJ):
                        ins = e.matmul(
                            p_q[k % 2], otst[par][:, jc * 128:(jc + 1) * 128],
                            big0[:, jc * T + qg * 512: jc * T + (qg + 1) * 512],
                            start=(jc == 0), stop=(jc == NJ - 1))
                    M['p2_mm', k] = ST.inc(ins)
            TE(_mm)
            M['p2_mm_l', ic] = M['p2_mm', ic * 4 + 3]

            def _ev(e, ic=ic, qev_base=qev_base):
                for qg in range(4):
                    k = ic * 4 + qg
                    e.wait_ge(ST.h, M['p2_mm', k])
                    M['p2_qev', k] = SS.inc(e.activation(
                        big1[:, ic * T + qg * 512: ic * T + (qg + 1) * 512],
                        p_q[k % 2], AF.Copy, scale=SCALE))
                    assert M['p2_qev', k] == qev_base + qg + 1
            SC(_ev)
        M['p2_done'] = SS.n

        def _p3v0(e):
            e.wait_ge(SS.h, M['p2_done'])
            e.memset(vpb[0][:], 1.0)
            M['consts_v2'] = SV.inc(e.memset(vpb[1][:], 1.0))
        VE(_p3v0)

        prev_exp = [None, None]
        prev_omm = [None, None]
        prev_oev = None
        for h in range(HEADS):
            hp = h % 2

            def _gp(e, h=h, hp=hp):
                if h == 0:
                    e.wait_ge(SS.h, M['p2_done'])
                    e.wait_ge(SV.h, M['consts_v2'])
                if h >= 2:
                    e.wait_ge(ST.h, M['p3_lastS', h - 2])
                M['p3_kdone', h] = SG.inc(e.dma_start(
                    out=ktb[hp][:], in_=d_kvk[h * 128:(h + 1) * 128, :]), 16)
                if h >= 2:
                    e.wait_ge(ST.h, M['p3_lastO', h - 2])
                srcv = d_kvv[:, :, h * 128:(h + 1) * 128].rearrange(
                    "kb kk dd -> kk kb dd")
                dstv = vpb[hp][:].rearrange("p (kb c) -> p kb c", c=132)[:, :, 0:128]
                SG.inc(e.dma_start(out=dstv[:, 0:16], in_=srcv[:, 0:16]), 16)
                M['p3_v', h] = SG.inc(e.dma_start(out=dstv[:, 16:32], in_=srcv[:, 16:32]), 16)
            GP(_gp)

            for qg in range(4):
                qbase = h * T + qg * 512
                exp_base = SS.n

                def _t3(e, h=h, hp=hp, qg=qg, qbase=qbase, exp_base=exp_base,
                        prev_exp=prev_exp, prev_oev=prev_oev):
                    if (h, qg) == (0, 0):
                        e.wait_ge(SS.h, M['p2_done'])
                        e.wait_ge(SV.h, M['consts_v2'])
                    e.wait_ge(SG.h, M['p3_kdone', h])
                    e.wait_ge(SG.h, M['p3_v', h])

                    def do_omm(pp, start_pair):
                        e.wait_ge(SS.h, exp_base + pp + 1)
                        if pp == 0 and prev_oev is not None:
                            e.wait_ge(SV.h, prev_oev)
                        ins2 = None
                        for b in range(2):
                            kb = 2 * pp + b
                            for qs in range(4):
                                ins2 = e.matmul(
                                    p_o[:, qs * 256: qs * 256 + 129],
                                    exps[pp % 2][:, b * 512 + qs * 128: b * 512 + (qs + 1) * 128],
                                    vpb[hp][:, kb * 132: kb * 132 + 129],
                                    start=(pp == 0 and b == 0 and qs % 2 == 0),
                                    stop=(pp == 15 and b == 1))
                        M['p3_omm', h, qg, pp] = ST.inc(ins2)

                    for p in range(16):
                        if p >= 2:
                            e.wait_ge(SS.h, exp_base + (p - 2) + 1)
                        elif prev_exp[p] is not None:
                            e.wait_ge(SS.h, prev_exp[p])
                        ins = None
                        for b in range(2):
                            kb = 2 * p + b
                            ins = e.matmul(
                                p_s2[p % 2][:, b * 512:(b + 1) * 512],
                                ktb[hp][:, kb * 128:(kb + 1) * 128],
                                big1[:, qbase:qbase + 512],
                                start=True, stop=True)
                        M['p3_s', h, qg, p] = ST.inc(ins)
                        if p >= 1:
                            do_omm(p - 1, False)
                    do_omm(15, False)
                TE(_t3)

                def _s3(e, h=h, qg=qg, exp_base=exp_base, prev_omm=prev_omm):
                    for p in range(16):
                        e.wait_ge(ST.h, M['p3_s', h, qg, p])
                        if p >= 2:
                            e.wait_ge(ST.h, M['p3_omm', h, qg, p - 2])
                        elif prev_omm[p] is not None:
                            e.wait_ge(ST.h, prev_omm[p])
                        M['p3_exp', h, qg, p] = SS.inc(e.activation(
                            exps[p % 2][:], p_s2[p % 2][:], AF.Exp))
                        assert M['p3_exp', h, qg, p] == exp_base + p + 1
                SC(_s3)

                def _oev(e, h=h, qg=qg):
                    e.wait_ge(ST.h, M['p3_omm', h, qg, 15])
                    ins = None
                    for qs in range(4):
                        ins = e.reciprocal(dinv[:, qs:qs + 1],
                                           p_o[:, qs * 256 + 128: qs * 256 + 129])
                    m_r = SV.inc(ins)
                    e.wait_ge(SV.h, m_r)
                    ins = None
                    for qs in range(4):
                        qt = qg * 4 + qs
                        ins = e.tensor_scalar_mul(
                            big2[:, qt * D + h * 128: qt * D + (h + 1) * 128],
                            p_o[:, qs * 256: qs * 256 + 128], dinv[:, qs:qs + 1])
                    M['p3_oev', h, qg] = SV.inc(ins)
                VE(_oev)

                prev_exp = [M['p3_exp', h, qg, 14], M['p3_exp', h, qg, 15]]
                prev_omm = [M['p3_omm', h, qg, 14], M['p3_omm', h, qg, 15]]
                prev_oev = M['p3_oev', h, qg]
                if qg == 3:
                    M['p3_lastS', h] = M['p3_s', h, 3, 15]
                    M['p3_lastO', h] = M['p3_omm', h, 3, 15]
        M['p3_done_v'] = SV.n
        M['p3_done_s'] = SS.n
        M['p3_done_t'] = ST.n

        regB.close()
        p_w = regC.enter_context(nc.psum_tensor("p_w", [128, 2048], F32))
        pt_f = [regC.enter_context(nc.psum_tensor(f"pt_f{p}", [128, 128], F32))
                for p in range(2)]

        def _gp4(e):
            for ic in range(NJ):
                e.wait_ge(ST.h, M['p3_lastS', ic])
                M['p4_w', ic] = SG.inc(e.dma_start(
                    out=big1[:, ic * T:(ic + 1) * T],
                    in_=d_wo[ic * 128:(ic + 1) * 128, :]), 16)
        GP(_gp4)

        for step in range(NT + 1):
            if step >= 1:
                qtl = step - 1
                parl = qtl % 2

                def _pre_x1tr(e, qtl=qtl):
                    e.wait_ge(SV.h, M['p4_x1', qtl])

                def _pre_x1ev(e, qtl=qtl):
                    prev = M.get(('p4_x1Tst', qtl - 2))
                    if prev is not None:
                        e.wait_ge(SX.h, prev)
                tr_ev(NJ,
                      (lambda j, parl=parl: x1buf[parl][:, j * 128:(j + 1) * 128]),
                      (lambda j, parl=parl: x1Tbuf[parl][:, j * 128:(j + 1) * 128]),
                      pt_f, eye_f, 'p4x1t', 'scalar', SS, qtl * NJ,
                      pre_tr=_pre_x1tr, pre_ev=_pre_x1ev)
                M['p4x1t', 'trdone', qtl] = ST.n
                M['p4_x1evd', qtl] = SS.n

                def _pre_lntr(e, qtl=qtl):
                    e.wait_ge(SV.h, M['ln2', 'norm', qtl])
                    e.wait_ge(SS.h, M['p4o', 'ev', qtl * NJ + NJ - 1])

                def _pre_lnev(e, qtl=qtl):
                    prev = M.get(('p4_lnst', qtl - 2))
                    if prev is not None:
                        e.wait_ge(SX.h, prev)
                tr_ev(NJ,
                      (lambda j, parl=parl: lnstage[parl][:, j * 128:(j + 1) * 128]),
                      (lambda j, parl=parl: ln2Tbuf[parl][:, j * 128:(j + 1) * 128]),
                      pt_b, eye_b, 'p4l', 'vector', SV, qtl * NJ,
                      pre_tr=_pre_lntr, pre_ev=_pre_lnev)
                M['p4l', 'trdone', qtl] = ST.n
                M['p4_lnevd', qtl] = SV.n

                def _sy3(e, qtl=qtl, parl=parl):
                    e.wait_ge(SV.h, M['p4_lnevd', qtl])
                    M['p4_lnst', qtl] = SX.inc(e.dma_start(
                        out=d_ln2T[:, qtl * 128:(qtl + 1) * 128].rearrange(
                            "(jc p) t -> p jc t", p=128),
                        in_=ln2Tbuf[parl][:].rearrange("p (jc t) -> p jc t", t=128)), 16)
                    e.wait_ge(SS.h, M['p4_x1evd', qtl])
                    M['p4_x1Tst', qtl] = SX.inc(e.dma_start(
                        out=d_x1T[:, qtl * 128:(qtl + 1) * 128].rearrange(
                            "(jc p) t -> p jc t", p=128),
                        in_=x1Tbuf[parl][:].rearrange("p (jc t) -> p jc t", t=128)), 16)
                SY(_sy3)

            if step >= NT:
                continue
            qt = step
            par = qt % 2

            def _pre_otr(e, qt=qt):
                if qt == 0:
                    e.wait_ge(SS.h, M['p1_done'])
                    e.wait_ge(SV.h, M['p3_done_v'])
                else:
                    e.wait_ge(SV.h, M['p4l', 'ev', (qt - 1) * NJ + NJ - 1])
            tr_ev(NJ,
                  (lambda j, qt=qt: big2[:, qt * D + j * 128: qt * D + (j + 1) * 128]),
                  (lambda j, par=par: otst[par][:, j * 128:(j + 1) * 128]),
                  pt_b, eye_b, 'p4o', 'scalar', SS, qt * NJ, pre_tr=_pre_otr)

            def _sy1(e, qt=qt, par=par):
                prev = M.get(('p4_x1', qt - 2))
                if prev is not None:
                    e.wait_ge(SV.h, prev)
                elif qt < 2:
                    e.wait_ge(ST.h, M['p3_done_t'])
                M['p4_x', qt] = SX.inc(e.dma_start(
                    out=xstage[par][:], in_=d_x[qt * 128:(qt + 1) * 128, :]), 16)
            SY(_sy1)

            def _mm(e, qt=qt, par=par):
                e.wait_ge(SS.h, M['p4o', 'ev', qt * NJ + NJ - 1])
                if qt == 0:
                    e.wait_ge(SG.h, M['p4_w', NJ - 1])
                prev = M.get(('p4_x1', qt - 1))
                if prev is not None:
                    e.wait_ge(SV.h, prev)
                ins = None
                for ob in range(4):
                    for ic in range(NJ):
                        ins = e.matmul(
                            p_w[:, ob * 512:(ob + 1) * 512],
                            otst[par][:, ic * 128:(ic + 1) * 128],
                            big1[:, ic * T + ob * 512: ic * T + (ob + 1) * 512],
                            start=(ic == 0), stop=(ic == NJ - 1))
                M['p4_mm', qt] = ST.inc(ins)
            TE(_mm)

            def _vx1(e, qt=qt, par=par):
                e.wait_ge(ST.h, M['p4_mm', qt])
                e.wait_ge(SX.h, M['p4_x', qt])
                if qt >= 2:
                    e.wait_ge(SS.h, M['ln2', 'sq', qt - 2])
                    e.wait_ge(ST.h, M['p4x1t', 'trdone', qt - 2])
                    e.wait_ge(ST.h, M['p4l', 'trdone', qt - 2])
                M['p4_x1', qt] = SV.inc(e.tensor_tensor(
                    x1buf[par][:], p_w[:], xstage[par][:], ALU.add))
            VE(_vx1)

            emit_ln('ln2', x1buf[par], lnstage[par], qt,
                    pre_scalar=(lambda e, qt=qt: e.wait_ge(SV.h, M['p4_x1', qt])))
        M['p4_done_ss'] = SS.n
        M['p4_done_sv'] = SV.n
        M['p4_done_st'] = ST.n

        regC.close()
        ptb_stack.close()
        p_r = regD.enter_context(nc.psum_tensor("p_r", [128, 17], F32))
        pt_f5 = [regD.enter_context(nc.psum_tensor(f"pt_f5{p}", [128, 128], F32))
                 for p in range(2)]

        for qt in range(NT):
            def _sy(e, qt=qt):
                if qt == 0:
                    e.wait_ge(SS.h, M['p4_done_ss'])
                    e.wait_ge(SV.h, M['p4_done_sv'])
                    e.wait_ge(ST.h, M['p4_done_st'])
                elif qt >= 2:
                    e.wait_ge(ST.h, M['p5_mm', qt - 2])
                M['p5_ln', qt] = SX.inc(e.dma_start(
                    out=routst[qt % 2][:].rearrange("p (jc t) -> p jc t", t=128),
                    in_=d_ln2T[:, qt * 128:(qt + 1) * 128].rearrange(
                        "(jc p) t -> p jc t", p=128)), 16)
            SY(_sy)

            def _mm(e, qt=qt):
                e.wait_ge(SX.h, M['p5_ln', qt])
                if qt == 0:
                    e.wait_ge(SG.h, M['wr'])
                else:
                    e.wait_ge(SS.h, M['p5_pr', qt - 1])
                ins = None
                for jc in range(NJ):
                    ins = e.matmul(p_r[:], routst[qt % 2][:, jc * 128:(jc + 1) * 128],
                                   wr17b[:, jc * 17:(jc + 1) * 17],
                                   start=(jc == 0), stop=(jc == NJ - 1))
                M['p5_mm', qt] = ST.inc(ins)
            TE(_mm)

            def _sc(e, qt=qt):
                e.wait_ge(ST.h, M['p5_mm', qt])
                prev = M.get(('p5_comb', qt - 1))
                if prev is not None:
                    e.wait_ge(SV.h, prev)
                e.activation(pe_r[:], p_r[:, 0:16], AF.Exp)
                M['p5_pr', qt] = SS.inc(e.activation(
                    sgcol[:, qt:qt + 1], p_r[:, 16:17], AF.Sigmoid))
            SC(_sc)

            def _v(e, qt=qt):
                e.wait_ge(SS.h, M['p5_pr', qt])
                m1 = SV.inc(e.reduce_sum(sv[:, 8:9], pe_r[:], axis=AX.X))
                e.wait_ge(SV.h, m1)
                m2 = SV.inc(e.reciprocal(sv[:, 9:10], sv[:, 8:9]))
                mx = SV.inc(e.max(pwork[:, 0:8], pe_r[:]))
                e.wait_ge(SV.h, mx)
                e.tensor_copy(pwork[:, 8:12], pwork[:, 0:4])
                mc = SV.inc(e.tensor_copy(pwork[:, 12:16], pwork[:, 0:4]))
                e.wait_ge(SV.h, mc)
                mr = SV.inc(e.match_replace(pmask[:], pwork[:, 8:16], pe_r[:], 0.0))
                e.wait_ge(SV.h, mr)
                e.tensor_tensor(cacc[:], pe_r[:], pmask[:], ALU.subtract)
                e.wait_ge(SV.h, m2)
                M['p5_comb', qt] = SV.inc(e.tensor_scalar_mul(
                    comb[:, qt * 16:(qt + 1) * 16], cacc[:], sv[:, 9:10]))
            VE(_v)

            def _ctr(e, qt=qt):
                e.wait_ge(SV.h, M['p5_comb', qt])
                prev = M.get(('p5_cev', qt - 2))
                if prev is not None:
                    e.wait_ge(SV.h, prev)
                M['p5_ctr', qt] = ST.inc(e.transpose(
                    pt_f5[qt % 2][0:16, :], comb[:, qt * 16:(qt + 1) * 16], eye_f[:]))
            TE(_ctr)

            def _cev(e, qt=qt):
                e.wait_ge(ST.h, M['p5_ctr', qt])
                M['p5_cev', qt] = SV.inc(e.tensor_copy(
                    combT[:, qt * 128:(qt + 1) * 128], pt_f5[qt % 2][0:16, :]))
            VE(_cev)

        def _str(e):
            e.wait_ge(SS.h, M['p5_pr', NT - 1])
            e.wait_ge(SV.h, M['p5_cev', NT - 2])
            M['p5_str'] = ST.inc(e.transpose(pt_f5[0][0:16, :], sgcol[:, 0:16], eye_f[:]))
        TE(_str)

        def _sgev(e):
            e.wait_ge(ST.h, M['p5_str'])
            M['p5_sgT'] = SV.inc(e.tensor_copy(sgT[:], pt_f5[0][0:16, :]))
        VE(_sgev)
        M['p5_done_v'] = SV.n
        M['p5_done_s'] = SS.n
        M['p5_done_t'] = ST.n

        regD.close()
        p_gu = [regE.enter_context(nc.psum_tensor(f"p_gu{p}", [128, 512], F32))
                for p in range(4)]
        p_y = [regE.enter_context(nc.psum_tensor(f"p_y{p}", [128, 512], F32))
               for p in range(2)]
        pt_f6 = [regE.enter_context(nc.psum_tensor(f"pt_f6{p}", [128, 128], F32))
                 for p in range(2)]

        prev_guh = None
        prev_yh = None
        prev_yev = None
        prev_silu_l = None
        prev_uev_l = None
        prev_aev_l = None
        prev_selread = None
        for Hh in range(2):
            tof = Hh * HALF

            def _sy(e, Hh=Hh, tof=tof):
                if Hh == 0:
                    e.wait_ge(SV.h, M['p5_done_v'])
                    e.wait_ge(SS.h, M['p5_done_s'])
                    e.wait_ge(ST.h, M['p5_done_t'])
                else:
                    e.wait_ge(ST.h, M['p6o', 'tr', 0, NTH * NJ - 1])
                M['p6_lnh', Hh] = SX.inc(e.dma_start(
                    out=ln2h[:].rearrange("p (jc t) -> p jc t", t=HALF),
                    in_=d_ln2T[:, tof:tof + HALF].rearrange(
                        "(jc p) t -> p jc t", p=128)), 16)
                M['p6_acc', Hh] = SX.inc(e.dma_start(
                    out=accT[:].rearrange("p (hc t) -> p hc t", t=HALF),
                    in_=d_x1T[:, tof:tof + HALF].rearrange(
                        "(hc p) t -> p hc t", p=128)), 16)
            SY(_sy)

            for q8 in range(NTH):
                qt = Hh * NTH + q8

                def _vsel(e, q8=q8, qt=qt, Hh=Hh, prev_selread=prev_selread):
                    if (Hh, q8) == (0, 0):
                        e.wait_ge(SV.h, M['p5_sgT'])
                    if prev_selread is not None:
                        e.wait_ge(ST.h, prev_selread)
                    M['p6_sel', Hh, q8] = SV.inc(e.tensor_scalar_mul(
                        sel_b[:], ones16[:], eye16[:, qt:qt + 1]))
                VE(_vsel)

                def _tsg(e, q8=q8, Hh=Hh, prev_silu_l=prev_silu_l,
                         prev_uev_l=prev_uev_l):
                    e.wait_ge(SV.h, M['p6_sel', Hh, q8])
                    prev = M.get(('p6_sgev', Hh, q8 - 1))
                    if prev is not None:
                        e.wait_ge(SV.h, prev)
                    elif Hh == 0:
                        e.wait_ge(SS.h, M['p5_done_s'])
                    else:
                        e.wait_ge(SS.h, prev_silu_l)
                        e.wait_ge(SV.h, prev_uev_l)
                    M['p6_sgmm', Hh, q8] = ST.inc(e.matmul(
                        p_gu[0][:, 0:128], sel_b[:], sgT[:], start=True, stop=True))
                TE(_tsg)
                prev_selread = M['p6_sgmm', Hh, q8]

                def _vsg(e, q8=q8, Hh=Hh):
                    e.wait_ge(ST.h, M['p6_sgmm', Hh, q8])
                    M['p6_sgev', Hh, q8] = SV.inc(e.tensor_copy(
                        sg_b[:, q8 * 128:(q8 + 1) * 128], p_gu[0][:, 0:128]))
                VE(_vsg)

            for u in range(E + NBAND):
                is_sh = u >= E
                bb = u - E

                if not is_sh:
                    def _vsel2(e, u=u, Hh=Hh, prev_selread=prev_selread):
                        e.wait_ge(ST.h, prev_selread)
                        M['p6_sel2', Hh, u] = SV.inc(e.tensor_scalar_mul(
                            sel_b[:], ones16[:], eye16[:, u:u + 1]))
                    VE(_vsel2)

                    def _tcb(e, u=u, Hh=Hh, tof=tof,
                             prev_silu_l=prev_silu_l, prev_uev_l=prev_uev_l):
                        e.wait_ge(SV.h, M['p6_sel2', Hh, u])
                        if prev_silu_l is not None:
                            e.wait_ge(SS.h, prev_silu_l)
                            e.wait_ge(SV.h, prev_uev_l)
                        else:
                            e.wait_ge(SV.h, M['p6_sgev', 0, NTH - 1])
                        e.matmul(p_gu[0][:], sel_b[:],
                                 combT[:, tof:tof + 512], start=True, stop=True)
                        ins = e.matmul(p_gu[1][:], sel_b[:],
                                       combT[:, tof + 512:tof + 1024],
                                       start=True, stop=True)
                        M['p6_cbmm', Hh, u] = ST.inc(ins)
                    TE(_tcb)
                    prev_selread = M['p6_cbmm', Hh, u]

                    def _vcb(e, u=u, Hh=Hh):
                        e.wait_ge(ST.h, M['p6_cbmm', Hh, u])
                        e.tensor_copy(cb_b[:, 0:512], p_gu[0][:])
                        M['p6_cbev', Hh, u] = SV.inc(
                            e.tensor_copy(cb_b[:, 512:1024], p_gu[1][:]))
                    VE(_vcb)

                NG = 2 * NI
                st_gu = ST.n
                ss_gu = SS.n
                sv_gu = SV.n

                def _gpw(e, u=u, Hh=Hh, is_sh=is_sh, bb=bb,
                         prev_guh=prev_guh, st_gu=st_gu):
                    if (Hh, u) == (0, 0):
                        e.wait_ge(ST.h, M['p5_done_t'])
                    for icc in range(NI):
                        gp = icc % 3
                        if icc >= 3:
                            e.wait_ge(ST.h, st_gu + 4 * (icc - 2))
                        elif prev_guh is not None:
                            e.wait_ge(ST.h, prev_guh)
                        if is_sh:
                            colg = d_wsg[:, (bb * NI + icc) * 128:(bb * NI + icc + 1) * 128]
                            colu = d_wsu[:, (bb * NI + icc) * 128:(bb * NI + icc + 1) * 128]
                        else:
                            colg = d_wg[u][:, icc * 128:(icc + 1) * 128]
                            colu = d_wu[u][:, icc * 128:(icc + 1) * 128]
                        SG.inc(e.dma_start(
                            out=wgc[gp][:].rearrange("p (jc i) -> p jc i", i=128),
                            in_=colg.rearrange("(jc p) i -> p jc i", p=128)), 16)
                        M['p6_w', Hh, u, icc] = SG.inc(e.dma_start(
                            out=wuc[gp][:].rearrange("p (jc i) -> p jc i", i=128),
                            in_=colu.rearrange("(jc p) i -> p jc i", p=128)), 16)
                GP(_gpw)

                def _tgu(e, u=u, Hh=Hh, is_sh=is_sh,
                         prev_silu_l=prev_silu_l, prev_uev_l=prev_uev_l,
                         st_gu=st_gu, ss_gu=ss_gu, sv_gu=sv_gu):
                    for k in range(NG):
                        icc, tg = k // 2, k % 2
                        gp = icc % 3
                        if tg == 0:
                            e.wait_ge(SG.h, M['p6_w', Hh, u, icc])
                        if k == 0:
                            e.wait_ge(SX.h, M['p6_lnh', Hh])
                        if k < 2:
                            if not is_sh:
                                e.wait_ge(SV.h, M['p6_cbev', Hh, u])
                            elif prev_silu_l is not None:
                                e.wait_ge(SS.h, prev_silu_l)
                                e.wait_ge(SV.h, prev_uev_l)
                        else:
                            e.wait_ge(SS.h, ss_gu + k - 1)
                            e.wait_ge(SV.h, sv_gu + 2 * k - 3)
                        ins = None
                        for jc in range(NJ):
                            ins = e.matmul(
                                p_gu[tg][:], wgc[gp][:, jc * 128:(jc + 1) * 128],
                                ln2h[:, jc * HALF + tg * 512: jc * HALF + (tg + 1) * 512],
                                start=(jc == 0), stop=(jc == NJ - 1))
                        M['p6_gmm', Hh, u, k] = ST.inc(ins)
                        ins = None
                        for jc in range(NJ):
                            ins = e.matmul(
                                p_gu[2 + tg][:], wuc[gp][:, jc * 128:(jc + 1) * 128],
                                ln2h[:, jc * HALF + tg * 512: jc * HALF + (tg + 1) * 512],
                                start=(jc == 0), stop=(jc == NJ - 1))
                        M['p6_umm', Hh, u, k] = ST.inc(ins)
                        if tg == 1:
                            M['p6_guh', Hh, u, icc] = ST.n
                            assert M['p6_guh', Hh, u, icc] == st_gu + 4 * (icc + 1)
                TE(_tgu)

                def _ssi(e, u=u, Hh=Hh, prev_aev_l=prev_aev_l,
                         ss_gu=ss_gu, sv_gu=sv_gu):
                    for k in range(NG):
                        tg = k % 2
                        e.wait_ge(ST.h, M['p6_gmm', Hh, u, k])
                        if k >= 2:
                            e.wait_ge(SV.h, sv_gu + 2 * k - 2)
                        elif prev_aev_l is not None:
                            e.wait_ge(SV.h, prev_aev_l)
                        M['p6_sgevac', Hh, u, k] = SS.inc(e.activation(
                            sgb_[:, tg * 512:(tg + 1) * 512], p_gu[tg][:], AF.Silu))
                        assert M['p6_sgevac', Hh, u, k] == ss_gu + k + 1
                SC(_ssi)

                def _vu(e, u=u, Hh=Hh, is_sh=is_sh, prev_yh=prev_yh, sv_gu=sv_gu):
                    gate_src = sg_b if is_sh else cb_b
                    for k in range(NG):
                        icc, tg = k // 2, k % 2
                        e.wait_ge(ST.h, M['p6_umm', Hh, u, k])
                        M['p6_uevac', Hh, u, k] = SV.inc(e.tensor_tensor(
                            ubuf[:, tg * 512:(tg + 1) * 512], p_gu[2 + tg][:],
                            gate_src[:, tg * 512:(tg + 1) * 512], ALU.mult))
                        e.wait_ge(SS.h, M['p6_sgevac', Hh, u, k])
                        if k == 0 and prev_yh is not None:
                            e.wait_ge(ST.h, prev_yh)
                        M['p6_aev', Hh, u, k] = SV.inc(e.tensor_tensor(
                            a_sb[:, icc * HALF + tg * 512: icc * HALF + (tg + 1) * 512],
                            sgb_[:, tg * 512:(tg + 1) * 512],
                            ubuf[:, tg * 512:(tg + 1) * 512], ALU.mult))
                        assert M['p6_aev', Hh, u, k] == sv_gu + 2 * k + 2
                VE(_vu)
                prev_guh = M['p6_guh', Hh, u, NI - 1]
                prev_silu_l = M['p6_sgevac', Hh, u, NG - 1]
                prev_uev_l = M['p6_uevac', Hh, u, NG - 1]
                prev_aev_l = M['p6_aev', Hh, u, NG - 1]
                M['p6_adone', Hh, u] = SV.n

                st_dn = ST.n
                sv_dn = SV.n

                def _gpd(e, u=u, Hh=Hh, is_sh=is_sh, bb=bb,
                         prev_yh=prev_yh, st_dn=st_dn):
                    for hc in range(NJ):
                        wp = hc % 3
                        if hc >= 3:
                            e.wait_ge(ST.h, st_dn + 2 * (hc - 2))
                        elif prev_yh is not None:
                            e.wait_ge(ST.h, prev_yh)
                        if is_sh:
                            blk = d_wsd[bb * I_:(bb + 1) * I_, hc * 128:(hc + 1) * 128]
                        else:
                            blk = d_wd[u][:, hc * 128:(hc + 1) * 128]
                        M['p6_wd', Hh, u, hc] = SG.inc(e.dma_start(
                            out=wdc[wp][:].rearrange("p (ic hh) -> p ic hh", hh=128),
                            in_=blk.rearrange("(ic p) hh -> p ic hh", p=128)), 16)
                GP(_gpd)

                def _tdn(e, u=u, Hh=Hh, prev_yev=prev_yev, st_dn=st_dn, sv_dn=sv_dn):
                    for hc in range(NJ):
                        wp = hc % 3
                        e.wait_ge(SG.h, M['p6_wd', Hh, u, hc])
                        if hc == 0:
                            e.wait_ge(SV.h, M['p6_adone', Hh, u])
                            e.wait_ge(SX.h, M['p6_acc', Hh])
                            if (Hh, u) == (0, 0):
                                e.wait_ge(SS.h, M['p5_done_s'])
                        ins = None
                        for tg in range(2):
                            y = hc * 2 + tg
                            if y >= 2:
                                e.wait_ge(SV.h, sv_dn + y - 1)
                            elif prev_yev is not None:
                                e.wait_ge(SV.h, prev_yev)
                            for icc in range(NI):
                                ins = e.matmul(
                                    p_y[y % 2][:], wdc[wp][:, icc * 128:(icc + 1) * 128],
                                    a_sb[:, icc * HALF + tg * 512: icc * HALF + (tg + 1) * 512],
                                    start=(icc == 0), stop=(icc == NI - 1))
                            M['p6_ymm', Hh, u, y] = ST.inc(ins)
                        M['p6_yh', Hh, u, hc] = ST.n
                        assert M['p6_yh', Hh, u, hc] == st_dn + 2 * (hc + 1)
                TE(_tdn)

                def _vy(e, u=u, Hh=Hh, sv_dn=sv_dn):
                    for hc in range(NJ):
                        for tg in range(2):
                            y = hc * 2 + tg
                            e.wait_ge(ST.h, M['p6_ymm', Hh, u, y])
                            dst = accT[:, hc * HALF + tg * 512: hc * HALF + (tg + 1) * 512]
                            M['p6_yev', Hh, u, y] = SV.inc(e.tensor_tensor(
                                dst, dst, p_y[y % 2][:], ALU.add))
                            assert M['p6_yev', Hh, u, y] == sv_dn + y + 1
                VE(_vy)
                prev_yh = M['p6_yh', Hh, u, NJ - 1]
                prev_yev = M['p6_yev', Hh, u, 2 * NJ - 1]

            for q8 in range(NTH):
                op = q8 % 2

                def _pre_otr(e, q8=q8, Hh=Hh, prev_yev=prev_yev):
                    if q8 == 0:
                        e.wait_ge(SV.h, prev_yev)
                    prev = M.get(('p6_obst', Hh, q8 - 2))
                    if prev is not None:
                        e.wait_ge(SO.h, prev)
                tr_ev(NJ,
                      (lambda j, q8=q8: accT[:, j * HALF + q8 * 128: j * HALF + (q8 + 1) * 128]),
                      (lambda j, op=op: outbuf[op][:, j * 128:(j + 1) * 128]),
                      pt_f6, eye_f, ('p6o', Hh), 'scalar', SS, q8 * NJ,
                      pre_tr=_pre_otr)
                M['p6_obuf', Hh, q8] = SS.n

                def _syo(e, q8=q8, op=op, Hh=Hh, tof=tof):
                    e.wait_ge(SS.h, M['p6_obuf', Hh, q8])
                    r0 = tof + q8 * 128
                    M['p6_obst', Hh, q8] = SO.inc(e.dma_start(
                        out=d_out[r0:r0 + 128, :], in_=outbuf[op][:]), 16)
                SY(_syo)
            M['p6o', 'tr', Hh, NTH * NJ - 1] = M[('p6o', Hh), 'tr', NTH * NJ - 1]

        def _fin(e):
            e.wait_ge(SO.h, SO.n)
        SY(_fin)

    return nc


_NC_CACHE = None


def _get_program():
    global _NC_CACHE
    if _NC_CACHE is None:
        _NC_CACHE = build_program()
    return _NC_CACHE


def kernel(L0_attn_query, L0_attn_kv_k, L0_attn_kv_v, L0_attn_Wq, L0_attn_Wo,
           L0_ffn_w_router, L0_ffn_w_gate, L0_ffn_w_up, L0_ffn_w_down,
           L0_ffn_ws_gate, L0_ffn_ws_up, L0_ffn_ws_down, L0_ffn_w_shared_gate):
    xq = np.ascontiguousarray(np.asarray(L0_attn_query, dtype=np.float32)).reshape(-1, D)
    wr17 = np.ascontiguousarray(
        np.concatenate([np.asarray(L0_ffn_w_router, dtype=np.float32),
                        np.asarray(L0_ffn_w_shared_gate, dtype=np.float32)], axis=1))
    common = {
        "kv_kT": np.ascontiguousarray(
            np.asarray(L0_attn_kv_k, dtype=np.float32).reshape(KV, D).T),
        "kv_v": np.ascontiguousarray(np.asarray(L0_attn_kv_v, dtype=np.float32)),
        "WqT": np.ascontiguousarray(np.asarray(L0_attn_Wq, dtype=np.float32).T),
        "WoT": np.ascontiguousarray(np.asarray(L0_attn_Wo, dtype=np.float32).T),
        "wr17": wr17,
        "w_gate": np.ascontiguousarray(np.asarray(L0_ffn_w_gate, dtype=np.float32)),
        "w_up": np.ascontiguousarray(np.asarray(L0_ffn_w_up, dtype=np.float32)),
        "w_down": np.ascontiguousarray(np.asarray(L0_ffn_w_down, dtype=np.float32)),
        "ws_gate": np.ascontiguousarray(np.asarray(L0_ffn_ws_gate, dtype=np.float32)),
        "ws_up": np.ascontiguousarray(np.asarray(L0_ffn_ws_up, dtype=np.float32)),
        "ws_down": np.ascontiguousarray(np.asarray(L0_ffn_ws_down, dtype=np.float32)),
        "eye_b": np.eye(128, dtype=np.float32).astype(ml_dtypes.bfloat16),
        "eye_f": np.eye(128, dtype=np.float32),
        "eye16": np.eye(16, dtype=np.float32),
    }
    in_maps = []
    for c in range(NCORES):
        m = dict(common)
        m["xq"] = np.ascontiguousarray(xq[c * T:(c + 1) * T, :])
        in_maps.append(m)
    nc = _get_program()
    res = run_bass_kernel_spmd(nc, in_maps, core_ids=list(range(NCORES)))
    out = np.concatenate([np.asarray(r["out"]) for r in res.results], axis=0)
    return out.reshape(4, 4096, 2048).astype(np.float32)


# version 12 (final)
# speedup vs baseline: 1.4796x; 1.0111x over previous
import math
from contextlib import ExitStack

import numpy as np
import ml_dtypes
import concourse.bass as bass
import concourse.mybir as mybir
from concourse.bass_utils import run_bass_kernel_spmd

F32 = mybir.dt.float32
BF16 = mybir.dt.bfloat16
AF = mybir.ActivationFunctionType
ALU = mybir.AluOpType
AX = mybir.AxisListType

NCORES = 8
T = 2048
D = 2048
NT = T // 128
NJ = D // 128
HEADS = 16
KV = 4096
NKB = KV // 128
E = 16
I_ = 1408
NI = I_ // 128
IS = 5632
NBAND = 4
SCALE = 1.0 / math.sqrt(128.0)
HALF = T // 2
NTH = HALF // 128


class Sem:
    def __init__(self, nc, name):
        self.h = nc.alloc_semaphore(name)
        self.n = 0

    def inc(self, instr, k=1):
        instr.then_inc(self.h, k)
        self.n += k
        return self.n


def build_program():
    nc = bass.Bass("TRN2", target_bir_lowering=False)

    d_x = nc.dram_tensor("xq", [T, D], F32, kind="ExternalInput").ap()
    d_kvk = nc.dram_tensor("kv_kT", [D, KV], F32, kind="ExternalInput").ap()
    d_kvv = nc.dram_tensor("kv_v", [NKB, 128, D], F32, kind="ExternalInput").ap()
    d_wq = nc.dram_tensor("WqT", [D, D], F32, kind="ExternalInput").ap()
    d_wo = nc.dram_tensor("WoT", [D, D], F32, kind="ExternalInput").ap()
    d_wr17 = nc.dram_tensor("wr17", [D, 17], F32, kind="ExternalInput").ap()
    d_wg = nc.dram_tensor("w_gate", [E, D, I_], F32, kind="ExternalInput").ap()
    d_wu = nc.dram_tensor("w_up", [E, D, I_], F32, kind="ExternalInput").ap()
    d_wd = nc.dram_tensor("w_down", [E, I_, D], F32, kind="ExternalInput").ap()
    d_wsg = nc.dram_tensor("ws_gate", [D, IS], F32, kind="ExternalInput").ap()
    d_wsu = nc.dram_tensor("ws_up", [D, IS], F32, kind="ExternalInput").ap()
    d_wsd = nc.dram_tensor("ws_down", [IS, D], F32, kind="ExternalInput").ap()
    d_eyeb = nc.dram_tensor("eye_b", [128, 128], BF16, kind="ExternalInput").ap()
    d_eyef = nc.dram_tensor("eye_f", [128, 128], F32, kind="ExternalInput").ap()
    d_eye16 = nc.dram_tensor("eye16", [16, 16], F32, kind="ExternalInput").ap()
    d_out = nc.dram_tensor("out", [T, D], F32, kind="ExternalOutput").ap()
    d_x1 = nc.dram_tensor("x1_scr", [T, D], F32, kind="Internal").ap()
    d_x1T = nc.dram_tensor("x1T_scr", [D, T], F32, kind="Internal").ap()
    d_ln2T = nc.dram_tensor("ln2T_scr", [D, T], BF16, kind="Internal").ap()

    A = nc.alloc_sbuf_tensor
    eye_b = A("s_eyeb", [128, 128], BF16)
    eye_f = A("s_eyef", [128, 128], F32)
    eye16 = A("eye16s", [16, 16], F32)
    ones16 = A("ones16", [16, 128], F32)
    sel_b = A("sel_b", [16, 128], F32)
    eps_t = A("eps_t", [128, 1], F32)
    wr17b = A("wr17b", [128, NJ * 17], BF16)
    comb = A("comb", [128, NT * 16], F32)
    combT = A("combT", [16, T], F32)
    sgcol = A("sgcol", [128, NT], F32)
    sgT = A("sgT", [16, 128], F32)
    sv = A("sv", [128, 32], F32)
    dinv = A("dinv", [128, 4], F32)
    pmask = A("pmask", [128, 16], F32)
    pwork = A("pwork", [128, 16], F32)
    cacc = A("cacc", [128, 16], F32)
    pe_r = A("pe_r", [128, 16], F32)

    KB = 1024
    MB = (nc.sbuf_base + 31) & ~31
    AT = nc.alloc_sbuf_tensor_at

    def at(name, shape, dtype, off):
        return AT(name, shape, dtype, offset=MB + off)

    big1 = at("big1", [128, NJ * T], BF16, 0)
    big0 = at("big0", [128, NJ * T], BF16, 64 * KB)
    big2 = at("big2", [128, NT * D], BF16, 64 * KB)
    xstage = [at(f"xstage{p}", [128, D], F32, 128 * KB + p * 8 * KB) for p in range(2)]
    lnstage = [at(f"lnstage{p}", [128, D], BF16, 144 * KB + p * 4 * KB) for p in range(2)]
    xsq = at("xsq", [128, D], F32, 152 * KB)
    wstage = [at(f"wstage{p}", [128, D], BF16, 160 * KB + p * 4 * KB) for p in range(2)]
    otst = [at(f"otst{p}", [128, NJ * 128], BF16, 168 * KB + p * 4 * KB) for p in range(2)]
    x1buf = [at(f"x1buf{p}", [128, D], F32, 176 * KB + p * 8 * KB) for p in range(2)]
    kstg = at("kstg", [128, NKB * 128], BF16, 128 * KB)
    ktb = [at(f"ktb{p}", [128, NKB * 128], BF16, 136 * KB + p * 8 * KB) for p in range(2)]
    vpb = [at(f"vpb{p}", [128, NKB * 132], BF16, 152 * KB + p * 8448) for p in range(2)]
    exps = [at(f"exps{p}", [128, 1024], BF16, 152 * KB + 16896 + p * 2048) for p in range(2)]
    x1Tbuf = [at(f"x1Tbuf{p}", [128, NJ * 128], F32, 128 * KB + p * 8 * KB) for p in range(2)]
    ln2Tbuf = [at(f"ln2Tbuf{p}", [128, NJ * 128], BF16, 160 * KB + p * 4 * KB) for p in range(2)]
    routst = [at(f"routst{p}", [128, NJ * 128], BF16, 168 * KB + p * 4 * KB)
              for p in range(2)]
    ln2h = at("ln2h", [128, NJ * HALF], BF16, 0)
    accT = at("accT", [128, NJ * HALF], F32, 32 * KB)
    a_sb = at("a_sb", [128, NI * HALF], BF16, 96 * KB)
    wgc = [at(f"wgc{p}", [128, NJ * 128], BF16, off)
           for p, off in enumerate((118 * KB, 122 * KB, 176 * KB))]
    wuc = [at(f"wuc{p}", [128, NJ * 128], BF16, off)
           for p, off in enumerate((126 * KB, 130 * KB, 180 * KB))]
    wdc = [at(f"wdc{p}", [128, NI * 128], BF16, off)
           for p, off in enumerate((134 * KB, 134 * KB + 2816, 184 * KB))]
    sgb_ = at("sgb", [128, 1024], F32, 140 * KB)
    ubuf = at("ubuf", [128, 1024], F32, 144 * KB)
    cb_b = at("cb_b", [128, HALF], F32, 148 * KB)
    sg_b = at("sg_b", [128, HALF], F32, 152 * KB)
    outbuf = [at(f"outbuf{p}", [128, D], F32, 156 * KB + p * 8 * KB) for p in range(2)]

    SX = Sem(nc, "SX")
    SG = Sem(nc, "SG")
    SV = Sem(nc, "SV")
    SS = Sem(nc, "SS")
    ST = Sem(nc, "ST")
    SO = Sem(nc, "SO")
    M = {}

    ptb_stack = ExitStack()
    regB = ExitStack()
    regC = ExitStack()
    regD = ExitStack()
    regE = ExitStack()
    pt_b = [ptb_stack.enter_context(nc.psum_tensor(f"pt_b{p}", [128, 128], BF16))
            for p in range(2)]
    p_s2 = [regB.enter_context(nc.psum_tensor(f"p_s2{p}", [128, 1024], F32))
            for p in range(2)]
    p_o = regB.enter_context(nc.psum_tensor("p_o", [128, 1024], F32))
    p_q = [p_o[:, 0:512], p_o[:, 512:1024]]

    with nc.Block() as block:
        TE, SC, VE, GP, SY = (block.tensor, block.scalar, block.vector,
                              block.gpsimd, block.sync)

        def tr_ev(n, tr_src, ev_dst, pt, eye, tag, ev_eng, ev_sem, gbase,
                  pre_tr=None, pre_ev=None):
            ev_base = ev_sem.n

            def _t(e):
                if pre_tr is not None:
                    pre_tr(e)
                for j in range(n):
                    g = gbase + j
                    if j < 2:
                        prev = M.get((tag, 'ev', g - 2))
                        if prev is not None:
                            e.wait_ge(ev_sem.h, prev)
                    else:
                        e.wait_ge(ev_sem.h, ev_base + (j - 2) + 1)
                    M[tag, 'tr', g] = ST.inc(e.transpose(pt[g % 2][:], tr_src(j), eye[:]))
            TE(_t)

            def _e(e):
                if pre_ev is not None:
                    pre_ev(e)
                for j in range(n):
                    g = gbase + j
                    e.wait_ge(ST.h, M[tag, 'tr', g])
                    if ev_eng == 'scalar':
                        ins = e.copy(ev_dst(j), pt[g % 2][:])
                    else:
                        ins = e.tensor_copy(ev_dst(j), pt[g % 2][:])
                    M[tag, 'ev', g] = ev_sem.inc(ins)
                    assert M[tag, 'ev', g] == ev_base + j + 1
            (SC if ev_eng == 'scalar' else VE)(_e)

        def emit_ln(tag, src, dst, tt, pre_scalar=None):
            def _s1(e):
                if pre_scalar is not None:
                    pre_scalar(e)
                prev = M.get((tag, 'var', tt - 1))
                if prev is not None:
                    e.wait_ge(SV.h, prev)
                M[tag, 'sq', tt] = SS.inc(e.activation(
                    xsq[:], src[:], AF.Square, accum_out=sv[:, 1:2]))
            SC(_s1)

            def _v1(e):
                m_red = SV.inc(e.reduce_sum(sv[:, 0:1], src[:], axis=AX.X))
                e.wait_ge(SV.h, m_red)
                e.wait_ge(SS.h, M[tag, 'sq', tt])
                e.tensor_tensor(sv[:, 2:3], sv[:, 0:1], sv[:, 0:1], ALU.mult)
                m_mv = SV.inc(e.tensor_scalar_mul(sv[:, 3:4], sv[:, 0:1], 1.0 / D))
                e.wait_ge(SV.h, m_mv)
                M[tag, 'var', tt] = SV.inc(e.scalar_tensor_tensor(
                    sv[:, 4:5], sv[:, 2:3], -1.0 / D, sv[:, 1:2], ALU.mult, ALU.add))
            VE(_v1)

            def _s2(e):
                e.wait_ge(SV.h, M[tag, 'var', tt])
                M[tag, 'sqrt', tt] = SS.inc(e.activation(
                    sv[:, 5:6], sv[:, 4:5], AF.Sqrt, scale=1.0 / D, bias=eps_t[:]))
            SC(_s2)

            def _v2(e):
                e.wait_ge(SS.h, M[tag, 'sqrt', tt])
                m_r = SV.inc(e.reciprocal(sv[:, 6:7], sv[:, 5:6]))
                e.wait_ge(SV.h, m_r)
                M[tag, 'norm', tt] = SV.inc(e.tensor_scalar(
                    dst[:], src[:], sv[:, 3:4], sv[:, 6:7], ALU.subtract, ALU.mult))
            VE(_v2)

        def _p0sy(e):
            SX.inc(e.dma_start(out=eye_b[:], in_=d_eyeb), 16)
            SX.inc(e.dma_start(out=eye_f[:], in_=d_eyef), 16)
            SX.inc(e.dma_start(out=eye16[:], in_=d_eye16), 16)
        SY(_p0sy)
        M['eyes'] = SX.n

        def _p0g(e):
            SG.inc(e.dma_start(
                out=wr17b[:].rearrange("p (jc q) -> p jc q", q=17),
                in_=d_wr17.rearrange("(jc p) q -> p jc q", p=128)), 16)
        GP(_p0g)
        M['wr'] = SG.n

        def _p0v(e):
            e.memset(eps_t[:], 1e-5)
            SV.inc(e.memset(ones16[:], 1.0))
        VE(_p0v)
        M['consts_v'] = SV.n

        for tt in range(NT):
            par = tt % 2

            def _sy(e, tt=tt, par=par):
                if tt >= 2:
                    e.wait_ge(SV.h, M['ln1', 'norm', tt - 2])
                    e.wait_ge(SS.h, M['ln1', 'sq', tt - 2])
                M['p1_x', tt] = SX.inc(e.dma_start(
                    out=xstage[par][:], in_=d_x[tt * 128:(tt + 1) * 128, :]), 16)
            SY(_sy)

            def _vw(e, tt=tt):
                e.wait_ge(SX.h, M['p1_x', tt])
                prev = M.get(('p1t', 'trdone', tt - 2))
                if prev is not None:
                    e.wait_ge(ST.h, prev)
            VE(_vw)

            emit_ln('ln1', xstage[par], lnstage[par], tt,
                    pre_scalar=(lambda e, tt=tt: e.wait_ge(SX.h, M['p1_x', tt])))

            def _pre_tr(e, tt=tt):
                if tt == 0:
                    e.wait_ge(SX.h, M['eyes'])
                e.wait_ge(SV.h, M['ln1', 'norm', tt])
            tr_ev(NJ,
                  (lambda j, par=par: lnstage[par][:, j * 128:(j + 1) * 128]),
                  (lambda j, tt=tt: big0[:, j * T + tt * 128: j * T + (tt + 1) * 128]),
                  pt_b, eye_b, 'p1tr', 'scalar', SS, tt * NJ, pre_tr=_pre_tr)
            M['p1t', 'trdone', tt] = ST.n
        M['p1_done'] = SS.n

        for ic in range(NJ):
            par = ic % 2

            def _gp(e, ic=ic, par=par):
                prev = M.get(('p2_mm_l', ic - 2))
                if prev is not None:
                    e.wait_ge(ST.h, prev)
                M['p2_w', ic] = SG.inc(e.dma_start(
                    out=otst[par][:].rearrange("p (jc m) -> p jc m", m=128),
                    in_=d_wq[:, ic * 128:(ic + 1) * 128].rearrange(
                        "(jc p) m -> p jc m", p=128)), 16)
            GP(_gp)

            qev_base = SS.n

            def _mm(e, ic=ic, par=par, qev_base=qev_base):
                e.wait_ge(SG.h, M['p2_w', ic])
                for qg in range(4):
                    if ic == 0:
                        e.wait_ge(SS.h, M['p1tr', 'ev', (qg * 4 + 4) * NJ - 1])
                    k = ic * 4 + qg
                    if qg < 2:
                        prev = M.get(('p2_qev', k - 2))
                        if prev is not None:
                            e.wait_ge(SS.h, prev)
                    else:
                        e.wait_ge(SS.h, qev_base + (qg - 2) + 1)
                    ins = None
                    for jc in range(NJ):
                        ins = e.matmul(
                            p_q[k % 2], otst[par][:, jc * 128:(jc + 1) * 128],
                            big0[:, jc * T + qg * 512: jc * T + (qg + 1) * 512],
                            start=(jc == 0), stop=(jc == NJ - 1))
                    M['p2_mm', k] = ST.inc(ins)
            TE(_mm)
            M['p2_mm_l', ic] = M['p2_mm', ic * 4 + 3]

            def _ev(e, ic=ic, qev_base=qev_base):
                for qg in range(4):
                    k = ic * 4 + qg
                    e.wait_ge(ST.h, M['p2_mm', k])
                    M['p2_qev', k] = SS.inc(e.activation(
                        big1[:, ic * T + qg * 512: ic * T + (qg + 1) * 512],
                        p_q[k % 2], AF.Copy, scale=SCALE))
                    assert M['p2_qev', k] == qev_base + qg + 1
            SC(_ev)
        M['p2_done'] = SS.n

        def _p3v0(e):
            e.wait_ge(SS.h, M['p2_done'])
            e.memset(vpb[0][:], 1.0)
            M['consts_v2'] = SV.inc(e.memset(vpb[1][:], 1.0))
        VE(_p3v0)

        prev_exp = [None, None]
        prev_omm = [None, None]
        prev_oev = None
        for h in range(HEADS):
            hp = h % 2

            def _gp(e, h=h, hp=hp):
                if h == 0:
                    e.wait_ge(SS.h, M['p2_done'])
                    e.wait_ge(SV.h, M['consts_v2'])
                if h >= 2:
                    e.wait_ge(ST.h, M['p3_lastS', h - 2])
                M['p3_kdone', h] = SG.inc(e.dma_start(
                    out=ktb[hp][:], in_=d_kvk[h * 128:(h + 1) * 128, :]), 16)
                if h >= 2:
                    e.wait_ge(ST.h, M['p3_lastO', h - 2])
                srcv = d_kvv[:, :, h * 128:(h + 1) * 128].rearrange(
                    "kb kk dd -> kk kb dd")
                dstv = vpb[hp][:].rearrange("p (kb c) -> p kb c", c=132)[:, :, 0:128]
                SG.inc(e.dma_start(out=dstv[:, 0:16], in_=srcv[:, 0:16]), 16)
                M['p3_v', h] = SG.inc(e.dma_start(out=dstv[:, 16:32], in_=srcv[:, 16:32]), 16)
            GP(_gp)

            for qg in range(4):
                qbase = h * T + qg * 512
                exp_base = SS.n

                def _t3(e, h=h, hp=hp, qg=qg, qbase=qbase, exp_base=exp_base,
                        prev_exp=prev_exp, prev_oev=prev_oev):
                    if (h, qg) == (0, 0):
                        e.wait_ge(SS.h, M['p2_done'])
                        e.wait_ge(SV.h, M['consts_v2'])
                    e.wait_ge(SG.h, M['p3_kdone', h])
                    e.wait_ge(SG.h, M['p3_v', h])

                    def do_omm(pp, start_pair):
                        e.wait_ge(SS.h, exp_base + pp + 1)
                        if pp == 0 and prev_oev is not None:
                            e.wait_ge(SV.h, prev_oev)
                        ins2 = None
                        for b in range(2):
                            kb = 2 * pp + b
                            for qs in range(4):
                                ins2 = e.matmul(
                                    p_o[:, qs * 256: qs * 256 + 129],
                                    exps[pp % 2][:, b * 512 + qs * 128: b * 512 + (qs + 1) * 128],
                                    vpb[hp][:, kb * 132: kb * 132 + 129],
                                    start=(pp == 0 and b == 0 and qs % 2 == 0),
                                    stop=(pp == 15 and b == 1))
                        M['p3_omm', h, qg, pp] = ST.inc(ins2)

                    for p in range(16):
                        if p >= 2:
                            e.wait_ge(SS.h, exp_base + (p - 2) + 1)
                        elif prev_exp[p] is not None:
                            e.wait_ge(SS.h, prev_exp[p])
                        ins = None
                        for b in range(2):
                            kb = 2 * p + b
                            ins = e.matmul(
                                p_s2[p % 2][:, b * 512:(b + 1) * 512],
                                ktb[hp][:, kb * 128:(kb + 1) * 128],
                                big1[:, qbase:qbase + 512],
                                start=True, stop=True)
                        M['p3_s', h, qg, p] = ST.inc(ins)
                        if p >= 1:
                            do_omm(p - 1, False)
                    do_omm(15, False)
                TE(_t3)

                def _s3(e, h=h, qg=qg, exp_base=exp_base, prev_omm=prev_omm):
                    for p in range(16):
                        e.wait_ge(ST.h, M['p3_s', h, qg, p])
                        if p >= 2:
                            e.wait_ge(ST.h, M['p3_omm', h, qg, p - 2])
                        elif prev_omm[p] is not None:
                            e.wait_ge(ST.h, prev_omm[p])
                        M['p3_exp', h, qg, p] = SS.inc(e.activation(
                            exps[p % 2][:], p_s2[p % 2][:], AF.Exp))
                        assert M['p3_exp', h, qg, p] == exp_base + p + 1
                SC(_s3)

                def _oev(e, h=h, qg=qg):
                    e.wait_ge(ST.h, M['p3_omm', h, qg, 15])
                    ins = None
                    for qs in range(4):
                        ins = e.reciprocal(dinv[:, qs:qs + 1],
                                           p_o[:, qs * 256 + 128: qs * 256 + 129])
                    m_r = SV.inc(ins)
                    e.wait_ge(SV.h, m_r)
                    ins = None
                    for qs in range(4):
                        qt = qg * 4 + qs
                        ins = e.tensor_scalar_mul(
                            big2[:, qt * D + h * 128: qt * D + (h + 1) * 128],
                            p_o[:, qs * 256: qs * 256 + 128], dinv[:, qs:qs + 1])
                    M['p3_oev', h, qg] = SV.inc(ins)
                VE(_oev)

                prev_exp = [M['p3_exp', h, qg, 14], M['p3_exp', h, qg, 15]]
                prev_omm = [M['p3_omm', h, qg, 14], M['p3_omm', h, qg, 15]]
                prev_oev = M['p3_oev', h, qg]
                if qg == 3:
                    M['p3_lastS', h] = M['p3_s', h, 3, 15]
                    M['p3_lastO', h] = M['p3_omm', h, 3, 15]
        M['p3_done_v'] = SV.n
        M['p3_done_s'] = SS.n
        M['p3_done_t'] = ST.n

        regB.close()
        p_w = regC.enter_context(nc.psum_tensor("p_w", [128, 2048], F32))
        pt_f = [regC.enter_context(nc.psum_tensor(f"pt_f{p}", [128, 128], F32))
                for p in range(2)]

        def _gp4(e):
            for ic in range(NJ):
                e.wait_ge(ST.h, M['p3_lastS', ic])
                M['p4_w', ic] = SG.inc(e.dma_start(
                    out=big1[:, ic * T:(ic + 1) * T],
                    in_=d_wo[ic * 128:(ic + 1) * 128, :]), 16)
        GP(_gp4)

        for step in range(NT + 1):
            if step >= 1:
                qtl = step - 1
                parl = qtl % 2

                def _pre_x1tr(e, qtl=qtl):
                    e.wait_ge(SV.h, M['p4_x1', qtl])

                def _pre_x1ev(e, qtl=qtl):
                    prev = M.get(('p4_x1Tst', qtl - 2))
                    if prev is not None:
                        e.wait_ge(SX.h, prev)
                tr_ev(NJ,
                      (lambda j, parl=parl: x1buf[parl][:, j * 128:(j + 1) * 128]),
                      (lambda j, parl=parl: x1Tbuf[parl][:, j * 128:(j + 1) * 128]),
                      pt_f, eye_f, 'p4x1t', 'scalar', SS, qtl * NJ,
                      pre_tr=_pre_x1tr, pre_ev=_pre_x1ev)
                M['p4x1t', 'trdone', qtl] = ST.n
                M['p4_x1evd', qtl] = SS.n

                def _pre_lntr(e, qtl=qtl):
                    e.wait_ge(SV.h, M['ln2', 'norm', qtl])
                    e.wait_ge(SS.h, M['p4o', 'ev', qtl * NJ + NJ - 1])

                def _pre_lnev(e, qtl=qtl):
                    prev = M.get(('p4_lnst', qtl - 2))
                    if prev is not None:
                        e.wait_ge(SX.h, prev)
                tr_ev(NJ,
                      (lambda j, parl=parl: lnstage[parl][:, j * 128:(j + 1) * 128]),
                      (lambda j, parl=parl: ln2Tbuf[parl][:, j * 128:(j + 1) * 128]),
                      pt_b, eye_b, 'p4l', 'vector', SV, qtl * NJ,
                      pre_tr=_pre_lntr, pre_ev=_pre_lnev)
                M['p4l', 'trdone', qtl] = ST.n
                M['p4_lnevd', qtl] = SV.n

                def _sy3(e, qtl=qtl, parl=parl):
                    e.wait_ge(SV.h, M['p4_lnevd', qtl])
                    M['p4_lnst', qtl] = SX.inc(e.dma_start(
                        out=d_ln2T[:, qtl * 128:(qtl + 1) * 128].rearrange(
                            "(jc p) t -> p jc t", p=128),
                        in_=ln2Tbuf[parl][:].rearrange("p (jc t) -> p jc t", t=128)), 16)
                    e.wait_ge(SS.h, M['p4_x1evd', qtl])
                    M['p4_x1Tst', qtl] = SX.inc(e.dma_start(
                        out=d_x1T[:, qtl * 128:(qtl + 1) * 128].rearrange(
                            "(jc p) t -> p jc t", p=128),
                        in_=x1Tbuf[parl][:].rearrange("p (jc t) -> p jc t", t=128)), 16)
                SY(_sy3)

            if step >= NT:
                continue
            qt = step
            par = qt % 2

            def _pre_otr(e, qt=qt):
                if qt == 0:
                    e.wait_ge(SS.h, M['p1_done'])
                    e.wait_ge(SV.h, M['p3_done_v'])
                else:
                    e.wait_ge(SV.h, M['p4l', 'ev', (qt - 1) * NJ + NJ - 1])
            tr_ev(NJ,
                  (lambda j, qt=qt: big2[:, qt * D + j * 128: qt * D + (j + 1) * 128]),
                  (lambda j, par=par: otst[par][:, j * 128:(j + 1) * 128]),
                  pt_b, eye_b, 'p4o', 'scalar', SS, qt * NJ, pre_tr=_pre_otr)

            def _sy1(e, qt=qt, par=par):
                prev = M.get(('p4_x1', qt - 2))
                if prev is not None:
                    e.wait_ge(SV.h, prev)
                elif qt < 2:
                    e.wait_ge(ST.h, M['p3_done_t'])
                M['p4_x', qt] = SX.inc(e.dma_start(
                    out=xstage[par][:], in_=d_x[qt * 128:(qt + 1) * 128, :]), 16)
            SY(_sy1)

            def _mm(e, qt=qt, par=par):
                e.wait_ge(SS.h, M['p4o', 'ev', qt * NJ + NJ - 1])
                if qt == 0:
                    e.wait_ge(SG.h, M['p4_w', NJ - 1])
                prev = M.get(('p4_x1', qt - 1))
                if prev is not None:
                    e.wait_ge(SV.h, prev)
                ins = None
                for ob in range(4):
                    for ic in range(NJ):
                        ins = e.matmul(
                            p_w[:, ob * 512:(ob + 1) * 512],
                            otst[par][:, ic * 128:(ic + 1) * 128],
                            big1[:, ic * T + ob * 512: ic * T + (ob + 1) * 512],
                            start=(ic == 0), stop=(ic == NJ - 1))
                M['p4_mm', qt] = ST.inc(ins)
            TE(_mm)

            def _vx1(e, qt=qt, par=par):
                e.wait_ge(ST.h, M['p4_mm', qt])
                e.wait_ge(SX.h, M['p4_x', qt])
                if qt >= 2:
                    e.wait_ge(SS.h, M['ln2', 'sq', qt - 2])
                    e.wait_ge(ST.h, M['p4x1t', 'trdone', qt - 2])
                    e.wait_ge(ST.h, M['p4l', 'trdone', qt - 2])
                M['p4_x1', qt] = SV.inc(e.tensor_tensor(
                    x1buf[par][:], p_w[:], xstage[par][:], ALU.add))
            VE(_vx1)

            emit_ln('ln2', x1buf[par], lnstage[par], qt,
                    pre_scalar=(lambda e, qt=qt: e.wait_ge(SV.h, M['p4_x1', qt])))
        M['p4_done_ss'] = SS.n
        M['p4_done_sv'] = SV.n
        M['p4_done_st'] = ST.n

        regC.close()
        ptb_stack.close()
        p_r = regD.enter_context(nc.psum_tensor("p_r", [128, 17], F32))
        pt_f5 = [regD.enter_context(nc.psum_tensor(f"pt_f5{p}", [128, 128], F32))
                 for p in range(2)]

        def _gp_pre6(e):
            e.wait_ge(ST.h, M['p4_done_st'])
            e.wait_ge(SX.h, M['p4_lnst', NTH - 1])
            e.wait_ge(SX.h, M['p4_x1Tst', NTH - 1])
            M['p6_lnh', 0] = SG.inc(e.dma_start(
                out=ln2h[:].rearrange("p (jc t) -> p jc t", t=HALF),
                in_=d_ln2T[:, 0:HALF].rearrange("(jc p) t -> p jc t", p=128)), 16)
            M['p6_acc', 0] = SG.inc(e.dma_start(
                out=accT[:].rearrange("p (hc t) -> p hc t", t=HALF),
                in_=d_x1T[:, 0:HALF].rearrange("(hc p) t -> p hc t", p=128)), 16)
        GP(_gp_pre6)

        for qt in range(NT):
            def _sy(e, qt=qt):
                if qt == 0:
                    e.wait_ge(SS.h, M['p4_done_ss'])
                    e.wait_ge(SV.h, M['p4_done_sv'])
                    e.wait_ge(ST.h, M['p4_done_st'])
                elif qt >= 2:
                    e.wait_ge(ST.h, M['p5_mm', qt - 2])
                M['p5_ln', qt] = SX.inc(e.dma_start(
                    out=routst[qt % 2][:].rearrange("p (jc t) -> p jc t", t=128),
                    in_=d_ln2T[:, qt * 128:(qt + 1) * 128].rearrange(
                        "(jc p) t -> p jc t", p=128)), 16)
            SY(_sy)

            def _mm(e, qt=qt):
                e.wait_ge(SX.h, M['p5_ln', qt])
                if qt == 0:
                    e.wait_ge(SG.h, M['wr'])
                else:
                    e.wait_ge(SS.h, M['p5_pr', qt - 1])
                ins = None
                for jc in range(NJ):
                    ins = e.matmul(p_r[:], routst[qt % 2][:, jc * 128:(jc + 1) * 128],
                                   wr17b[:, jc * 17:(jc + 1) * 17],
                                   start=(jc == 0), stop=(jc == NJ - 1))
                M['p5_mm', qt] = ST.inc(ins)
            TE(_mm)

            def _sc(e, qt=qt):
                e.wait_ge(ST.h, M['p5_mm', qt])
                prev = M.get(('p5_comb', qt - 1))
                if prev is not None:
                    e.wait_ge(SV.h, prev)
                e.activation(pe_r[:], p_r[:, 0:16], AF.Exp)
                M['p5_pr', qt] = SS.inc(e.activation(
                    sgcol[:, qt:qt + 1], p_r[:, 16:17], AF.Sigmoid))
            SC(_sc)

            def _v(e, qt=qt):
                e.wait_ge(SS.h, M['p5_pr', qt])
                m1 = SV.inc(e.reduce_sum(sv[:, 8:9], pe_r[:], axis=AX.X))
                e.wait_ge(SV.h, m1)
                m2 = SV.inc(e.reciprocal(sv[:, 9:10], sv[:, 8:9]))
                mx = SV.inc(e.max(pwork[:, 0:8], pe_r[:]))
                e.wait_ge(SV.h, mx)
                e.tensor_copy(pwork[:, 8:12], pwork[:, 0:4])
                mc = SV.inc(e.tensor_copy(pwork[:, 12:16], pwork[:, 0:4]))
                e.wait_ge(SV.h, mc)
                mr = SV.inc(e.match_replace(pmask[:], pwork[:, 8:16], pe_r[:], 0.0))
                e.wait_ge(SV.h, mr)
                e.tensor_tensor(cacc[:], pe_r[:], pmask[:], ALU.subtract)
                e.wait_ge(SV.h, m2)
                M['p5_comb', qt] = SV.inc(e.tensor_scalar_mul(
                    comb[:, qt * 16:(qt + 1) * 16], cacc[:], sv[:, 9:10]))
            VE(_v)

            def _ctr(e, qt=qt):
                e.wait_ge(SV.h, M['p5_comb', qt])
                prev = M.get(('p5_cev', qt - 2))
                if prev is not None:
                    e.wait_ge(SV.h, prev)
                M['p5_ctr', qt] = ST.inc(e.transpose(
                    pt_f5[qt % 2][0:16, :], comb[:, qt * 16:(qt + 1) * 16], eye_f[:]))
            TE(_ctr)

            def _cev(e, qt=qt):
                e.wait_ge(ST.h, M['p5_ctr', qt])
                M['p5_cev', qt] = SV.inc(e.tensor_copy(
                    combT[:, qt * 128:(qt + 1) * 128], pt_f5[qt % 2][0:16, :]))
            VE(_cev)

        def _str(e):
            e.wait_ge(SS.h, M['p5_pr', NT - 1])
            e.wait_ge(SV.h, M['p5_cev', NT - 2])
            M['p5_str'] = ST.inc(e.transpose(pt_f5[0][0:16, :], sgcol[:, 0:16], eye_f[:]))
        TE(_str)

        def _sgev(e):
            e.wait_ge(ST.h, M['p5_str'])
            M['p5_sgT'] = SV.inc(e.tensor_copy(sgT[:], pt_f5[0][0:16, :]))
        VE(_sgev)
        M['p5_done_v'] = SV.n
        M['p5_done_s'] = SS.n
        M['p5_done_t'] = ST.n

        regD.close()
        p_gu = [regE.enter_context(nc.psum_tensor(f"p_gu{p}", [128, 512], F32))
                for p in range(4)]
        p_y = [regE.enter_context(nc.psum_tensor(f"p_y{p}", [128, 512], F32))
               for p in range(2)]
        pt_f6 = [regE.enter_context(nc.psum_tensor(f"pt_f6{p}", [128, 128], F32))
                 for p in range(2)]

        prev_guh = None
        prev_yh = None
        prev_yev = None
        prev_silu_l = None
        prev_uev_l = None
        prev_aev_l = None
        prev_selread = None
        for Hh in range(2):
            tof = Hh * HALF

            if Hh == 1:
                def _sy(e, Hh=Hh, tof=tof):
                    e.wait_ge(ST.h, M['p6o', 'tr', 0, NTH * NJ - 1])
                    M['p6_lnh', Hh] = SX.inc(e.dma_start(
                        out=ln2h[:].rearrange("p (jc t) -> p jc t", t=HALF),
                        in_=d_ln2T[:, tof:tof + HALF].rearrange(
                            "(jc p) t -> p jc t", p=128)), 16)
                    M['p6_acc', Hh] = SX.inc(e.dma_start(
                        out=accT[:].rearrange("p (hc t) -> p hc t", t=HALF),
                        in_=d_x1T[:, tof:tof + HALF].rearrange(
                            "(hc p) t -> p hc t", p=128)), 16)
                SY(_sy)

            for q8 in range(NTH):
                qt = Hh * NTH + q8

                def _vsel(e, q8=q8, qt=qt, Hh=Hh, prev_selread=prev_selread):
                    if (Hh, q8) == (0, 0):
                        e.wait_ge(SV.h, M['p5_sgT'])
                    if prev_selread is not None:
                        e.wait_ge(ST.h, prev_selread)
                    M['p6_sel', Hh, q8] = SV.inc(e.tensor_scalar_mul(
                        sel_b[:], ones16[:], eye16[:, qt:qt + 1]))
                VE(_vsel)

                def _tsg(e, q8=q8, Hh=Hh, prev_silu_l=prev_silu_l,
                         prev_uev_l=prev_uev_l):
                    e.wait_ge(SV.h, M['p6_sel', Hh, q8])
                    prev = M.get(('p6_sgev', Hh, q8 - 1))
                    if prev is not None:
                        e.wait_ge(SV.h, prev)
                    elif Hh == 0:
                        e.wait_ge(SS.h, M['p5_done_s'])
                    else:
                        e.wait_ge(SS.h, prev_silu_l)
                        e.wait_ge(SV.h, prev_uev_l)
                    M['p6_sgmm', Hh, q8] = ST.inc(e.matmul(
                        p_gu[0][:, 0:128], sel_b[:], sgT[:], start=True, stop=True))
                TE(_tsg)
                prev_selread = M['p6_sgmm', Hh, q8]

                def _vsg(e, q8=q8, Hh=Hh):
                    e.wait_ge(ST.h, M['p6_sgmm', Hh, q8])
                    M['p6_sgev', Hh, q8] = SV.inc(e.tensor_copy(
                        sg_b[:, q8 * 128:(q8 + 1) * 128], p_gu[0][:, 0:128]))
                VE(_vsg)

            for u in range(E + NBAND):
                is_sh = u >= E
                bb = u - E

                if not is_sh:
                    def _vsel2(e, u=u, Hh=Hh, prev_selread=prev_selread):
                        e.wait_ge(ST.h, prev_selread)
                        M['p6_sel2', Hh, u] = SV.inc(e.tensor_scalar_mul(
                            sel_b[:], ones16[:], eye16[:, u:u + 1]))
                    VE(_vsel2)

                    def _tcb(e, u=u, Hh=Hh, tof=tof,
                             prev_silu_l=prev_silu_l, prev_uev_l=prev_uev_l):
                        e.wait_ge(SV.h, M['p6_sel2', Hh, u])
                        if prev_silu_l is not None:
                            e.wait_ge(SS.h, prev_silu_l)
                            e.wait_ge(SV.h, prev_uev_l)
                        else:
                            e.wait_ge(SV.h, M['p6_sgev', 0, NTH - 1])
                        e.matmul(p_gu[0][:], sel_b[:],
                                 combT[:, tof:tof + 512], start=True, stop=True)
                        ins = e.matmul(p_gu[1][:], sel_b[:],
                                       combT[:, tof + 512:tof + 1024],
                                       start=True, stop=True)
                        M['p6_cbmm', Hh, u] = ST.inc(ins)
                    TE(_tcb)
                    prev_selread = M['p6_cbmm', Hh, u]

                    def _vcb(e, u=u, Hh=Hh):
                        e.wait_ge(ST.h, M['p6_cbmm', Hh, u])
                        e.tensor_copy(cb_b[:, 0:512], p_gu[0][:])
                        M['p6_cbev', Hh, u] = SV.inc(
                            e.tensor_copy(cb_b[:, 512:1024], p_gu[1][:]))
                    VE(_vcb)

                NG = 2 * NI
                st_gu = ST.n
                ss_gu = SS.n
                sv_gu = SV.n

                def _gpw(e, u=u, Hh=Hh, is_sh=is_sh, bb=bb,
                         prev_guh=prev_guh, st_gu=st_gu):
                    if (Hh, u) == (0, 0):
                        e.wait_ge(ST.h, M['p5_done_t'])
                    for icc in range(NI):
                        gp = icc % 3
                        if icc >= 3:
                            e.wait_ge(ST.h, st_gu + 4 * (icc - 2))
                        elif prev_guh is not None:
                            e.wait_ge(ST.h, prev_guh)
                        if is_sh:
                            colg = d_wsg[:, (bb * NI + icc) * 128:(bb * NI + icc + 1) * 128]
                            colu = d_wsu[:, (bb * NI + icc) * 128:(bb * NI + icc + 1) * 128]
                        else:
                            colg = d_wg[u][:, icc * 128:(icc + 1) * 128]
                            colu = d_wu[u][:, icc * 128:(icc + 1) * 128]
                        SG.inc(e.dma_start(
                            out=wgc[gp][:].rearrange("p (jc i) -> p jc i", i=128),
                            in_=colg.rearrange("(jc p) i -> p jc i", p=128)), 16)
                        M['p6_w', Hh, u, icc] = SG.inc(e.dma_start(
                            out=wuc[gp][:].rearrange("p (jc i) -> p jc i", i=128),
                            in_=colu.rearrange("(jc p) i -> p jc i", p=128)), 16)
                GP(_gpw)

                def _tgu(e, u=u, Hh=Hh, is_sh=is_sh,
                         prev_silu_l=prev_silu_l, prev_uev_l=prev_uev_l,
                         st_gu=st_gu, ss_gu=ss_gu, sv_gu=sv_gu):
                    for k in range(NG):
                        icc, tg = k // 2, k % 2
                        gp = icc % 3
                        if tg == 0:
                            e.wait_ge(SG.h, M['p6_w', Hh, u, icc])
                        if k == 0:
                            e.wait_ge(SG.h if Hh == 0 else SX.h, M['p6_lnh', Hh])
                        if k < 2:
                            if not is_sh:
                                e.wait_ge(SV.h, M['p6_cbev', Hh, u])
                            elif prev_silu_l is not None:
                                e.wait_ge(SS.h, prev_silu_l)
                                e.wait_ge(SV.h, prev_uev_l)
                        else:
                            e.wait_ge(SS.h, ss_gu + k - 1)
                            e.wait_ge(SV.h, sv_gu + 2 * k - 3)
                        ins = None
                        for jc in range(NJ):
                            ins = e.matmul(
                                p_gu[tg][:], wgc[gp][:, jc * 128:(jc + 1) * 128],
                                ln2h[:, jc * HALF + tg * 512: jc * HALF + (tg + 1) * 512],
                                start=(jc == 0), stop=(jc == NJ - 1))
                        M['p6_gmm', Hh, u, k] = ST.inc(ins)
                        ins = None
                        for jc in range(NJ):
                            ins = e.matmul(
                                p_gu[2 + tg][:], wuc[gp][:, jc * 128:(jc + 1) * 128],
                                ln2h[:, jc * HALF + tg * 512: jc * HALF + (tg + 1) * 512],
                                start=(jc == 0), stop=(jc == NJ - 1))
                        M['p6_umm', Hh, u, k] = ST.inc(ins)
                        if tg == 1:
                            M['p6_guh', Hh, u, icc] = ST.n
                            assert M['p6_guh', Hh, u, icc] == st_gu + 4 * (icc + 1)
                TE(_tgu)

                def _ssi(e, u=u, Hh=Hh, prev_aev_l=prev_aev_l,
                         ss_gu=ss_gu, sv_gu=sv_gu):
                    for k in range(NG):
                        tg = k % 2
                        e.wait_ge(ST.h, M['p6_gmm', Hh, u, k])
                        if k >= 2:
                            e.wait_ge(SV.h, sv_gu + 2 * k - 2)
                        elif prev_aev_l is not None:
                            e.wait_ge(SV.h, prev_aev_l)
                        M['p6_sgevac', Hh, u, k] = SS.inc(e.activation(
                            sgb_[:, tg * 512:(tg + 1) * 512], p_gu[tg][:], AF.Silu))
                        assert M['p6_sgevac', Hh, u, k] == ss_gu + k + 1
                SC(_ssi)

                def _vu(e, u=u, Hh=Hh, is_sh=is_sh, prev_yh=prev_yh, sv_gu=sv_gu):
                    gate_src = sg_b if is_sh else cb_b
                    for k in range(NG):
                        icc, tg = k // 2, k % 2
                        e.wait_ge(ST.h, M['p6_umm', Hh, u, k])
                        M['p6_uevac', Hh, u, k] = SV.inc(e.tensor_tensor(
                            ubuf[:, tg * 512:(tg + 1) * 512], p_gu[2 + tg][:],
                            gate_src[:, tg * 512:(tg + 1) * 512], ALU.mult))
                        e.wait_ge(SS.h, M['p6_sgevac', Hh, u, k])
                        if k == 0 and prev_yh is not None:
                            e.wait_ge(ST.h, prev_yh)
                        M['p6_aev', Hh, u, k] = SV.inc(e.tensor_tensor(
                            a_sb[:, icc * HALF + tg * 512: icc * HALF + (tg + 1) * 512],
                            sgb_[:, tg * 512:(tg + 1) * 512],
                            ubuf[:, tg * 512:(tg + 1) * 512], ALU.mult))
                        assert M['p6_aev', Hh, u, k] == sv_gu + 2 * k + 2
                VE(_vu)
                prev_guh = M['p6_guh', Hh, u, NI - 1]
                prev_silu_l = M['p6_sgevac', Hh, u, NG - 1]
                prev_uev_l = M['p6_uevac', Hh, u, NG - 1]
                prev_aev_l = M['p6_aev', Hh, u, NG - 1]
                M['p6_adone', Hh, u] = SV.n

                st_dn = ST.n
                sv_dn = SV.n

                def _gpd(e, u=u, Hh=Hh, is_sh=is_sh, bb=bb,
                         prev_yh=prev_yh, st_dn=st_dn):
                    for hc in range(NJ):
                        wp = hc % 3
                        if hc >= 3:
                            e.wait_ge(ST.h, st_dn + 2 * (hc - 2))
                        elif prev_yh is not None:
                            e.wait_ge(ST.h, prev_yh)
                        if is_sh:
                            blk = d_wsd[bb * I_:(bb + 1) * I_, hc * 128:(hc + 1) * 128]
                        else:
                            blk = d_wd[u][:, hc * 128:(hc + 1) * 128]
                        M['p6_wd', Hh, u, hc] = SG.inc(e.dma_start(
                            out=wdc[wp][:].rearrange("p (ic hh) -> p ic hh", hh=128),
                            in_=blk.rearrange("(ic p) hh -> p ic hh", p=128)), 16)
                GP(_gpd)

                def _tdn(e, u=u, Hh=Hh, prev_yev=prev_yev, st_dn=st_dn, sv_dn=sv_dn):
                    for hc in range(NJ):
                        wp = hc % 3
                        e.wait_ge(SG.h, M['p6_wd', Hh, u, hc])
                        if hc == 0:
                            e.wait_ge(SV.h, M['p6_adone', Hh, u])
                            e.wait_ge(SG.h if Hh == 0 else SX.h, M['p6_acc', Hh])
                            if (Hh, u) == (0, 0):
                                e.wait_ge(SS.h, M['p5_done_s'])
                        ins = None
                        for tg in range(2):
                            y = hc * 2 + tg
                            if y >= 2:
                                e.wait_ge(SV.h, sv_dn + y - 1)
                            elif prev_yev is not None:
                                e.wait_ge(SV.h, prev_yev)
                            for icc in range(NI):
                                ins = e.matmul(
                                    p_y[y % 2][:], wdc[wp][:, icc * 128:(icc + 1) * 128],
                                    a_sb[:, icc * HALF + tg * 512: icc * HALF + (tg + 1) * 512],
                                    start=(icc == 0), stop=(icc == NI - 1))
                            M['p6_ymm', Hh, u, y] = ST.inc(ins)
                        M['p6_yh', Hh, u, hc] = ST.n
                        assert M['p6_yh', Hh, u, hc] == st_dn + 2 * (hc + 1)
                TE(_tdn)

                def _vy(e, u=u, Hh=Hh, sv_dn=sv_dn):
                    for hc in range(NJ):
                        for tg in range(2):
                            y = hc * 2 + tg
                            e.wait_ge(ST.h, M['p6_ymm', Hh, u, y])
                            dst = accT[:, hc * HALF + tg * 512: hc * HALF + (tg + 1) * 512]
                            M['p6_yev', Hh, u, y] = SV.inc(e.tensor_tensor(
                                dst, dst, p_y[y % 2][:], ALU.add))
                            assert M['p6_yev', Hh, u, y] == sv_dn + y + 1
                VE(_vy)
                prev_yh = M['p6_yh', Hh, u, NJ - 1]
                prev_yev = M['p6_yev', Hh, u, 2 * NJ - 1]

            for q8 in range(NTH):
                op = q8 % 2

                def _pre_otr(e, q8=q8, Hh=Hh, prev_yev=prev_yev):
                    if q8 == 0:
                        e.wait_ge(SV.h, prev_yev)
                    prev = M.get(('p6_obst', Hh, q8 - 2))
                    if prev is not None:
                        e.wait_ge(SO.h, prev)
                tr_ev(NJ,
                      (lambda j, q8=q8: accT[:, j * HALF + q8 * 128: j * HALF + (q8 + 1) * 128]),
                      (lambda j, op=op: outbuf[op][:, j * 128:(j + 1) * 128]),
                      pt_f6, eye_f, ('p6o', Hh), 'scalar', SS, q8 * NJ,
                      pre_tr=_pre_otr)
                M['p6_obuf', Hh, q8] = SS.n

                def _syo(e, q8=q8, op=op, Hh=Hh, tof=tof):
                    e.wait_ge(SS.h, M['p6_obuf', Hh, q8])
                    r0 = tof + q8 * 128
                    M['p6_obst', Hh, q8] = SO.inc(e.dma_start(
                        out=d_out[r0:r0 + 128, :], in_=outbuf[op][:]), 16)
                SY(_syo)
            M['p6o', 'tr', Hh, NTH * NJ - 1] = M[('p6o', Hh), 'tr', NTH * NJ - 1]

        def _fin(e):
            e.wait_ge(SO.h, SO.n)
        SY(_fin)

    return nc


_NC_CACHE = None


def _get_program():
    global _NC_CACHE
    if _NC_CACHE is None:
        _NC_CACHE = build_program()
    return _NC_CACHE


def kernel(L0_attn_query, L0_attn_kv_k, L0_attn_kv_v, L0_attn_Wq, L0_attn_Wo,
           L0_ffn_w_router, L0_ffn_w_gate, L0_ffn_w_up, L0_ffn_w_down,
           L0_ffn_ws_gate, L0_ffn_ws_up, L0_ffn_ws_down, L0_ffn_w_shared_gate):
    xq = np.ascontiguousarray(np.asarray(L0_attn_query, dtype=np.float32)).reshape(-1, D)
    wr17 = np.ascontiguousarray(
        np.concatenate([np.asarray(L0_ffn_w_router, dtype=np.float32),
                        np.asarray(L0_ffn_w_shared_gate, dtype=np.float32)], axis=1))
    common = {
        "kv_kT": np.ascontiguousarray(
            np.asarray(L0_attn_kv_k, dtype=np.float32).reshape(KV, D).T),
        "kv_v": np.ascontiguousarray(np.asarray(L0_attn_kv_v, dtype=np.float32)),
        "WqT": np.ascontiguousarray(np.asarray(L0_attn_Wq, dtype=np.float32).T),
        "WoT": np.ascontiguousarray(np.asarray(L0_attn_Wo, dtype=np.float32).T),
        "wr17": wr17,
        "w_gate": np.ascontiguousarray(np.asarray(L0_ffn_w_gate, dtype=np.float32)),
        "w_up": np.ascontiguousarray(np.asarray(L0_ffn_w_up, dtype=np.float32)),
        "w_down": np.ascontiguousarray(np.asarray(L0_ffn_w_down, dtype=np.float32)),
        "ws_gate": np.ascontiguousarray(np.asarray(L0_ffn_ws_gate, dtype=np.float32)),
        "ws_up": np.ascontiguousarray(np.asarray(L0_ffn_ws_up, dtype=np.float32)),
        "ws_down": np.ascontiguousarray(np.asarray(L0_ffn_ws_down, dtype=np.float32)),
        "eye_b": np.eye(128, dtype=np.float32).astype(ml_dtypes.bfloat16),
        "eye_f": np.eye(128, dtype=np.float32),
        "eye16": np.eye(16, dtype=np.float32),
    }
    in_maps = []
    for c in range(NCORES):
        m = dict(common)
        m["xq"] = np.ascontiguousarray(xq[c * T:(c + 1) * T, :])
        in_maps.append(m)
    nc = _get_program()
    res = run_bass_kernel_spmd(nc, in_maps, core_ids=list(range(NCORES)))
    out = np.concatenate([np.asarray(r["out"]) for r in res.results], axis=0)
    return out.reshape(4, 4096, 2048).astype(np.float32)
